# version 14
# baseline (speedup 1.0000x reference)
DEBUG = False
STOP_AFTER = 99
CAST_OFF = False
TRY_FUSED = True
import contextlib
import numpy as np
import concourse.bass as bass
import concourse.mybir as mybir
from concourse.bass_utils import run_bass_kernel_spmd

F32 = mybir.dt.float32
BF16 = mybir.dt.bfloat16
AF = mybir.ActivationFunctionType
ALU = mybir.AluOpType
AX = mybir.AxisListType

D = 1024
NCH = 8
LAT = 256
LCH = 2
T_OWN = 2048
T_TOT = 4096
N_PAIRS = 8
N_GROUPS = 4
GRP_D = 256
SCALE = 0.125
NQT = 4
NKC = 32

PS_SLOT = [0, 1024]
PS_PVA = 2048
PS_PVB = 2560
PS_RHA = 3072
PS_RHB = 3584

RB_BV = 3072
RB_ST = 4096
RB_W = 5120
RB_NB = 5184


class _StopEmit(Exception):
    pass


class Sched:
    def __init__(self, ctx):
        self.ctx = ctx
        self.items = {k: [] for k in ("sync", "gpsimd", "tensor", "vector", "scalar")}

    def emit(self, eng, fn, waits=(), inc=None, amt=1):
        tok = None
        flat = []

        def _add(w):
            if w is None:
                return
            if isinstance(w, (list, tuple)) and (len(w) != 2 or not isinstance(w[0], str)):
                for x in w:
                    _add(x)
            else:
                flat.append(w)
        for w in waits:
            _add(w)
        if inc is not None:
            self.ctx.n[inc] = self.ctx.n.get(inc, 0) + amt
            tok = (inc, self.ctx.n[inc])
        self.items[eng].append((fn, tuple(flat), inc, amt))
        return tok

    def fence(self):
        ctx = self.ctx
        w = []
        for q in ("qd", "sd", "gd"):
            w.extend(ctx.qtoks(q))
        self.emit("sync", lambda e: e.nop(), waits=w)

    def dma(self, eng, qname, fn, waits=()):
        ctx = self.ctx
        n = 1 if qname == "sd" else NCHAIN
        i = ctx.qrr[qname] = (ctx.qrr.get(qname, -1) + 1) % n
        sem = "%s%d" % (qname, i)
        prev = ctx.tok(sem)
        return self.emit(eng, fn, waits=list(waits) + [prev], inc=sem, amt=16)

    def realize(self, nc):
        ctx = self.ctx
        with nc.Block() as block:
            for eng_name in ("sync", "gpsimd", "tensor", "vector", "scalar"):
                items = self.items[eng_name]
                if not items:
                    continue

                def runner(e, items=items):
                    for fn, waits, inc, amt in items:
                        for (s, v) in waits:
                            if v > 0:
                                e.wait_ge(ctx.sems[s], v)
                        r = fn(e)
                        if inc is not None:
                            r.then_inc(ctx.sems[inc], amt)

                getattr(block, eng_name)(runner)


NCHAIN = 4


class Ctx:
    def __init__(self, nc, sems):
        self.nc = nc
        self.sems = sems
        self.n = {}
        self.qrr = {}

    def tok(self, name):
        return (name, self.n.get(name, 0))

    def qtoks(self, qname):
        return [self.tok("%s%d" % (qname, i)) for i in range(NCHAIN)]


def mm_tile(S, out_ap, pieces, waits=(), inc="tn", pad=None):
    n = len(pieces)
    tok = None
    for j, (lhsT, rhs, tp, st) in enumerate(pieces):
        first, last = j == 0, j == n - 1

        def fn(t, lhsT=lhsT, rhs=rhs, tp=tp, st=st, first=first, last=last):
            return t.matmul(out_ap, lhsT, rhs, start=st if st is not None else first,
                            stop=last, tile_position=tp)
        tok = S.emit("tensor", fn, waits=waits if j == 0 else (),
                     inc=None if (pad is not None and last) else (inc if last else None))
    if pad is not None:
        tok = S.emit("tensor", lambda t: t.ldweights(pad), inc=inc)
    return tok


PAD = [None]


def row_to_fm(S, ctx, ps, id1f, src_ap, n, dst, dst_col, src_tok):
    prev = src_tok
    for c in range(n // 128):
        pcol = 1552 + 2 * (c % 2)
        S.emit("tensor", lambda t, c=c, pcol=pcol: t.transpose(
            ps[:, pcol:pcol + 1], src_ap[0:1, c * 128:(c + 1) * 128],
            id1f[:, :]), waits=[prev])
        t_t = S.emit("tensor", lambda t: t.ldweights(PAD[0]), inc="tn")
        prev = S.emit("vector", lambda v, c=c, pcol=pcol: v.tensor_copy(
            dst[:, dst_col + c:dst_col + c + 1], ps[:, pcol:pcol + 1]),
            waits=[t_t], inc="vc")
    return prev


def build_nc():
    nc = bass.Bass("TRN2", target_bir_lowering=False)

    x_own = nc.dram_tensor("x_own", [T_OWN, D], F32, kind="ExternalInput").ap()
    x_oth = nc.dram_tensor("x_oth", [T_OWN, D], F32, kind="ExternalInput").ap()
    aWq = nc.dram_tensor("aWq", [D, D], F32, kind="ExternalInput").ap()
    abq = nc.dram_tensor("abq", [D], F32, kind="ExternalInput").ap()
    aWl = nc.dram_tensor("aWl", [D, LAT], F32, kind="ExternalInput").ap()
    abl = nc.dram_tensor("abl", [LAT], F32, kind="ExternalInput").ap()
    aWk = nc.dram_tensor("aWk", [LAT, D], F32, kind="ExternalInput").ap()
    abk = nc.dram_tensor("abk", [D], F32, kind="ExternalInput").ap()
    aWv = nc.dram_tensor("aWv", [LAT, D], F32, kind="ExternalInput").ap()
    abv = nc.dram_tensor("abv", [D], F32, kind="ExternalInput").ap()
    aWo = nc.dram_tensor("aWo", [D, D], F32, kind="ExternalInput").ap()
    abo = nc.dram_tensor("abo", [D], F32, kind="ExternalInput").ap()
    fWq = nc.dram_tensor("fWq", [4, D, D], F32, kind="ExternalInput").ap()
    fbq = nc.dram_tensor("fbq", [4, D], F32, kind="ExternalInput").ap()
    fWk = nc.dram_tensor("fWk", [4, D, D], F32, kind="ExternalInput").ap()
    fbk = nc.dram_tensor("fbk", [4, D], F32, kind="ExternalInput").ap()
    fWv = nc.dram_tensor("fWv", [4, D, D], F32, kind="ExternalInput").ap()
    fbv = nc.dram_tensor("fbv", [4, D], F32, kind="ExternalInput").ap()
    fWg1 = nc.dram_tensor("fWg1", [D, 128], F32, kind="ExternalInput").ap()
    fbg1 = nc.dram_tensor("fbg1", [128], F32, kind="ExternalInput").ap()
    fWg2 = nc.dram_tensor("fWg2", [128, 4], F32, kind="ExternalInput").ap()
    fbg2 = nc.dram_tensor("fbg2", [4], F32, kind="ExternalInput").ap()
    fWo = nc.dram_tensor("fWo", [D, D], F32, kind="ExternalInput").ap()
    fbo = nc.dram_tensor("fbo", [D], F32, kind="ExternalInput").ap()

    cident = nc.dram_tensor("cident", [128, 128], F32, kind="ExternalInput").ap()
    y = nc.dram_tensor("y", [T_OWN, D], F32, kind="ExternalOutput").ap()
    if DEBUG:
        dbg_xnf = nc.dram_tensor("dbg_xnf", [128, 4096], BF16, kind="ExternalOutput").ap()
        dbg_attO = nc.dram_tensor("dbg_attO", [128, 4096], BF16, kind="ExternalOutput").ap()
        dbg_lg = nc.dram_tensor("dbg_lg", [1, 32], F32, kind="ExternalOutput").ap()
        dbg_qkm = nc.dram_tensor("dbg_qkm", [1, 32], F32, kind="ExternalOutput").ap()
        dbg_rr = nc.dram_tensor("dbg_rr", [1, 1024], F32, kind="ExternalOutput").ap()
        dbg_negB = nc.dram_tensor("dbg_negB", [128, 4], F32, kind="ExternalOutput").ap()
    spill = nc.dram_tensor("spill", [128, NCH * T_OWN], BF16, kind="ExternalOutput" if DEBUG else "Internal").ap()
    wcv = nc.dram_tensor("wcv", [128, NCH * D], BF16, kind="Internal").ap()

    est = contextlib.ExitStack()
    try:
      with est:
        sems = {}
        for s in ("tn", "vc", "ac"):
            sems[s] = est.enter_context(nc.semaphore(s))
        for q in ("qd", "sd", "gd"):
            for i in range(4):
                n = "%s%d" % (q, i)
                sems[n] = est.enter_context(nc.semaphore(n))
        ctx = Ctx(nc, sems)

        xnf = est.enter_context(nc.sbuf_tensor("xnf", [128, NCH * T_TOT], BF16))
        PAD[0] = xnf[:, 0:128]
        attO = est.enter_context(nc.sbuf_tensor("attO", [128, N_PAIRS * T_OWN], BF16))
        ps = est.enter_context(nc.psum_tensor("ps", [128, 4096], F32))
        bA = est.enter_context(nc.sbuf_tensor("bA", [128, 32], F32))
        bF = est.enter_context(nc.sbuf_tensor("bF", [128, 32], F32))
        rowsA = est.enter_context(nc.sbuf_tensor("rowsA", [1, 1024], F32))
        rowsB = est.enter_context(nc.sbuf_tensor("rowsB", [1, 5376], BF16))
        onesC = est.enter_context(nc.sbuf_tensor("onesC", [1, 128], BF16))
        uA = est.enter_context(nc.sbuf_tensor("uA", [1, 128], BF16))
        uB = est.enter_context(nc.sbuf_tensor("uB", [1, 128], BF16))
        ones128 = est.enter_context(nc.sbuf_tensor("ones128", [128, 1], BF16))
        id1 = est.enter_context(nc.sbuf_tensor("id1", [1, 1], BF16))
        id1f = est.enter_context(nc.sbuf_tensor("id1f", [1, 1], F32))
        id128 = est.enter_context(nc.sbuf_tensor("id128", [128, 128], F32))
        negB = est.enter_context(nc.sbuf_tensor("negB", [128, 4], F32))
        wbc = est.enter_context(nc.sbuf_tensor("wbc", [128, 4], F32))
        wT4 = est.enter_context(nc.sbuf_tensor("wT4", [4, 1], BF16))
        lg = est.enter_context(nc.sbuf_tensor("lg", [1, 32], F32))
        qkm = est.enter_context(nc.sbuf_tensor("qkm", [1, 32], F32))
        rr = est.enter_context(nc.sbuf_tensor("rr", [1, 1024], F32))
        rrb = est.enter_context(nc.sbuf_tensor("rrb", [1, 1024], BF16))
        Rsb = est.enter_context(nc.sbuf_tensor("Rsb", [128, 512], F32))
        pooled = est.enter_context(nc.sbuf_tensor("pooled", [128, 8], F32))
        pooled_b = est.enter_context(nc.sbuf_tensor("pooled_b", [128, 8], BF16))
        hgate = est.enter_context(nc.sbuf_tensor("hgate", [128, 1], BF16))
        WG2 = est.enter_context(nc.sbuf_tensor("WG2", [128, 4], BF16))
        bvt = est.enter_context(nc.sbuf_tensor("bvt", [128, GRP_D], F32))

        S0 = Sched(ctx)
        S0.dma("sync", "qd", lambda e: e.dma_start(out=id128[:, :], in_=cident[:, :]))
        S0.emit("gpsimd", lambda g: g.memset(onesC[:, :], 1.0))
        S0.emit("gpsimd", lambda g: g.memset(uA[0:1, 0:64], 1.0))
        S0.emit("gpsimd", lambda g: g.memset(uA[0:1, 64:128], 0.0))
        S0.emit("gpsimd", lambda g: g.memset(uB[0:1, 0:64], 0.0))
        S0.emit("gpsimd", lambda g: g.memset(uB[0:1, 64:128], 1.0))
        S0.emit("gpsimd", lambda g: g.memset(ones128[:, :], 1.0))
        S0.emit("gpsimd", lambda g: g.memset(id1[:, :], 1.0))
        S0.emit("gpsimd", lambda g: g.memset(id1f[:, :], 1.0))
        S0.emit("gpsimd", lambda g: g.memset(lg[:, :], 0.0))
        S0.emit("gpsimd", lambda g: g.memset(qkm[:, :], 0.0))
        S0.fence()
        S0.realize(nc)

        aux = dict(negB=negB, rr=rr, rrb=rrb, Rsb=Rsb, onesC=onesC, uA=uA, uB=uB,
                   ones128=ones128, qkm=qkm, bvt=bvt, rowsA=rowsA, rowsB=rowsB,
                   id1=id1, id1f=id1f)

        with (
            nc.sbuf_tensor("Wq_b", [128, NCH * D], BF16) as Wq_b,
            nc.sbuf_tensor("Wl_b", [128, NCH * LAT], BF16) as Wl_b,
            nc.sbuf_tensor("Wkv_b", [128, 2 * LCH * D], BF16) as Wkv_b,
            nc.sbuf_tensor("Wo_b", [128, NCH * D], BF16) as Wo_b,
            nc.sbuf_tensor("latf", [128, LCH * T_TOT], BF16) as latf,
        ):
            with (
                nc.sbuf_tensor("xt", [128, 2 * D], F32) as xt,
                nc.sbuf_tensor("xsq", [128, D], F32) as xsq,
                nc.sbuf_tensor("xnt", [128, 2 * D], F32) as xnt,
                nc.sbuf_tensor("st8", [128, 16], F32) as st8,
                nc.sbuf_tensor("WG1", [128, NCH * 128], BF16) as WG1,
            ):
                S = Sched(ctx)

                def wload(dst_ap, src_ap):
                    if CAST_OFF:
                        return S.dma("gpsimd", "gd", lambda g: g.dma_start(
                            out=xt[:, 0:D], in_=aWq[0:128, :]))
                    return S.dma("gpsimd", "gd", lambda g, d=dst_ap, s=src_ap: g.dma_start(
                        out=d, in_=s))
                wload(Wq_b[:, :].rearrange("p (c o) -> p c o", o=D), aWq.rearrange("(c p) o -> p c o", p=128))
                wload(Wl_b[:, :].rearrange("p (c o) -> p c o", o=LAT), aWl.rearrange("(c p) o -> p c o", p=128))
                wload(Wkv_b[:, 0:LCH * D].rearrange("p (c o) -> p c o", o=D), aWk.rearrange("(c p) o -> p c o", p=128))
                wload(Wkv_b[:, LCH * D:2 * LCH * D].rearrange("p (c o) -> p c o", o=D), aWv.rearrange("(c p) o -> p c o", p=128))
                wload(Wo_b[:, :].rearrange("p (c o) -> p c o", o=D), aWo.rearrange("(c p) o -> p c o", p=128))
                wload(WG1[:, :].rearrange("p (c o) -> p c o", o=128), fWg1.rearrange("(c p) o -> p c o", p=128))
                wload(WG2[:, :], fWg2[:, :])

                t_bg1 = S.dma("sync", "qd", lambda e: e.dma_start(
                    out=bA[:, 26:27], in_=fbg1.rearrange("(p c) -> p c", c=1)))
                t_bg2 = S.dma("sync", "qd", lambda e: e.dma_start(
                    out=lg[0:1, 28:32], in_=fbg2[None, :]))
                wload(rowsB[0:1, RB_BV:RB_BV + D], abv[None, :])

                xt_free = [None, None]
                xnt_free = [None, None]
                tr_slot = [None, None, None, None]
                for it in range(32):
                    par = it % 2
                    src = x_own if it < 16 else x_oth
                    row0 = (it % 16) * 128
                    t_ld = S.dma("sync", "qd", lambda e, s=src, r=row0, par=par: e.dma_start(
                        out=xt[:, par * D:(par + 1) * D], in_=s[r:r + 128, :]),
                        waits=[xt_free[par]])
                    c0 = par * 8

                    def stats(v, par=par, c0=c0):
                        v.reduce_sum(st8[:, c0:c0 + 1], xt[:, par * D:(par + 1) * D], axis=AX.X)
                        v.tensor_tensor(xsq[:, :], xt[:, par * D:(par + 1) * D],
                                        xt[:, par * D:(par + 1) * D], op=ALU.mult)
                        v.drain()
                        v.reduce_sum(st8[:, c0 + 1:c0 + 2], xsq[:, :], axis=AX.X)
                        v.drain()
                        v.tensor_scalar_mul(st8[:, c0:c0 + 1], st8[:, c0:c0 + 1], 1.0 / D)
                        v.tensor_scalar_mul(st8[:, c0 + 1:c0 + 2], st8[:, c0 + 1:c0 + 2], 1.0 / D)
                        v.drain()
                        v.tensor_tensor(st8[:, c0 + 2:c0 + 3], st8[:, c0:c0 + 1],
                                        st8[:, c0:c0 + 1], op=ALU.mult)
                        v.drain()
                        v.tensor_sub(st8[:, c0 + 1:c0 + 2], st8[:, c0 + 1:c0 + 2], st8[:, c0 + 2:c0 + 3])
                        v.drain()
                        return v.tensor_scalar_add(st8[:, c0 + 1:c0 + 2], st8[:, c0 + 1:c0 + 2], 1e-5)
                    t_st = S.emit("vector", stats, waits=[t_ld], inc="vc")
                    t_sd = S.emit("scalar", lambda a, c0=c0: a.activation(
                        st8[:, c0 + 2:c0 + 3], st8[:, c0 + 1:c0 + 2], AF.Sqrt),
                        waits=[t_st], inc="ac")

                    def fin(v, c0=c0):
                        v.reciprocal(st8[:, c0 + 3:c0 + 4], st8[:, c0 + 2:c0 + 3])
                        v.drain()
                        v.tensor_tensor(st8[:, c0 + 4:c0 + 5], st8[:, c0:c0 + 1],
                                        st8[:, c0 + 3:c0 + 4], op=ALU.mult)
                        v.drain()
                        return v.tensor_scalar_mul(st8[:, c0 + 4:c0 + 5], st8[:, c0 + 4:c0 + 5], -1.0)
                    t_fin = S.emit("vector", fin, waits=[t_sd], inc="vc")
                    t_xn = S.emit("scalar", lambda a, par=par, c0=c0: a.activation(
                        xnt[:, par * D:(par + 1) * D], xt[:, par * D:(par + 1) * D], AF.Identity,
                        bias=st8[:, c0 + 4:c0 + 5], scale=st8[:, c0 + 3:c0 + 4]),
                        waits=[t_fin, xnt_free[par]], inc="ac")
                    xt_free[par] = t_xn
                    last_ev = None
                    for c in range(NCH):
                        bk = c % 4
                        S.emit("tensor", lambda t, par=par, c=c, bk=bk: t.transpose(
                            ps[:, bk * 512:bk * 512 + 128],
                            xnt[:, par * D + c * 128:par * D + (c + 1) * 128], id128[:, :]),
                            waits=[t_xn, tr_slot[bk]])
                        t_t = S.emit("tensor", lambda t: t.ldweights(xnf[:, 0:128]), inc="tn")
                        last_ev = S.emit("vector", lambda v, it=it, c=c, bk=bk: v.tensor_copy(
                            xnf[:, c * T_TOT + it * 128:c * T_TOT + (it + 1) * 128],
                            ps[:, bk * 512:bk * 512 + 128]), waits=[t_t], inc="vc")
                        tr_slot[bk] = last_ev
                    xnt_free[par] = last_ev
                t_allxnf = ctx.tok("vc")

                def do_pooled(v):
                    for ci in range(NCH):
                        v.reduce_sum(pooled[:, ci:ci + 1], xnf[:, ci * T_TOT:(ci + 1) * T_TOT], axis=AX.X)
                    v.drain()
                    return v.tensor_scalar_mul(pooled_b[:, :], pooled[:, :], 1.0 / T_TOT)
                t_pool = S.emit("vector", do_pooled, waits=[t_allxnf], inc="vc")
                t_h = mm_tile(S, ps[:, 1536:1537],
                              [(WG1[:, ci * 128:(ci + 1) * 128], pooled_b[:, ci:ci + 1], None, None)
                               for ci in range(NCH)], waits=[t_pool, ctx.qtoks("gd")], pad=xnf[:, 0:128])
                t_hr = S.emit("scalar", lambda a: a.activation(
                    hgate[:, :], ps[:, 1536:1537], AF.Relu, bias=bA[:, 26:27], scale=1.0),
                    waits=[t_h, t_bg1], inc="ac")
                t_lgmm = mm_tile(S, ps[0:1, 1540:1544], [(hgate[:, :], WG2[:, :], None, None)],
                                 waits=[t_hr], pad=xnf[:, 0:128])

                def gate(v):
                    v.tensor_add(lg[0:1, 0:4], lg[0:1, 28:32], ps[0:1, 1540:1544])
                    v.drain()
                    v.reduce_max(lg[0:1, 4:5], lg[0:1, 0:4], axis=AX.X)
                    v.drain()
                    v.tensor_scalar_mul(lg[0:1, 5:6], lg[0:1, 4:5], -1.0)
                    v.tensor_single_scalar(lg[0:1, 8:12], lg[0:1, 0:4], lg[0:1, 4:5], op=ALU.is_ge)
                    v.drain()
                    v.scalar_tensor_tensor(lg[0:1, 12:16], lg[0:1, 8:12], -1e30, lg[0:1, 0:4],
                                           op0=ALU.mult, op1=ALU.add)
                    v.drain()
                    return v.reduce_max(lg[0:1, 16:17], lg[0:1, 12:16], axis=AX.X)
                t_gate = S.emit("vector", gate, waits=[t_lgmm, t_bg2], inc="vc")
                t_e = S.emit("scalar", lambda a: a.activation(
                    lg[0:1, 20:24], lg[0:1, 0:4], AF.Exp, bias=lg[0:1, 5:6], scale=1.0),
                    waits=[t_gate], inc="ac")

                def gate2(v):
                    v.tensor_single_scalar(lg[0:1, 8:12], lg[0:1, 0:4], lg[0:1, 16:17], op=ALU.is_ge)
                    v.drain()
                    v.tensor_mul(lg[0:1, 24:28], lg[0:1, 20:24], lg[0:1, 8:12])
                    v.drain()
                    v.reduce_sum(lg[0:1, 17:18], lg[0:1, 24:28], axis=AX.X)
                    v.drain()
                    v.reciprocal(lg[0:1, 18:19], lg[0:1, 17:18])
                    v.drain()
                    v.tensor_single_scalar(lg[0:1, 24:28], lg[0:1, 24:28], lg[0:1, 18:19], op=ALU.mult)
                    v.drain()
                    return v.tensor_copy(rowsB[0:1, RB_W:RB_W + 4], lg[0:1, 24:28])
                t_g2 = S.emit("vector", gate2, waits=[t_e], inc="vc")
                S.emit("tensor", lambda t: t.transpose(
                    ps[0:4, 1544:1545], lg[0:1, 24:28], id1f[:, :]),
                    waits=[t_g2])
                t_wt = S.emit("tensor", lambda t: t.ldweights(xnf[:, 0:128]), inc="tn")
                t_wbc_mm = mm_tile(S, ps[:, 1548:1552],
                                   [(onesC[:, :], rowsB[0:1, RB_W:RB_W + 4], None, None)],
                                   waits=[t_g2], pad=xnf[:, 0:128])

                def wfin(v):
                    v.tensor_copy(wT4[:, :], ps[0:4, 1544:1545])
                    return v.tensor_copy(wbc[:, :], ps[:, 1548:1552])
                t_wfin = S.emit("vector", wfin, waits=[t_wt, t_wbc_mm], inc="vc")

                prev_fm = None
                for src_t, n, dcol in ((abq, D, 0), (abk, D, 8), (abo, D, 16), (abl, LAT, 24)):
                    t_r = S.dma("sync", "qd", lambda e, s=src_t, n=n: e.dma_start(
                        out=rowsA[0:1, 0:n], in_=s[None, :]),
                        waits=[prev_fm])
                    prev_fm = row_to_fm(S, ctx, ps, id1f, rowsA, n, bA, dcol, t_r)
                S.fence()
                S.realize(nc)

            if STOP_AFTER >= 2:
                emit_branch(nc, ctx, dict(
                kind="attn", Wqb=Wq_b, Wkb=Wkv_b, Wk_off=0, Wvb=Wkv_b,
                Wv_off=LCH * D, bq_col=0, bk_col=8, bias_fm=bA,
                bv_off=RB_BV, Wl=Wl_b, bl_col=24,
                ), xnf, attO, ps, latf, aux, None)

            with nc.sbuf_tensor("stage", [128, T_OWN], BF16) as stage:
                if STOP_AFTER < 3:
                    raise _StopEmit
                S = Sched(ctx)
                pslot = {0: [ctx.tok("vc"), ctx.tok("ac")], 1: [ctx.tok("vc"), ctx.tok("ac")]}
                prev_st = None
                for oc in range(NCH):
                    ev_toks = []
                    for nt in range(NQT):
                        sl = nt % 2
                        col = sl * 512
                        t_mm = mm_tile(S, ps[:, col:col + 512],
                                       [(Wo_b[:, pr * D + oc * 128:pr * D + oc * 128 + 128],
                                         attO[:, pr * T_OWN + nt * 512:pr * T_OWN + nt * 512 + 512],
                                         None, None) for pr in range(N_PAIRS)],
                                       waits=[pslot[sl]])
                        ev_toks.append(S.emit("vector", lambda v, nt=nt, col=col, oc=oc: v.tensor_scalar(
                            stage[:, nt * 512:nt * 512 + 512], ps[:, col:col + 512],
                            bA[:, 16 + oc:16 + oc + 1], None, op0=ALU.add),
                            waits=[t_mm, prev_st], inc="vc"))
                        pslot[sl] = [ev_toks[-1]]
                    prev_st = S.dma("sync", "qd", lambda e, oc=oc: e.dma_start(
                        out=spill[:, oc * T_OWN:(oc + 1) * T_OWN], in_=stage[:, :]),
                        waits=[ev_toks[-1]])
                S.fence()
                S.realize(nc)

        if STOP_AFTER < 4:
            raise _StopEmit
        with nc.sbuf_tensor("Wo2_b", [128, NCH * D], BF16) as Wo2_b:
            with nc.sbuf_tensor("Wc_b", [128, 2 * NCH * D], BF16) as Wc_b:
                with (
                    nc.sbuf_tensor("acc", [128, 2048], F32) as acc,
                    nc.sbuf_tensor("stm", [128, 2 * 2048], F32) as stm,
                    nc.sbuf_tensor("vstg", [128, 2048], BF16) as vstg,
                    nc.sbuf_tensor("bE", [4, 3 * D], BF16) as bE,
                ):
                    S = Sched(ctx)
                    S.dma("gpsimd", "gd", lambda g: g.dma_start(
                        out=Wo2_b[:, :].rearrange("p (c o) -> p c o", o=D),
                        in_=fWo.rearrange("(c p) o -> p c o", p=128)))
                    S.dma("gpsimd", "gd", lambda g: g.dma_start(out=bE[:, 0:D], in_=fbq[:, :]))
                    S.dma("gpsimd", "gd", lambda g: g.dma_start(out=bE[:, D:2 * D], in_=fbk[:, :]))
                    S.dma("gpsimd", "gd", lambda g: g.dma_start(out=bE[:, 2 * D:3 * D], in_=fbv[:, :]))
                    slices = [(P, sl) for P in range(3) for sl in range(4)]
                    dma_toks = {}
                    cons_toks = {}
                    prev_wcv = [None]
                    def emit_dma(idx, e):
                        P, sl = slices[idx]
                        buf = (idx * 4 + e) % 2
                        Wsrc = (fWq, fWk, fWv)[P]
                        w = [cons_toks.get((idx * 4 + e) - 2)]
                        eng, q = ("sync", "qd") if (idx * 4 + e) % 2 == 0 else ("gpsimd", "gd")
                        dma_toks[idx * 4 + e] = S.dma(eng, q, lambda s, e=e, sl=sl, buf=buf, Wsrc=Wsrc: s.dma_start(
                            out=stm[:, buf * 2048:(buf + 1) * 2048].rearrange("p (c o) -> p c o", o=256),
                            in_=Wsrc[e][:, sl * 256:(sl + 1) * 256].rearrange("(c p) o -> p c o", p=128)),
                            waits=w)
                    emit_dma(0, 0)
                    emit_dma(0, 1)
                    for idx, (P, sl) in enumerate(slices):
                        for e in range(4):
                            buf = (idx * 4 + e) % 2
                            if e == 0:
                                fn = lambda v, buf=buf: (v.drain(), v.tensor_scalar(
                                    acc[:, :], stm[:, buf * 2048:(buf + 1) * 2048],
                                    wbc[0:128, 0:1], None, op0=ALU.mult))[1]
                            else:
                                fn = lambda v, e=e, buf=buf: (v.drain(), v.scalar_tensor_tensor(
                                    acc[:, :], stm[:, buf * 2048:(buf + 1) * 2048],
                                    wbc[0:128, e:e + 1], acc[:, :],
                                    op0=ALU.mult, op1=ALU.add))[1]
                            cons_toks[idx * 4 + e] = S.emit("vector", fn, waits=[dma_toks[idx * 4 + e]], inc="vc")
                            nxt = idx * 4 + e + 2
                            if nxt < len(slices) * 4:
                                emit_dma(nxt // 4, nxt % 4)
                        if P < 2:
                            S.emit("vector", lambda v, P=P, sl=sl: (v.drain(), v.tensor_copy(
                                Wc_b[:, :].rearrange("p (P c o) -> p P c o", P=2, c=NCH)[:, P, :, sl * 256:(sl + 1) * 256],
                                acc[:, :].rearrange("p (c o) -> p c o", c=NCH)))[1], inc="vc")
                        else:
                            t_cv = S.emit("vector", lambda v: (v.drain(), v.tensor_copy(
                                vstg[:, :], acc[:, :]))[1], waits=[prev_wcv[0]], inc="vc")
                            prev_wcv[0] = S.dma("sync", "qd", lambda e2, sl=sl: e2.dma_start(
                                out=wcv.rearrange("p (c o) -> p c o", c=NCH)[:, :, sl * 256:(sl + 1) * 256],
                                in_=vstg[:, :].rearrange("p (c o) -> p c o", o=256)),
                                waits=[t_cv])
                    for P in range(3):
                        t1 = mm_tile(S, ps[0:1, 1024:1536],
                                     [(wT4[:, :], bE[:, P * D:P * D + 512], None, None)],
                                     waits=[ctx.tok("vc"), ctx.qtoks("gd")])
                        t2 = mm_tile(S, ps[0:1, 1536:2048],
                                     [(wT4[:, :], bE[:, P * D + 512:(P + 1) * D], None, None)],
                                     pad=xnf[:, 0:128])

                        if P < 2:
                            def cprow(v, P=P):
                                v.tensor_copy(rowsA[0:1, 0:512], ps[0:1, 1024:1536])
                                return v.tensor_copy(rowsA[0:1, 512:1024], ps[0:1, 1536:2048])
                            t_cp = S.emit("vector", cprow, waits=[t1, t2, ctx.tok("tn")], inc="vc")
                            row_to_fm(S, ctx, ps, id1f, rowsA, D, bF, P * 8, t_cp)
                        else:
                            def cprow(v, P=P):
                                v.tensor_copy(rowsB[0:1, P * D:P * D + 512], ps[0:1, 1024:1536])
                                return v.tensor_copy(rowsB[0:1, P * D + 512:(P + 1) * D], ps[0:1, 1536:2048])
                            S.emit("vector", cprow, waits=[t1, t2], inc="vc")
                    t_bo2 = S.dma("sync", "qd", lambda e: e.dma_start(
                        out=rowsA[0:1, 0:D], in_=fbo[None, :]), waits=[ctx.tok("tn"), ctx.tok("vc")])
                    row_to_fm(S, ctx, ps, id1f, rowsA, D, bF, 16, t_bo2)
                    S.fence()
                    S.realize(nc)

                emit_branch(nc, ctx, dict(
                    kind="ffn", Wqb=Wc_b, Wq_base=0, Wkb=Wc_b, Wk_base=NCH * D,
                    bq_col=0, bk_col=8, bias_fm=bF, bv_off=2 * D,
                ), xnf, attO, ps, None, aux, wcv)

            if STOP_AFTER < 6:
                raise _StopEmit
            with (
                nc.sbuf_tensor("fsl", [128, 4096], BF16) as fsl,
                nc.sbuf_tensor("sprd", [128, 2 * 4096], BF16) as sprd,
                nc.sbuf_tensor("dtok", [128, 1024], BF16) as dtok,
                nc.sbuf_tensor("d32", [128, 1024], F32) as d32,
                nc.sbuf_tensor("dbf", [128, 512], BF16) as dbf,
                nc.sbuf_tensor("xres", [128, 2 * 1024], F32) as xres,
                nc.sbuf_tensor("yt", [128, 1024], F32) as yt,
            ):
                S = Sched(ctx)
                sp_free = [None, None]
                xres_free = [None, None]
                yt_free = None
                dtok_free = None
                for nt in range(NQT):
                    par = nt % 2
                    t_sp = S.dma("sync", "qd", lambda e, nt=nt, par=par: e.dma_start(
                        out=sprd[:, par * 4096:(par + 1) * 4096].rearrange("p (c f) -> p c f", c=NCH),
                        in_=spill.rearrange("p (c q) -> p c q", c=NCH)[:, :, nt * 512:(nt + 1) * 512]),
                        waits=[sp_free[par]])
                    ev_toks = []
                    if nt == 0:
                        pslot = {0: [ctx.tok("vc"), ctx.tok("ac")], 1: [ctx.tok("vc"), ctx.tok("ac")]}
                    for oc in range(NCH):
                        col = (oc % 2) * 512
                        t_mm = mm_tile(S, ps[:, col:col + 512],
                                       [(Wo2_b[:, pr * D + oc * 128:pr * D + oc * 128 + 128],
                                         attO[:, pr * T_OWN + nt * 512:pr * T_OWN + nt * 512 + 512],
                                         None, None) for pr in range(N_PAIRS)],
                                       waits=[pslot[oc % 2]])

                        def ev(v, oc=oc, col=col, par=par):
                            v.drain()
                            v.tensor_scalar(dbf[:, 0:512], ps[:, col:col + 512],
                                            bF[:, 16 + oc:16 + oc + 1], None, op0=ALU.add)
                            v.drain()
                            r = None
                            for t4 in range(4):
                                r = v.tensor_add(
                                    fsl[:, t4 * 1024 + oc * 128:t4 * 1024 + (oc + 1) * 128],
                                    dbf[:, t4 * 128:(t4 + 1) * 128],
                                    sprd[:, par * 4096 + oc * 512 + t4 * 128:par * 4096 + oc * 512 + (t4 + 1) * 128])
                            return r
                        ev_toks.append(S.emit("vector", ev, waits=[t_mm, t_sp], inc="vc"))
                        pslot[oc % 2] = [ev_toks[-1]]
                    sp_free[par] = ev_toks[-1]
                    for t4 in range(4):
                        tok_row = nt * 512 + t4 * 128
                        t_tr = S.dma("scalar", "sd", lambda a, t4=t4: a.dma_start(
                            out=dtok[:, :].rearrange("p (c t) -> p c t", c=NCH),
                            in_=fsl[:, t4 * 1024:(t4 + 1) * 1024].rearrange("p (c f) -> p c f", c=NCH),
                            transpose=True), waits=[ev_toks[-1], dtok_free])
                        xp = t4 % 2
                        t_x = S.dma("sync", "qd", lambda e, tok_row=tok_row, xp=xp: e.dma_start(
                            out=xres[:, xp * D:(xp + 1) * D], in_=x_own[tok_row:tok_row + 128, :]),
                            waits=[xres_free[xp]])

                        def res(v, xp=xp):
                            v.tensor_copy(d32[:, :], dtok[:, :])
                            v.drain()
                            return v.tensor_add(yt[:, :], xres[:, xp * D:(xp + 1) * D], d32[:, :])
                        t_y = S.emit("vector", res, waits=[t_tr, t_x, yt_free], inc="vc")
                        dtok_free = t_y
                        xres_free[xp] = t_y
                        yt_free = S.dma("sync", "qd", lambda e, tok_row=tok_row: e.dma_start(
                            out=y[tok_row:tok_row + 128, :], in_=yt[:, :]),
                            waits=[t_y])
                S.emit("sync", lambda e: e.nop(), waits=[ctx.qtoks("qd"), ctx.qtoks("sd")])
                S.realize(nc)

            if DEBUG:
                S = Sched(ctx)
                S.dma("sync", "qd", lambda e: e.dma_start(out=dbg_xnf[:, :], in_=xnf[:, 0:4096]))
                S.dma("sync", "qd", lambda e: e.dma_start(out=dbg_attO[:, :], in_=attO[:, 0:4096]))
                S.dma("sync", "qd", lambda e: e.dma_start(out=dbg_lg[:, :], in_=lg[0:1, :]))
                S.dma("sync", "qd", lambda e: e.dma_start(out=dbg_qkm[:, :], in_=qkm[0:1, :]))
                S.dma("sync", "qd", lambda e: e.dma_start(out=dbg_rr[:, :], in_=rr[0:1, :]))
                S.dma("sync", "qd", lambda e: e.dma_start(out=dbg_negB[:, :], in_=negB[:, :]))
                S.emit("sync", lambda e: e.nop(), waits=[ctx.qtoks("qd")])
                S.realize(nc)
    except _StopEmit:
        pass

    return nc


def emit_branch(nc, ctx, cfg, xnf, attO, ps, latf, aux, wcv):
    negB, rr, rrb, Rsb = aux["negB"], aux["rr"], aux["rrb"], aux["Rsb"]
    onesC, uA, uB, ones128 = aux["onesC"], aux["uA"], aux["uB"], aux["ones128"]
    qkm, bvt = aux["qkm"], aux["bvt"]
    rowsA, rowsB, id1 = aux["rowsA"], aux["rowsB"], aux["id1"]
    bias_fm = cfg["bias_fm"]
    attn = cfg["kind"] == "attn"
    vgd = 128 if attn else 256
    n_vg = N_PAIRS if attn else N_GROUPS
    ppg = 1 if attn else 2

    with (
        nc.sbuf_tensor("Kp_%s" % cfg["kind"], [128, T_TOT], BF16) as Kp,
        nc.sbuf_tensor("Vg_%s" % cfg["kind"], [128, NKC * vgd], BF16) as Vg,
        nc.sbuf_tensor("Qp_%s" % cfg["kind"], [128, T_OWN], BF16) as Qp,
        nc.sbuf_tensor("pb_%s" % cfg["kind"], [128, 3 * 1024], BF16) as pbuf,
        contextlib.ExitStack() as bst,
    ):
        sqscr = pbuf
        Vwp = None
        if not attn:
            Vwp = bst.enter_context(nc.sbuf_tensor("Vwp", [128, NCH * vgd], BF16))
        S = Sched(ctx)
        if attn:
            pslot = {0: [ctx.tok("vc"), ctx.qtoks("sd"), ctx.qtoks("gd")],
                     1: [ctx.tok("vc"), ctx.qtoks("sd"), ctx.qtoks("gd")]}
            for lc in range(LCH):
                for nt8 in range(8):
                    sl = (lc * 8 + nt8) % 2
                    col = sl * 512
                    t_mm = mm_tile(S, ps[:, col:col + 512],
                                   [(cfg["Wl"][:, ci * LAT + lc * 128:ci * LAT + lc * 128 + 128],
                                     xnf[:, ci * T_TOT + nt8 * 512:ci * T_TOT + nt8 * 512 + 512],
                                     None, None) for ci in range(NCH)],
                                   waits=[pslot[sl]])
                    pslot[sl] = [S.emit("vector", lambda v, lc=lc, nt8=nt8, col=col: v.tensor_scalar(
                        latf[:, lc * T_TOT + nt8 * 512:lc * T_TOT + nt8 * 512 + 512],
                        ps[:, col:col + 512], bias_fm[:, cfg["bl_col"] + lc:cfg["bl_col"] + lc + 1],
                        None, op0=ALU.add), waits=[t_mm], inc="vc")]

        prev_unit = {"pv_evac": None, "rr_done": None, "exp30": None,
                     "exp31": None, "pvlast": None}
        for g in range(n_vg):
            bsrc = rowsB[0:1, cfg["bv_off"] + g * vgd:cfg["bv_off"] + (g + 1) * vgd]
            t_bmm = mm_tile(S, ps[:, 1024:1024 + vgd], [(onesC[:, :], bsrc, None, None)],
                            waits=[ctx.tok("vc"), ctx.qtoks("gd")], pad=xnf[:, 0:128])
            t_bvt = S.emit("vector", lambda v: v.tensor_copy(bvt[:, 0:vgd], ps[:, 1024:1024 + vgd]),
                           waits=[t_bmm], inc="vc")
            t_vw = None
            if not attn:
                t_vw = S.dma("sync", "qd", lambda e, g=g: e.dma_start(
                    out=Vwp[:, :].rearrange("p (c o) -> p c o", c=NCH),
                    in_=wcv.rearrange("p (c o) -> p c o", c=NCH)[:, :, g * vgd:(g + 1) * vgd]),
                    waits=[ctx.qtoks("qd"), ctx.tok("tn")])
            pslot = {0: [ctx.tok("vc"), ctx.tok("ac")], 1: [ctx.tok("vc"), ctx.tok("ac")]}
            for tt in range(NKC):
                sl = tt % 2
                col = sl * 512
                if attn:
                    pieces = [(latf[:, lc * T_TOT + tt * 128:lc * T_TOT + tt * 128 + 128],
                               cfg["Wvb"][:, cfg["Wv_off"] + lc * D + g * vgd:cfg["Wv_off"] + lc * D + (g + 1) * vgd],
                               None, None) for lc in range(LCH)]
                    w = [pslot[sl]]
                else:
                    pieces = [(xnf[:, ci * T_TOT + tt * 128:ci * T_TOT + tt * 128 + 128],
                               Vwp[:, ci * vgd:(ci + 1) * vgd],
                               None, None) for ci in range(NCH)]
                    w = [pslot[sl], t_vw]
                t_mm = mm_tile(S, ps[:, col:col + vgd], pieces, waits=w)
                pslot[sl] = [S.emit("vector", lambda v, tt=tt, col=col: v.tensor_add(
                    Vg[:, tt * vgd:(tt + 1) * vgd], ps[:, col:col + vgd], bvt[:, 0:vgd]),
                    waits=[t_mm, t_bvt], inc="vc")]

            for pc in range(ppg):
                pr = g * ppg + pc
                pslot = {0: [ctx.tok("vc"), ctx.tok("ac")], 1: [ctx.tok("vc"), ctx.tok("ac")]}
                for nt in range(NQT):
                    sl = nt % 2
                    col = sl * 512
                    base = 0 if attn else cfg["Wq_base"]
                    pieces = [(cfg["Wqb"][:, base + ci * D + pr * 128:base + ci * D + pr * 128 + 128],
                               xnf[:, ci * T_TOT + nt * 512:ci * T_TOT + nt * 512 + 512],
                               None, None) for ci in range(NCH)]
                    t_mm = mm_tile(S, ps[:, col:col + 512], pieces, waits=[pslot[sl]])
                    pslot[sl] = [S.emit("vector", lambda v, nt=nt, col=col, pr=pr: v.tensor_scalar(
                        Qp[:, nt * 512:nt * 512 + 512], ps[:, col:col + 512],
                        bias_fm[:, cfg["bq_col"] + pr:cfg["bq_col"] + pr + 1], None, op0=ALU.add),
                        waits=[t_mm], inc="vc")]
                for nt8 in range(8):
                    sl = nt8 % 2
                    col = sl * 512
                    if attn:
                        pieces = [(cfg["Wkb"][:, cfg["Wk_off"] + lc * D + pr * 128:cfg["Wk_off"] + lc * D + pr * 128 + 128],
                                   latf[:, lc * T_TOT + nt8 * 512:lc * T_TOT + nt8 * 512 + 512],
                                   None, None) for lc in range(LCH)]
                    else:
                        pieces = [(cfg["Wkb"][:, cfg["Wk_base"] + ci * D + pr * 128:cfg["Wk_base"] + ci * D + pr * 128 + 128],
                                   xnf[:, ci * T_TOT + nt8 * 512:ci * T_TOT + nt8 * 512 + 512],
                                   None, None) for ci in range(NCH)]
                    t_mm = mm_tile(S, ps[:, col:col + 512], pieces, waits=[pslot[sl]])
                    pslot[sl] = [S.emit("vector", lambda v, nt8=nt8, col=col, pr=pr: v.tensor_scalar(
                        Kp[:, nt8 * 512:nt8 * 512 + 512], ps[:, col:col + 512],
                        bias_fm[:, cfg["bk_col"] + pr:cfg["bk_col"] + pr + 1], None, op0=ALU.add),
                        waits=[t_mm], inc="vc")]

                t_qsq = S.emit("vector", lambda v: v.tensor_tensor(
                    sqscr[:, 0:2048], Qp[:, :], Qp[:, :], op=ALU.mult),
                    waits=[ctx.tok("vc"), ctx.tok("ac")], inc="vc")
                bcol = (1024, 1536)
                red_tok = [None, None]
                last = t_qsq
                seq = [("q", 0, nt) for nt in range(NQT)]
                seq += [("k", s2, q4) for s2 in range(2) for q4 in range(4)]
                ksq_tok = {}
                for j, (kind, s2, q4) in enumerate(seq):
                    par = j % 2
                    if kind == "k" and q4 == 0:
                        last = S.emit("vector", lambda v, s2=s2: (v.drain(), v.tensor_tensor(
                            sqscr[:, 0:2048], Kp[:, s2 * 2048:(s2 + 1) * 2048],
                            Kp[:, s2 * 2048:(s2 + 1) * 2048], op=ALU.mult))[1],
                            waits=[ctx.tok("tn")], inc="vc")
                    w = [last, red_tok[par]]
                    t_mm = mm_tile(S, ps[0:1, bcol[par]:bcol[par] + 512],
                                   [(ones128[:, :], sqscr[:, q4 * 512:(q4 + 1) * 512], None, None)],
                                   waits=w, pad=xnf[:, 0:128])
                    qi = q4 if kind == "q" else 8 + s2 * 4 + q4
                    red_tok[par] = S.emit("vector", lambda v, qi=qi, par=par: v.reduce_max(
                        qkm[0:1, qi:qi + 1], ps[0:1, bcol[par]:bcol[par] + 512], axis=AX.X),
                        waits=[t_mm], inc="vc")
                last = red_tok[0] if red_tok[0] and (red_tok[1] is None or red_tok[0][1] > red_tok[1][1]) else red_tok[1]
                last = S.emit("vector", lambda v: v.drain(), waits=[red_tok[0], red_tok[1]], inc="vc")

                def mk_negb(v):
                    v.drain()
                    v.reduce_max(qkm[0:1, 16:17], qkm[0:1, 8:16], axis=AX.X)
                    v.drain()
                    v.tensor_scalar(qkm[0:1, 20:24], qkm[0:1, 0:4], qkm[0:1, 16:17],
                                    -0.5 * SCALE, op0=ALU.add, op1=ALU.mult)
                    v.drain()
                    return v.tensor_copy(rowsB[0:1, RB_NB:RB_NB + 4], qkm[0:1, 20:24])
                t_nb = S.emit("vector", mk_negb, waits=[last, ctx.tok("ac")], inc="vc")
                t_nbm = mm_tile(S, ps[:, 1536:1540],
                                [(onesC[:, :], rowsB[0:1, RB_NB:RB_NB + 4], None, None)],
                                waits=[t_nb], pad=xnf[:, 0:128])
                t_nbf = S.emit("vector", lambda v: v.tensor_copy(negB[:, :], ps[:, 1536:1540]),
                               waits=[t_nbm], inc="vc")

                voff = pc * 128
                for nt in range(NQT):
                    exp_toks = []
                    pv_toks = []
                    qk_toks = []

                    def emit_qk(i, nt=nt):
                        j = i % 2
                        sc = PS_SLOT[j]
                        if i < 2:
                            w = [ctx.tok("vc"),
                                 prev_unit.get("exp30") if i == 0 else prev_unit.get("exp31")]
                        else:
                            w = [exp_toks[i - 2]]
                        S.emit("tensor", lambda t, i=i, sc=sc, nt=nt: t.matmul(
                            ps[:, sc:sc + 512], Kp[0:64, i * 128:(i + 1) * 128],
                            Qp[0:64, nt * 512:(nt + 1) * 512], start=True, stop=True,
                            tile_position=(0, 0)), waits=w)
                        qk_toks.append(S.emit("tensor", lambda t, i=i, sc=sc, nt=nt: t.matmul(
                            ps[:, sc + 512:sc + 1024], Kp[64:128, i * 128:(i + 1) * 128],
                            Qp[64:128, nt * 512:(nt + 1) * 512], start=True, stop=True,
                            tile_position=(64, 0)), inc="tn"))

                    for i in range(2):
                        emit_qk(i)
                    for i in range(NKC):
                        sc = PS_SLOT[i % 2]
                        pb = (i % 3) * 1024
                        wexp = [qk_toks[i]] + ([t_nbf] if i == 0 else [])
                        if i >= 3:
                            wexp.append(pv_toks[i - 3])
                        elif prev_unit.get("pvlast") is not None:
                            wexp.append(prev_unit["pvlast"])
                        exp_toks.append(S.emit("scalar", lambda a, sc=sc, pb=pb, nt=nt: a.activation(
                            pbuf[:, pb:pb + 1024], ps[:, sc:sc + 1024], AF.Exp,
                            bias=negB[:, nt:nt + 1], scale=SCALE), waits=wexp, inc="ac"))
                        if i + 2 < NKC:
                            emit_qk(i + 2)
                        wpv = [exp_toks[i]]
                        if i == 0:
                            wpv.append(prev_unit["pv_evac"])
                        S.emit("tensor", lambda t, i=i, pb=pb, voff=voff: t.matmul(
                            ps[0:64, PS_PVA:PS_PVA + 512],
                            Vg[:, i * vgd + voff:i * vgd + voff + 64],
                            pbuf[:, pb:pb + 512], start=(i == 0), stop=(i == NKC - 1),
                            tile_position=(0, 0)), waits=wpv)
                        S.emit("tensor", lambda t, i=i, pb=pb, voff=voff: t.matmul(
                            ps[64:128, PS_PVB:PS_PVB + 512],
                            Vg[:, i * vgd + voff + 64:i * vgd + voff + 128],
                            pbuf[:, pb + 512:pb + 1024], start=(i == 0), stop=(i == NKC - 1),
                            tile_position=(0, 64)))
                        wrho = [prev_unit["rr_done"], prev_unit["pv_evac"]] if i == 0 else ()
                        S.emit("tensor", lambda t, i=i, pb=pb: t.matmul(
                            ps[0:1, PS_RHA:PS_RHA + 512], ones128[:, :],
                            pbuf[:, pb:pb + 512], start=(i == 0), stop=(i == NKC - 1),
                            tile_position=(0, 0)), waits=wrho)
                        pv_toks.append(S.emit("tensor", lambda t, i=i, pb=pb: t.matmul(
                            ps[32:33, PS_RHB:PS_RHB + 512], ones128[:, :],
                            pbuf[:, pb + 512:pb + 1024], start=(i == 0), stop=(i == NKC - 1),
                            tile_position=(0, 32)), inc="tn"))

                    def rho_fin(v):
                        v.tensor_copy(rr[0:1, 0:512], ps[0:1, PS_RHA:PS_RHA + 512])
                        v.tensor_copy(rr[0:1, 512:1024], ps[32:33, PS_RHB:PS_RHB + 512])
                        v.drain()
                        v.reciprocal(rr[0:1, 0:1024], rr[0:1, 0:1024])
                        v.drain()
                        return v.tensor_copy(rrb[0:1, 0:1024], rr[0:1, 0:1024])
                    t_rr = S.emit("vector", rho_fin, waits=[pv_toks[NKC - 1]], inc="vc")
                    prev_unit["rr_done"] = t_rr
                    t_R = mm_tile(S, ps[:, PS_RHA:PS_RHA + 512],
                                  [(uA[:, :], rrb[0:1, 0:512], None, True),
                                   (uB[:, :], rrb[0:1, 512:1024], None, False)],
                                  waits=[t_rr], pad=xnf[:, 0:128])

                    def pvn(v, pr=pr, nt=nt):
                        v.tensor_copy(Rsb[:, :], ps[:, PS_RHA:PS_RHA + 512])
                        v.drain()
                        v.tensor_tensor(
                            attO[0:64, pr * T_OWN + nt * 512:pr * T_OWN + nt * 512 + 512],
                            ps[0:64, PS_PVA:PS_PVA + 512], Rsb[0:64, :], op=ALU.mult)
                        return v.tensor_tensor(
                            attO[64:128, pr * T_OWN + nt * 512:pr * T_OWN + nt * 512 + 512],
                            ps[64:128, PS_PVB:PS_PVB + 512], Rsb[64:128, :], op=ALU.mult)
                    t_pvn = S.emit("vector", pvn, waits=[t_R], inc="vc")
                    prev_unit["pv_evac"] = t_pvn
                    prev_unit["exp30"] = exp_toks[NKC - 2]
                    prev_unit["exp31"] = exp_toks[NKC - 1]
                    prev_unit["pvlast"] = pv_toks[NKC - 1]
        S.fence()
        S.realize(nc)


_NC_CACHE = {}


def kernel(**inputs):
    if "prog" not in _NC_CACHE:
        _NC_CACHE["prog"] = build_nc()
    nc = _NC_CACHE["prog"]

    X = np.ascontiguousarray(inputs["L0_attn_x"], dtype=np.float32)
    wm = dict(
        aWq=inputs["L0_attn_Wq"], abq=inputs["L0_attn_bq"], aWl=inputs["L0_attn_Wl"],
        abl=inputs["L0_attn_bl"], aWk=inputs["L0_attn_Wk"], abk=inputs["L0_attn_bk"],
        aWv=inputs["L0_attn_Wv"], abv=inputs["L0_attn_bv"], aWo=inputs["L0_attn_Wo"],
        abo=inputs["L0_attn_bo"], fWq=inputs["L0_ffn_Wq"], fbq=inputs["L0_ffn_bq"],
        fWk=inputs["L0_ffn_Wk"], fbk=inputs["L0_ffn_bk"], fWv=inputs["L0_ffn_Wv"],
        fbv=inputs["L0_ffn_bv"], fWg1=inputs["L0_ffn_Wg1"], fbg1=inputs["L0_ffn_bg1"],
        fWg2=inputs["L0_ffn_Wg2"], fbg2=inputs["L0_ffn_bg2"], fWo=inputs["L0_ffn_Wo"],
        fbo=inputs["L0_ffn_bo"],
    )
    wm = {k: np.ascontiguousarray(v, dtype=np.float32) for k, v in wm.items()}
    wm["cident"] = np.eye(128, dtype=np.float32)
    in_maps = []
    for c in range(8):
        b, h = c // 2, c % 2
        m = dict(wm)
        m["x_own"] = np.ascontiguousarray(X[b, h * T_OWN:(h + 1) * T_OWN])
        m["x_oth"] = np.ascontiguousarray(X[b, (1 - h) * T_OWN:(2 - h) * T_OWN])
        in_maps.append(m)

    import time
    out = np.empty((4, 4096, 1024), dtype=np.float32)

    def fill(first_core, results):
        for i, r in enumerate(results):
            c = first_core + i
            b, h = c // 2, c % 2
            out[b, h * T_OWN:(h + 1) * T_OWN] = r["y"]

    if TRY_FUSED:
        try:
            res = run_bass_kernel_spmd(nc, in_maps, core_ids=list(range(8)))
            fill(0, res.results)
            return out
        except Exception:
            time.sleep(10)
    for g in range(4):
        cores = [2 * g, 2 * g + 1]
        last = None
        for attempt in range(3):
            try:
                res = run_bass_kernel_spmd(nc, [in_maps[c] for c in cores], core_ids=cores)
                fill(2 * g, res.results)
                last = None
                break
            except Exception as e:
                last = e
                time.sleep(10)
        if last is not None:
            raise last
    return out


# version 15
# speedup vs baseline: 1.0202x; 1.0202x over previous
DEBUG = False
STOP_AFTER = 99
CAST_OFF = False
TRY_FUSED = True
import contextlib
import numpy as np
import concourse.bass as bass
import concourse.mybir as mybir
from concourse.bass_utils import run_bass_kernel_spmd

F32 = mybir.dt.float32
BF16 = mybir.dt.bfloat16
AF = mybir.ActivationFunctionType
ALU = mybir.AluOpType
AX = mybir.AxisListType

D = 1024
NCH = 8
LAT = 256
LCH = 2
T_OWN = 2048
T_TOT = 4096
N_PAIRS = 8
N_GROUPS = 4
GRP_D = 256
SCALE = 0.125
NQT = 4
NKC = 32

PS_SLOT = [0, 1024]
PS_PVA = 2048
PS_PVB = 2560
PS_RHA = 3072
PS_RHB = 3584

RB_BV = 3072
RB_ST = 4096
RB_W = 5120
RB_NB = 5184


class _StopEmit(Exception):
    pass


class Sched:
    def __init__(self, ctx):
        self.ctx = ctx
        self.items = {k: [] for k in ("sync", "gpsimd", "tensor", "vector", "scalar")}

    def emit(self, eng, fn, waits=(), inc=None, amt=1):
        tok = None
        flat = []

        def _add(w):
            if w is None:
                return
            if isinstance(w, (list, tuple)) and (len(w) != 2 or not isinstance(w[0], str)):
                for x in w:
                    _add(x)
            else:
                flat.append(w)
        for w in waits:
            _add(w)
        if inc is not None:
            self.ctx.n[inc] = self.ctx.n.get(inc, 0) + amt
            tok = (inc, self.ctx.n[inc])
        self.items[eng].append((fn, tuple(flat), inc, amt))
        return tok

    def fence(self):
        ctx = self.ctx
        w = []
        for q in ("qd", "sd", "gd"):
            w.extend(ctx.qtoks(q))
        self.emit("sync", lambda e: e.nop(), waits=w)

    def dma(self, eng, qname, fn, waits=()):
        ctx = self.ctx
        n = 1 if qname == "sd" else NCHAIN
        i = ctx.qrr[qname] = (ctx.qrr.get(qname, -1) + 1) % n
        sem = "%s%d" % (qname, i)
        prev = ctx.tok(sem)
        return self.emit(eng, fn, waits=list(waits) + [prev], inc=sem, amt=16)

    def realize(self, nc):
        ctx = self.ctx
        with nc.Block() as block:
            for eng_name in ("sync", "gpsimd", "tensor", "vector", "scalar"):
                items = self.items[eng_name]
                if not items:
                    continue

                def runner(e, items=items):
                    for fn, waits, inc, amt in items:
                        for (s, v) in waits:
                            if v > 0:
                                e.wait_ge(ctx.sems[s], v)
                        r = fn(e)
                        if inc is not None:
                            r.then_inc(ctx.sems[inc], amt)

                getattr(block, eng_name)(runner)


NCHAIN = 4


class Ctx:
    def __init__(self, nc, sems):
        self.nc = nc
        self.sems = sems
        self.n = {}
        self.qrr = {}

    def tok(self, name):
        return (name, self.n.get(name, 0))

    def qtoks(self, qname):
        return [self.tok("%s%d" % (qname, i)) for i in range(NCHAIN)]


def mm_tile(S, out_ap, pieces, waits=(), inc="tn", pad=None):
    n = len(pieces)
    tok = None
    for j, (lhsT, rhs, tp, st) in enumerate(pieces):
        first, last = j == 0, j == n - 1

        def fn(t, lhsT=lhsT, rhs=rhs, tp=tp, st=st, first=first, last=last):
            return t.matmul(out_ap, lhsT, rhs, start=st if st is not None else first,
                            stop=last, tile_position=tp)
        tok = S.emit("tensor", fn, waits=waits if j == 0 else (),
                     inc=None if (pad is not None and last) else (inc if last else None))
    if pad is not None:
        tok = S.emit("tensor", lambda t: t.ldweights(pad), inc=inc)
    return tok


PAD = [None]


def row_to_fm(S, ctx, ps, id1f, src_ap, n, dst, dst_col, src_tok):
    prev = src_tok
    for c in range(n // 128):
        pcol = 1552 + 2 * (c % 2)
        S.emit("tensor", lambda t, c=c, pcol=pcol: t.transpose(
            ps[:, pcol:pcol + 1], src_ap[0:1, c * 128:(c + 1) * 128],
            id1f[:, :]), waits=[prev])
        t_t = S.emit("tensor", lambda t: t.ldweights(PAD[0]), inc="tn")
        prev = S.emit("vector", lambda v, c=c, pcol=pcol: v.tensor_copy(
            dst[:, dst_col + c:dst_col + c + 1], ps[:, pcol:pcol + 1]),
            waits=[t_t], inc="vc")
    return prev


def build_nc():
    nc = bass.Bass("TRN2", target_bir_lowering=False)

    x_own = nc.dram_tensor("x_own", [T_OWN, D], F32, kind="ExternalInput").ap()
    x_oth = nc.dram_tensor("x_oth", [T_OWN, D], F32, kind="ExternalInput").ap()
    aWq = nc.dram_tensor("aWq", [D, D], F32, kind="ExternalInput").ap()
    abq = nc.dram_tensor("abq", [D], F32, kind="ExternalInput").ap()
    aWl = nc.dram_tensor("aWl", [D, LAT], F32, kind="ExternalInput").ap()
    abl = nc.dram_tensor("abl", [LAT], F32, kind="ExternalInput").ap()
    aWk = nc.dram_tensor("aWk", [LAT, D], F32, kind="ExternalInput").ap()
    abk = nc.dram_tensor("abk", [D], F32, kind="ExternalInput").ap()
    aWv = nc.dram_tensor("aWv", [LAT, D], F32, kind="ExternalInput").ap()
    abv = nc.dram_tensor("abv", [D], F32, kind="ExternalInput").ap()
    aWo = nc.dram_tensor("aWo", [D, D], F32, kind="ExternalInput").ap()
    abo = nc.dram_tensor("abo", [D], F32, kind="ExternalInput").ap()
    fWq = nc.dram_tensor("fWq", [4, D, D], F32, kind="ExternalInput").ap()
    fbq = nc.dram_tensor("fbq", [4, D], F32, kind="ExternalInput").ap()
    fWk = nc.dram_tensor("fWk", [4, D, D], F32, kind="ExternalInput").ap()
    fbk = nc.dram_tensor("fbk", [4, D], F32, kind="ExternalInput").ap()
    fWv = nc.dram_tensor("fWv", [4, D, D], F32, kind="ExternalInput").ap()
    fbv = nc.dram_tensor("fbv", [4, D], F32, kind="ExternalInput").ap()
    fWg1 = nc.dram_tensor("fWg1", [D, 128], F32, kind="ExternalInput").ap()
    fbg1 = nc.dram_tensor("fbg1", [128], F32, kind="ExternalInput").ap()
    fWg2 = nc.dram_tensor("fWg2", [128, 4], F32, kind="ExternalInput").ap()
    fbg2 = nc.dram_tensor("fbg2", [4], F32, kind="ExternalInput").ap()
    fWo = nc.dram_tensor("fWo", [D, D], F32, kind="ExternalInput").ap()
    fbo = nc.dram_tensor("fbo", [D], F32, kind="ExternalInput").ap()

    cident = nc.dram_tensor("cident", [128, 128], F32, kind="ExternalInput").ap()
    y = nc.dram_tensor("y", [T_OWN, D], F32, kind="ExternalOutput").ap()
    if DEBUG:
        dbg_xnf = nc.dram_tensor("dbg_xnf", [128, 4096], BF16, kind="ExternalOutput").ap()
        dbg_attO = nc.dram_tensor("dbg_attO", [128, 4096], BF16, kind="ExternalOutput").ap()
        dbg_lg = nc.dram_tensor("dbg_lg", [1, 32], F32, kind="ExternalOutput").ap()
        dbg_qkm = nc.dram_tensor("dbg_qkm", [1, 32], F32, kind="ExternalOutput").ap()
        dbg_rr = nc.dram_tensor("dbg_rr", [1, 1024], F32, kind="ExternalOutput").ap()
        dbg_negB = nc.dram_tensor("dbg_negB", [128, 4], F32, kind="ExternalOutput").ap()
    spill = nc.dram_tensor("spill", [128, NCH * T_OWN], BF16, kind="ExternalOutput" if DEBUG else "Internal").ap()
    wcv = nc.dram_tensor("wcv", [128, NCH * D], BF16, kind="Internal").ap()

    est = contextlib.ExitStack()
    try:
      with est:
        sems = {}
        for s in ("tn", "vc", "ac"):
            sems[s] = est.enter_context(nc.semaphore(s))
        for q in ("qd", "sd", "gd"):
            for i in range(4):
                n = "%s%d" % (q, i)
                sems[n] = est.enter_context(nc.semaphore(n))
        ctx = Ctx(nc, sems)

        xnf = est.enter_context(nc.sbuf_tensor("xnf", [128, NCH * T_TOT], BF16))
        PAD[0] = xnf[:, 0:128]
        attO = est.enter_context(nc.sbuf_tensor("attO", [128, N_PAIRS * T_OWN], BF16))
        ps = est.enter_context(nc.psum_tensor("ps", [128, 4096], F32))
        bA = est.enter_context(nc.sbuf_tensor("bA", [128, 32], F32))
        bF = est.enter_context(nc.sbuf_tensor("bF", [128, 32], F32))
        rowsA = est.enter_context(nc.sbuf_tensor("rowsA", [1, 1024], F32))
        rowsB = est.enter_context(nc.sbuf_tensor("rowsB", [1, 5376], BF16))
        onesC = est.enter_context(nc.sbuf_tensor("onesC", [1, 128], BF16))
        uA = est.enter_context(nc.sbuf_tensor("uA", [1, 128], BF16))
        uB = est.enter_context(nc.sbuf_tensor("uB", [1, 128], BF16))
        ones128 = est.enter_context(nc.sbuf_tensor("ones128", [128, 1], BF16))
        id1 = est.enter_context(nc.sbuf_tensor("id1", [1, 1], BF16))
        id1f = est.enter_context(nc.sbuf_tensor("id1f", [1, 1], F32))
        id128 = est.enter_context(nc.sbuf_tensor("id128", [128, 128], F32))
        negB = est.enter_context(nc.sbuf_tensor("negB", [128, 4], F32))
        wbc = est.enter_context(nc.sbuf_tensor("wbc", [128, 4], F32))
        wT4 = est.enter_context(nc.sbuf_tensor("wT4", [4, 1], BF16))
        lg = est.enter_context(nc.sbuf_tensor("lg", [1, 32], F32))
        qkm = est.enter_context(nc.sbuf_tensor("qkm", [1, 32], F32))
        rr = est.enter_context(nc.sbuf_tensor("rr", [1, 1024], F32))
        rrb = est.enter_context(nc.sbuf_tensor("rrb", [1, 1024], BF16))
        Rsb = est.enter_context(nc.sbuf_tensor("Rsb", [128, 512], F32))
        pooled = est.enter_context(nc.sbuf_tensor("pooled", [128, 8], F32))
        pooled_b = est.enter_context(nc.sbuf_tensor("pooled_b", [128, 8], BF16))
        hgate = est.enter_context(nc.sbuf_tensor("hgate", [128, 1], BF16))
        WG2 = est.enter_context(nc.sbuf_tensor("WG2", [128, 4], BF16))
        bvt = est.enter_context(nc.sbuf_tensor("bvt", [128, GRP_D], F32))

        S0 = Sched(ctx)
        S0.dma("sync", "qd", lambda e: e.dma_start(out=id128[:, :], in_=cident[:, :]))
        S0.emit("gpsimd", lambda g: g.memset(onesC[:, :], 1.0))
        S0.emit("gpsimd", lambda g: g.memset(uA[0:1, 0:64], 1.0))
        S0.emit("gpsimd", lambda g: g.memset(uA[0:1, 64:128], 0.0))
        S0.emit("gpsimd", lambda g: g.memset(uB[0:1, 0:64], 0.0))
        S0.emit("gpsimd", lambda g: g.memset(uB[0:1, 64:128], 1.0))
        S0.emit("gpsimd", lambda g: g.memset(ones128[:, :], 1.0))
        S0.emit("gpsimd", lambda g: g.memset(id1[:, :], 1.0))
        S0.emit("gpsimd", lambda g: g.memset(id1f[:, :], 1.0))
        S0.emit("gpsimd", lambda g: g.memset(lg[:, :], 0.0))
        S0.emit("gpsimd", lambda g: g.memset(qkm[:, :], 0.0))
        S0.fence()
        S0.realize(nc)

        aux = dict(negB=negB, rr=rr, rrb=rrb, Rsb=Rsb, onesC=onesC, uA=uA, uB=uB,
                   ones128=ones128, qkm=qkm, bvt=bvt, rowsA=rowsA, rowsB=rowsB,
                   id1=id1, id1f=id1f)

        with (
            nc.sbuf_tensor("Wq_b", [128, NCH * D], BF16) as Wq_b,
            nc.sbuf_tensor("Wl_b", [128, NCH * LAT], BF16) as Wl_b,
            nc.sbuf_tensor("Wkv_b", [128, 2 * LCH * D], BF16) as Wkv_b,
            nc.sbuf_tensor("Wo_b", [128, NCH * D], BF16) as Wo_b,
            nc.sbuf_tensor("latf", [128, LCH * T_TOT], BF16) as latf,
        ):
            with (
                nc.sbuf_tensor("xt", [128, 2 * D], F32) as xt,
                nc.sbuf_tensor("xsq", [128, D], F32) as xsq,
                nc.sbuf_tensor("xnt", [128, 2 * D], F32) as xnt,
                nc.sbuf_tensor("st8", [128, 16], F32) as st8,
                nc.sbuf_tensor("WG1", [128, NCH * 128], BF16) as WG1,
            ):
                S = Sched(ctx)

                def wload(dst_ap, src_ap):
                    if CAST_OFF:
                        return S.dma("gpsimd", "gd", lambda g: g.dma_start(
                            out=xt[:, 0:D], in_=aWq[0:128, :]))
                    return S.dma("gpsimd", "gd", lambda g, d=dst_ap, s=src_ap: g.dma_start(
                        out=d, in_=s))
                wload(Wq_b[:, :].rearrange("p (c o) -> p c o", o=D), aWq.rearrange("(c p) o -> p c o", p=128))
                wload(Wl_b[:, :].rearrange("p (c o) -> p c o", o=LAT), aWl.rearrange("(c p) o -> p c o", p=128))
                wload(Wkv_b[:, 0:LCH * D].rearrange("p (c o) -> p c o", o=D), aWk.rearrange("(c p) o -> p c o", p=128))
                wload(Wkv_b[:, LCH * D:2 * LCH * D].rearrange("p (c o) -> p c o", o=D), aWv.rearrange("(c p) o -> p c o", p=128))
                wload(Wo_b[:, :].rearrange("p (c o) -> p c o", o=D), aWo.rearrange("(c p) o -> p c o", p=128))
                wload(WG1[:, :].rearrange("p (c o) -> p c o", o=128), fWg1.rearrange("(c p) o -> p c o", p=128))
                wload(WG2[:, :], fWg2[:, :])

                t_bg1 = S.dma("sync", "qd", lambda e: e.dma_start(
                    out=bA[:, 26:27], in_=fbg1.rearrange("(p c) -> p c", c=1)))
                t_bg2 = S.dma("sync", "qd", lambda e: e.dma_start(
                    out=lg[0:1, 28:32], in_=fbg2[None, :]))
                wload(rowsB[0:1, RB_BV:RB_BV + D], abv[None, :])

                xt_free = [None, None]
                xnt_free = [None, None]
                tr_slot = [None, None, None, None]
                for it in range(32):
                    par = it % 2
                    src = x_own if it < 16 else x_oth
                    row0 = (it % 16) * 128
                    t_ld = S.dma("sync", "qd", lambda e, s=src, r=row0, par=par: e.dma_start(
                        out=xt[:, par * D:(par + 1) * D], in_=s[r:r + 128, :]),
                        waits=[xt_free[par]])
                    c0 = par * 8

                    def stats(v, par=par, c0=c0):
                        v.reduce_sum(st8[:, c0:c0 + 1], xt[:, par * D:(par + 1) * D], axis=AX.X)
                        v.tensor_tensor(xsq[:, :], xt[:, par * D:(par + 1) * D],
                                        xt[:, par * D:(par + 1) * D], op=ALU.mult)
                        v.drain()
                        v.reduce_sum(st8[:, c0 + 1:c0 + 2], xsq[:, :], axis=AX.X)
                        v.drain()
                        v.tensor_scalar_mul(st8[:, c0:c0 + 1], st8[:, c0:c0 + 1], 1.0 / D)
                        v.tensor_scalar_mul(st8[:, c0 + 1:c0 + 2], st8[:, c0 + 1:c0 + 2], 1.0 / D)
                        v.drain()
                        v.tensor_tensor(st8[:, c0 + 2:c0 + 3], st8[:, c0:c0 + 1],
                                        st8[:, c0:c0 + 1], op=ALU.mult)
                        v.drain()
                        v.tensor_sub(st8[:, c0 + 1:c0 + 2], st8[:, c0 + 1:c0 + 2], st8[:, c0 + 2:c0 + 3])
                        v.drain()
                        return v.tensor_scalar_add(st8[:, c0 + 1:c0 + 2], st8[:, c0 + 1:c0 + 2], 1e-5)
                    t_st = S.emit("vector", stats, waits=[t_ld], inc="vc")
                    t_sd = S.emit("scalar", lambda a, c0=c0: a.activation(
                        st8[:, c0 + 2:c0 + 3], st8[:, c0 + 1:c0 + 2], AF.Sqrt),
                        waits=[t_st], inc="ac")

                    def fin(v, c0=c0):
                        v.reciprocal(st8[:, c0 + 3:c0 + 4], st8[:, c0 + 2:c0 + 3])
                        v.drain()
                        v.tensor_tensor(st8[:, c0 + 4:c0 + 5], st8[:, c0:c0 + 1],
                                        st8[:, c0 + 3:c0 + 4], op=ALU.mult)
                        v.drain()
                        return v.tensor_scalar_mul(st8[:, c0 + 4:c0 + 5], st8[:, c0 + 4:c0 + 5], -1.0)
                    t_fin = S.emit("vector", fin, waits=[t_sd], inc="vc")
                    t_xn = S.emit("scalar", lambda a, par=par, c0=c0: a.activation(
                        xnt[:, par * D:(par + 1) * D], xt[:, par * D:(par + 1) * D], AF.Identity,
                        bias=st8[:, c0 + 4:c0 + 5], scale=st8[:, c0 + 3:c0 + 4]),
                        waits=[t_fin, xnt_free[par]], inc="ac")
                    xt_free[par] = t_xn
                    last_ev = None
                    for c in range(NCH):
                        bk = c % 4
                        S.emit("tensor", lambda t, par=par, c=c, bk=bk: t.transpose(
                            ps[:, bk * 512:bk * 512 + 128],
                            xnt[:, par * D + c * 128:par * D + (c + 1) * 128], id128[:, :]),
                            waits=[t_xn, tr_slot[bk]])
                        t_t = S.emit("tensor", lambda t: t.ldweights(xnf[:, 0:128]), inc="tn")
                        last_ev = S.emit("vector", lambda v, it=it, c=c, bk=bk: v.tensor_copy(
                            xnf[:, c * T_TOT + it * 128:c * T_TOT + (it + 1) * 128],
                            ps[:, bk * 512:bk * 512 + 128]), waits=[t_t], inc="vc")
                        tr_slot[bk] = last_ev
                    xnt_free[par] = last_ev
                t_allxnf = ctx.tok("vc")

                def do_pooled(v):
                    for ci in range(NCH):
                        v.reduce_sum(pooled[:, ci:ci + 1], xnf[:, ci * T_TOT:(ci + 1) * T_TOT], axis=AX.X)
                    v.drain()
                    return v.tensor_scalar_mul(pooled_b[:, :], pooled[:, :], 1.0 / T_TOT)
                t_pool = S.emit("vector", do_pooled, waits=[t_allxnf], inc="vc")
                t_h = mm_tile(S, ps[:, 1536:1537],
                              [(WG1[:, ci * 128:(ci + 1) * 128], pooled_b[:, ci:ci + 1], None, None)
                               for ci in range(NCH)], waits=[t_pool, ctx.qtoks("gd")], pad=xnf[:, 0:128])
                t_hr = S.emit("scalar", lambda a: a.activation(
                    hgate[:, :], ps[:, 1536:1537], AF.Relu, bias=bA[:, 26:27], scale=1.0),
                    waits=[t_h, t_bg1], inc="ac")
                t_lgmm = mm_tile(S, ps[0:1, 1540:1544], [(hgate[:, :], WG2[:, :], None, None)],
                                 waits=[t_hr], pad=xnf[:, 0:128])

                def gate(v):
                    v.tensor_add(lg[0:1, 0:4], lg[0:1, 28:32], ps[0:1, 1540:1544])
                    v.drain()
                    v.reduce_max(lg[0:1, 4:5], lg[0:1, 0:4], axis=AX.X)
                    v.drain()
                    v.tensor_scalar_mul(lg[0:1, 5:6], lg[0:1, 4:5], -1.0)
                    v.tensor_single_scalar(lg[0:1, 8:12], lg[0:1, 0:4], lg[0:1, 4:5], op=ALU.is_ge)
                    v.drain()
                    v.scalar_tensor_tensor(lg[0:1, 12:16], lg[0:1, 8:12], -1e30, lg[0:1, 0:4],
                                           op0=ALU.mult, op1=ALU.add)
                    v.drain()
                    return v.reduce_max(lg[0:1, 16:17], lg[0:1, 12:16], axis=AX.X)
                t_gate = S.emit("vector", gate, waits=[t_lgmm, t_bg2], inc="vc")
                t_e = S.emit("scalar", lambda a: a.activation(
                    lg[0:1, 20:24], lg[0:1, 0:4], AF.Exp, bias=lg[0:1, 5:6], scale=1.0),
                    waits=[t_gate], inc="ac")

                def gate2(v):
                    v.tensor_single_scalar(lg[0:1, 8:12], lg[0:1, 0:4], lg[0:1, 16:17], op=ALU.is_ge)
                    v.drain()
                    v.tensor_mul(lg[0:1, 24:28], lg[0:1, 20:24], lg[0:1, 8:12])
                    v.drain()
                    v.reduce_sum(lg[0:1, 17:18], lg[0:1, 24:28], axis=AX.X)
                    v.drain()
                    v.reciprocal(lg[0:1, 18:19], lg[0:1, 17:18])
                    v.drain()
                    v.tensor_single_scalar(lg[0:1, 24:28], lg[0:1, 24:28], lg[0:1, 18:19], op=ALU.mult)
                    v.drain()
                    return v.tensor_copy(rowsB[0:1, RB_W:RB_W + 4], lg[0:1, 24:28])
                t_g2 = S.emit("vector", gate2, waits=[t_e], inc="vc")
                S.emit("tensor", lambda t: t.transpose(
                    ps[0:4, 1544:1545], lg[0:1, 24:28], id1f[:, :]),
                    waits=[t_g2])
                t_wt = S.emit("tensor", lambda t: t.ldweights(xnf[:, 0:128]), inc="tn")
                t_wbc_mm = mm_tile(S, ps[:, 1548:1552],
                                   [(onesC[:, :], rowsB[0:1, RB_W:RB_W + 4], None, None)],
                                   waits=[t_g2], pad=xnf[:, 0:128])

                def wfin(v):
                    v.tensor_copy(wT4[:, :], ps[0:4, 1544:1545])
                    return v.tensor_copy(wbc[:, :], ps[:, 1548:1552])
                t_wfin = S.emit("vector", wfin, waits=[t_wt, t_wbc_mm], inc="vc")

                prev_fm = None
                for src_t, n, dcol in ((abq, D, 0), (abk, D, 8), (abo, D, 16), (abl, LAT, 24)):
                    t_r = S.dma("sync", "qd", lambda e, s=src_t, n=n: e.dma_start(
                        out=rowsA[0:1, 0:n], in_=s[None, :]),
                        waits=[prev_fm])
                    prev_fm = row_to_fm(S, ctx, ps, id1f, rowsA, n, bA, dcol, t_r)
                S.fence()
                S.realize(nc)

            if STOP_AFTER >= 2:
                emit_branch(nc, ctx, dict(
                kind="attn", Wqb=Wq_b, Wkb=Wkv_b, Wk_off=0, Wvb=Wkv_b,
                Wv_off=LCH * D, bq_col=0, bk_col=8, bias_fm=bA,
                bv_off=RB_BV, Wl=Wl_b, bl_col=24,
                ), xnf, attO, ps, latf, aux, None)

            with nc.sbuf_tensor("stage", [128, T_OWN], BF16) as stage:
                if STOP_AFTER < 3:
                    raise _StopEmit
                S = Sched(ctx)
                pslot = {0: [ctx.tok("vc"), ctx.tok("ac")], 1: [ctx.tok("vc"), ctx.tok("ac")]}
                prev_st = None
                for oc in range(NCH):
                    ev_toks = []
                    for nt in range(NQT):
                        sl = nt % 2
                        col = sl * 512
                        t_mm = mm_tile(S, ps[:, col:col + 512],
                                       [(Wo_b[:, pr * D + oc * 128:pr * D + oc * 128 + 128],
                                         attO[:, pr * T_OWN + nt * 512:pr * T_OWN + nt * 512 + 512],
                                         None, None) for pr in range(N_PAIRS)],
                                       waits=[pslot[sl]])
                        ev_toks.append(S.emit("vector", lambda v, nt=nt, col=col, oc=oc: v.tensor_scalar(
                            stage[:, nt * 512:nt * 512 + 512], ps[:, col:col + 512],
                            bA[:, 16 + oc:16 + oc + 1], None, op0=ALU.add),
                            waits=[t_mm, prev_st], inc="vc"))
                        pslot[sl] = [ev_toks[-1]]
                    prev_st = S.dma("sync", "qd", lambda e, oc=oc: e.dma_start(
                        out=spill[:, oc * T_OWN:(oc + 1) * T_OWN], in_=stage[:, :]),
                        waits=[ev_toks[-1]])
                S.fence()
                S.realize(nc)

        if STOP_AFTER < 4:
            raise _StopEmit
        with nc.sbuf_tensor("Wo2_b", [128, NCH * D], BF16) as Wo2_b:
            with nc.sbuf_tensor("Wc_b", [128, 2 * NCH * D], BF16) as Wc_b:
                with (
                    nc.sbuf_tensor("acc", [128, 2048], F32) as acc,
                    nc.sbuf_tensor("stm", [128, 2 * 2048], F32) as stm,
                    nc.sbuf_tensor("vstg", [128, 2048], BF16) as vstg,
                    nc.sbuf_tensor("bE", [4, 3 * D], BF16) as bE,
                ):
                    S = Sched(ctx)
                    S.dma("gpsimd", "gd", lambda g: g.dma_start(
                        out=Wo2_b[:, :].rearrange("p (c o) -> p c o", o=D),
                        in_=fWo.rearrange("(c p) o -> p c o", p=128)))
                    S.dma("gpsimd", "gd", lambda g: g.dma_start(out=bE[:, 0:D], in_=fbq[:, :]))
                    S.dma("gpsimd", "gd", lambda g: g.dma_start(out=bE[:, D:2 * D], in_=fbk[:, :]))
                    S.dma("gpsimd", "gd", lambda g: g.dma_start(out=bE[:, 2 * D:3 * D], in_=fbv[:, :]))
                    slices = [(P, sl) for P in range(3) for sl in range(4)]
                    dma_toks = {}
                    cons_toks = {}
                    prev_wcv = [None]
                    def emit_dma(idx, e):
                        P, sl = slices[idx]
                        buf = (idx * 4 + e) % 2
                        Wsrc = (fWq, fWk, fWv)[P]
                        w = [cons_toks.get((idx * 4 + e) - 2)]
                        eng, q = ("sync", "qd") if (idx * 4 + e) % 2 == 0 else ("gpsimd", "gd")
                        dma_toks[idx * 4 + e] = S.dma(eng, q, lambda s, e=e, sl=sl, buf=buf, Wsrc=Wsrc: s.dma_start(
                            out=stm[:, buf * 2048:(buf + 1) * 2048].rearrange("p (c o) -> p c o", o=256),
                            in_=Wsrc[e][:, sl * 256:(sl + 1) * 256].rearrange("(c p) o -> p c o", p=128)),
                            waits=w)
                    emit_dma(0, 0)
                    emit_dma(0, 1)
                    for idx, (P, sl) in enumerate(slices):
                        for e in range(4):
                            buf = (idx * 4 + e) % 2
                            if e == 0:
                                fn = lambda v, buf=buf: (v.drain(), v.tensor_scalar(
                                    acc[:, :], stm[:, buf * 2048:(buf + 1) * 2048],
                                    wbc[0:128, 0:1], None, op0=ALU.mult))[1]
                            else:
                                fn = lambda v, e=e, buf=buf: (v.drain(), v.scalar_tensor_tensor(
                                    acc[:, :], stm[:, buf * 2048:(buf + 1) * 2048],
                                    wbc[0:128, e:e + 1], acc[:, :],
                                    op0=ALU.mult, op1=ALU.add))[1]
                            cons_toks[idx * 4 + e] = S.emit("vector", fn, waits=[dma_toks[idx * 4 + e]], inc="vc")
                            nxt = idx * 4 + e + 2
                            if nxt < len(slices) * 4:
                                emit_dma(nxt // 4, nxt % 4)
                        if P < 2:
                            S.emit("vector", lambda v, P=P, sl=sl: (v.drain(), v.tensor_copy(
                                Wc_b[:, :].rearrange("p (P c o) -> p P c o", P=2, c=NCH)[:, P, :, sl * 256:(sl + 1) * 256],
                                acc[:, :].rearrange("p (c o) -> p c o", c=NCH)))[1], inc="vc")
                        else:
                            t_cv = S.emit("vector", lambda v: (v.drain(), v.tensor_copy(
                                vstg[:, :], acc[:, :]))[1], waits=[prev_wcv[0]], inc="vc")
                            prev_wcv[0] = S.dma("sync", "qd", lambda e2, sl=sl: e2.dma_start(
                                out=wcv.rearrange("p (c o) -> p c o", c=NCH)[:, :, sl * 256:(sl + 1) * 256],
                                in_=vstg[:, :].rearrange("p (c o) -> p c o", o=256)),
                                waits=[t_cv])
                    for P in range(3):
                        t1 = mm_tile(S, ps[0:1, 1024:1536],
                                     [(wT4[:, :], bE[:, P * D:P * D + 512], None, None)],
                                     waits=[ctx.tok("vc"), ctx.qtoks("gd")])
                        t2 = mm_tile(S, ps[0:1, 1536:2048],
                                     [(wT4[:, :], bE[:, P * D + 512:(P + 1) * D], None, None)],
                                     pad=xnf[:, 0:128])

                        if P < 2:
                            def cprow(v, P=P):
                                v.tensor_copy(rowsA[0:1, 0:512], ps[0:1, 1024:1536])
                                return v.tensor_copy(rowsA[0:1, 512:1024], ps[0:1, 1536:2048])
                            t_cp = S.emit("vector", cprow, waits=[t1, t2, ctx.tok("tn")], inc="vc")
                            row_to_fm(S, ctx, ps, id1f, rowsA, D, bF, P * 8, t_cp)
                        else:
                            def cprow(v, P=P):
                                v.tensor_copy(rowsB[0:1, P * D:P * D + 512], ps[0:1, 1024:1536])
                                return v.tensor_copy(rowsB[0:1, P * D + 512:(P + 1) * D], ps[0:1, 1536:2048])
                            S.emit("vector", cprow, waits=[t1, t2], inc="vc")
                    t_bo2 = S.dma("sync", "qd", lambda e: e.dma_start(
                        out=rowsA[0:1, 0:D], in_=fbo[None, :]), waits=[ctx.tok("tn"), ctx.tok("vc")])
                    row_to_fm(S, ctx, ps, id1f, rowsA, D, bF, 16, t_bo2)
                    S.fence()
                    S.realize(nc)

                emit_branch(nc, ctx, dict(
                    kind="ffn", Wqb=Wc_b, Wq_base=0, Wkb=Wc_b, Wk_base=NCH * D,
                    bq_col=0, bk_col=8, bias_fm=bF, bv_off=2 * D,
                ), xnf, attO, ps, None, aux, wcv)

            if STOP_AFTER < 6:
                raise _StopEmit
            with (
                nc.sbuf_tensor("fsl", [128, 4096], BF16) as fsl,
                nc.sbuf_tensor("sprd", [128, 2 * 4096], BF16) as sprd,
                nc.sbuf_tensor("dtok", [128, 1024], BF16) as dtok,
                nc.sbuf_tensor("d32", [128, 1024], F32) as d32,
                nc.sbuf_tensor("dbf", [128, 512], BF16) as dbf,
                nc.sbuf_tensor("xres", [128, 2 * 1024], F32) as xres,
                nc.sbuf_tensor("yt", [128, 1024], F32) as yt,
            ):
                S = Sched(ctx)
                sp_free = [None, None]
                xres_free = [None, None]
                yt_free = None
                dtok_free = None
                for nt in range(NQT):
                    par = nt % 2
                    t_sp = S.dma("sync", "qd", lambda e, nt=nt, par=par: e.dma_start(
                        out=sprd[:, par * 4096:(par + 1) * 4096].rearrange("p (c f) -> p c f", c=NCH),
                        in_=spill.rearrange("p (c q) -> p c q", c=NCH)[:, :, nt * 512:(nt + 1) * 512]),
                        waits=[sp_free[par]])
                    ev_toks = []
                    if nt == 0:
                        pslot = {0: [ctx.tok("vc"), ctx.tok("ac")], 1: [ctx.tok("vc"), ctx.tok("ac")]}
                    for oc in range(NCH):
                        col = (oc % 2) * 512
                        t_mm = mm_tile(S, ps[:, col:col + 512],
                                       [(Wo2_b[:, pr * D + oc * 128:pr * D + oc * 128 + 128],
                                         attO[:, pr * T_OWN + nt * 512:pr * T_OWN + nt * 512 + 512],
                                         None, None) for pr in range(N_PAIRS)],
                                       waits=[pslot[oc % 2]])

                        def ev(v, oc=oc, col=col, par=par):
                            v.drain()
                            v.tensor_scalar(dbf[:, 0:512], ps[:, col:col + 512],
                                            bF[:, 16 + oc:16 + oc + 1], None, op0=ALU.add)
                            v.drain()
                            r = None
                            for t4 in range(4):
                                r = v.tensor_add(
                                    fsl[:, t4 * 1024 + oc * 128:t4 * 1024 + (oc + 1) * 128],
                                    dbf[:, t4 * 128:(t4 + 1) * 128],
                                    sprd[:, par * 4096 + oc * 512 + t4 * 128:par * 4096 + oc * 512 + (t4 + 1) * 128])
                            return r
                        ev_toks.append(S.emit("vector", ev, waits=[t_mm, t_sp], inc="vc"))
                        pslot[oc % 2] = [ev_toks[-1]]
                    sp_free[par] = ev_toks[-1]
                    for t4 in range(4):
                        tok_row = nt * 512 + t4 * 128
                        t_tr = S.dma("scalar", "sd", lambda a, t4=t4: a.dma_start(
                            out=dtok[:, :].rearrange("p (c t) -> p c t", c=NCH),
                            in_=fsl[:, t4 * 1024:(t4 + 1) * 1024].rearrange("p (c f) -> p c f", c=NCH),
                            transpose=True), waits=[ev_toks[-1], dtok_free])
                        xp = t4 % 2
                        t_x = S.dma("sync", "qd", lambda e, tok_row=tok_row, xp=xp: e.dma_start(
                            out=xres[:, xp * D:(xp + 1) * D], in_=x_own[tok_row:tok_row + 128, :]),
                            waits=[xres_free[xp]])

                        def res(v, xp=xp):
                            v.tensor_copy(d32[:, :], dtok[:, :])
                            v.drain()
                            return v.tensor_add(yt[:, :], xres[:, xp * D:(xp + 1) * D], d32[:, :])
                        t_y = S.emit("vector", res, waits=[t_tr, t_x, yt_free], inc="vc")
                        dtok_free = t_y
                        xres_free[xp] = t_y
                        yt_free = S.dma("sync", "qd", lambda e, tok_row=tok_row: e.dma_start(
                            out=y[tok_row:tok_row + 128, :], in_=yt[:, :]),
                            waits=[t_y])
                S.emit("sync", lambda e: e.nop(), waits=[ctx.qtoks("qd"), ctx.qtoks("sd")])
                S.realize(nc)

            if DEBUG:
                S = Sched(ctx)
                S.dma("sync", "qd", lambda e: e.dma_start(out=dbg_xnf[:, :], in_=xnf[:, 0:4096]))
                S.dma("sync", "qd", lambda e: e.dma_start(out=dbg_attO[:, :], in_=attO[:, 0:4096]))
                S.dma("sync", "qd", lambda e: e.dma_start(out=dbg_lg[:, :], in_=lg[0:1, :]))
                S.dma("sync", "qd", lambda e: e.dma_start(out=dbg_qkm[:, :], in_=qkm[0:1, :]))
                S.dma("sync", "qd", lambda e: e.dma_start(out=dbg_rr[:, :], in_=rr[0:1, :]))
                S.dma("sync", "qd", lambda e: e.dma_start(out=dbg_negB[:, :], in_=negB[:, :]))
                S.emit("sync", lambda e: e.nop(), waits=[ctx.qtoks("qd")])
                S.realize(nc)
    except _StopEmit:
        pass

    return nc


def emit_branch(nc, ctx, cfg, xnf, attO, ps, latf, aux, wcv):
    negB, rr, rrb, Rsb = aux["negB"], aux["rr"], aux["rrb"], aux["Rsb"]
    onesC, uA, uB, ones128 = aux["onesC"], aux["uA"], aux["uB"], aux["ones128"]
    qkm, bvt = aux["qkm"], aux["bvt"]
    rowsA, rowsB, id1 = aux["rowsA"], aux["rowsB"], aux["id1"]
    bias_fm = cfg["bias_fm"]
    attn = cfg["kind"] == "attn"
    vgd = 128 if attn else 256
    n_vg = N_PAIRS if attn else N_GROUPS
    ppg = 1 if attn else 2

    with (
        nc.sbuf_tensor("Kp_%s" % cfg["kind"], [128, T_TOT], BF16) as Kp,
        nc.sbuf_tensor("Vg_%s" % cfg["kind"], [128, NKC * vgd], BF16) as Vg,
        nc.sbuf_tensor("Qp_%s" % cfg["kind"], [128, T_OWN], BF16) as Qp,
        nc.sbuf_tensor("pb_%s" % cfg["kind"], [128, 3 * 1024], BF16) as pbuf,
        contextlib.ExitStack() as bst,
    ):
        sqscr = pbuf
        Vwp = None
        if not attn:
            Vwp = bst.enter_context(nc.sbuf_tensor("Vwp", [128, NCH * vgd], BF16))
        S = Sched(ctx)
        if attn:
            pslot = {0: [ctx.tok("vc"), ctx.qtoks("sd"), ctx.qtoks("gd")],
                     1: [ctx.tok("vc"), ctx.qtoks("sd"), ctx.qtoks("gd")]}
            for lc in range(LCH):
                for nt8 in range(8):
                    sl = (lc * 8 + nt8) % 2
                    col = sl * 512
                    t_mm = mm_tile(S, ps[:, col:col + 512],
                                   [(cfg["Wl"][:, ci * LAT + lc * 128:ci * LAT + lc * 128 + 128],
                                     xnf[:, ci * T_TOT + nt8 * 512:ci * T_TOT + nt8 * 512 + 512],
                                     None, None) for ci in range(NCH)],
                                   waits=[pslot[sl]])
                    pslot[sl] = [S.emit("vector", lambda v, lc=lc, nt8=nt8, col=col: v.tensor_scalar(
                        latf[:, lc * T_TOT + nt8 * 512:lc * T_TOT + nt8 * 512 + 512],
                        ps[:, col:col + 512], bias_fm[:, cfg["bl_col"] + lc:cfg["bl_col"] + lc + 1],
                        None, op0=ALU.add), waits=[t_mm], inc="vc")]

        prev_unit = {"pv_evac": None, "rr_done": None, "exp30": None,
                     "exp31": None, "pvlast": None}
        for g in range(n_vg):
            bsrc = rowsB[0:1, cfg["bv_off"] + g * vgd:cfg["bv_off"] + (g + 1) * vgd]
            t_bmm = mm_tile(S, ps[:, 1024:1024 + vgd], [(onesC[:, :], bsrc, None, None)],
                            waits=[ctx.tok("vc"), ctx.qtoks("gd")], pad=xnf[:, 0:128])
            t_bvt = S.emit("vector", lambda v: v.tensor_copy(bvt[:, 0:vgd], ps[:, 1024:1024 + vgd]),
                           waits=[t_bmm], inc="vc")
            t_vw = None
            if not attn:
                t_vw = S.dma("sync", "qd", lambda e, g=g: e.dma_start(
                    out=Vwp[:, :].rearrange("p (c o) -> p c o", c=NCH),
                    in_=wcv.rearrange("p (c o) -> p c o", c=NCH)[:, :, g * vgd:(g + 1) * vgd]),
                    waits=[ctx.qtoks("qd"), ctx.tok("tn")])
            pslot = {0: [ctx.tok("vc"), ctx.tok("ac")], 1: [ctx.tok("vc"), ctx.tok("ac")]}
            for tt in range(NKC):
                sl = tt % 2
                col = sl * 512
                if attn:
                    pieces = [(latf[:, lc * T_TOT + tt * 128:lc * T_TOT + tt * 128 + 128],
                               cfg["Wvb"][:, cfg["Wv_off"] + lc * D + g * vgd:cfg["Wv_off"] + lc * D + (g + 1) * vgd],
                               None, None) for lc in range(LCH)]
                    w = [pslot[sl]]
                else:
                    pieces = [(xnf[:, ci * T_TOT + tt * 128:ci * T_TOT + tt * 128 + 128],
                               Vwp[:, ci * vgd:(ci + 1) * vgd],
                               None, None) for ci in range(NCH)]
                    w = [pslot[sl], t_vw]
                t_mm = mm_tile(S, ps[:, col:col + vgd], pieces, waits=w)
                pslot[sl] = [S.emit("vector", lambda v, tt=tt, col=col: v.tensor_add(
                    Vg[:, tt * vgd:(tt + 1) * vgd], ps[:, col:col + vgd], bvt[:, 0:vgd]),
                    waits=[t_mm, t_bvt], inc="vc")]

            for pc in range(ppg):
                pr = g * ppg + pc
                pslot = {0: [ctx.tok("vc"), ctx.tok("ac")], 1: [ctx.tok("vc"), ctx.tok("ac")]}
                for nt in range(NQT):
                    sl = nt % 2
                    col = sl * 512
                    base = 0 if attn else cfg["Wq_base"]
                    pieces = [(cfg["Wqb"][:, base + ci * D + pr * 128:base + ci * D + pr * 128 + 128],
                               xnf[:, ci * T_TOT + nt * 512:ci * T_TOT + nt * 512 + 512],
                               None, None) for ci in range(NCH)]
                    t_mm = mm_tile(S, ps[:, col:col + 512], pieces, waits=[pslot[sl]])
                    pslot[sl] = [S.emit("vector", lambda v, nt=nt, col=col, pr=pr: v.tensor_scalar(
                        Qp[:, nt * 512:nt * 512 + 512], ps[:, col:col + 512],
                        bias_fm[:, cfg["bq_col"] + pr:cfg["bq_col"] + pr + 1], None, op0=ALU.add),
                        waits=[t_mm], inc="vc")]
                for nt8 in range(8):
                    sl = nt8 % 2
                    col = sl * 512
                    if attn:
                        pieces = [(cfg["Wkb"][:, cfg["Wk_off"] + lc * D + pr * 128:cfg["Wk_off"] + lc * D + pr * 128 + 128],
                                   latf[:, lc * T_TOT + nt8 * 512:lc * T_TOT + nt8 * 512 + 512],
                                   None, None) for lc in range(LCH)]
                    else:
                        pieces = [(cfg["Wkb"][:, cfg["Wk_base"] + ci * D + pr * 128:cfg["Wk_base"] + ci * D + pr * 128 + 128],
                                   xnf[:, ci * T_TOT + nt8 * 512:ci * T_TOT + nt8 * 512 + 512],
                                   None, None) for ci in range(NCH)]
                    t_mm = mm_tile(S, ps[:, col:col + 512], pieces, waits=[pslot[sl]])
                    pslot[sl] = [S.emit("vector", lambda v, nt8=nt8, col=col, pr=pr: v.tensor_scalar(
                        Kp[:, nt8 * 512:nt8 * 512 + 512], ps[:, col:col + 512],
                        bias_fm[:, cfg["bk_col"] + pr:cfg["bk_col"] + pr + 1], None, op0=ALU.add),
                        waits=[t_mm], inc="vc")]

                t_qsq = S.emit("vector", lambda v: v.tensor_tensor(
                    sqscr[:, 0:2048], Qp[:, :], Qp[:, :], op=ALU.mult),
                    waits=[ctx.tok("vc"), ctx.tok("ac")], inc="vc")
                bcol = (1024, 1536)
                red_tok = [None, None]
                last = t_qsq
                seq = [("q", 0, nt) for nt in range(NQT)]
                seq += [("k", s2, q4) for s2 in range(2) for q4 in range(4)]
                ksq_tok = {}
                for j, (kind, s2, q4) in enumerate(seq):
                    par = j % 2
                    if kind == "k" and q4 == 0:
                        last = S.emit("vector", lambda v, s2=s2: (v.drain(), v.tensor_tensor(
                            sqscr[:, 0:2048], Kp[:, s2 * 2048:(s2 + 1) * 2048],
                            Kp[:, s2 * 2048:(s2 + 1) * 2048], op=ALU.mult))[1],
                            waits=[ctx.tok("tn")], inc="vc")
                    w = [last, red_tok[par]]
                    t_mm = mm_tile(S, ps[0:1, bcol[par]:bcol[par] + 512],
                                   [(ones128[:, :], sqscr[:, q4 * 512:(q4 + 1) * 512], None, None)],
                                   waits=w, pad=xnf[:, 0:128])
                    qi = q4 if kind == "q" else 8 + s2 * 4 + q4
                    red_tok[par] = S.emit("vector", lambda v, qi=qi, par=par: v.reduce_max(
                        qkm[0:1, qi:qi + 1], ps[0:1, bcol[par]:bcol[par] + 512], axis=AX.X),
                        waits=[t_mm], inc="vc")
                last = red_tok[0] if red_tok[0] and (red_tok[1] is None or red_tok[0][1] > red_tok[1][1]) else red_tok[1]
                last = S.emit("vector", lambda v: v.drain(), waits=[red_tok[0], red_tok[1]], inc="vc")

                def mk_negb(v):
                    v.drain()
                    v.reduce_max(qkm[0:1, 16:17], qkm[0:1, 8:16], axis=AX.X)
                    v.drain()
                    v.tensor_scalar(qkm[0:1, 20:24], qkm[0:1, 0:4], qkm[0:1, 16:17],
                                    -0.5 * SCALE, op0=ALU.add, op1=ALU.mult)
                    v.drain()
                    return v.tensor_copy(rowsB[0:1, RB_NB:RB_NB + 4], qkm[0:1, 20:24])
                t_nb = S.emit("vector", mk_negb, waits=[last, ctx.tok("ac")], inc="vc")
                t_nbm = mm_tile(S, ps[:, 1536:1540],
                                [(onesC[:, :], rowsB[0:1, RB_NB:RB_NB + 4], None, None)],
                                waits=[t_nb], pad=xnf[:, 0:128])
                t_nbf = S.emit("vector", lambda v: v.tensor_copy(negB[:, :], ps[:, 1536:1540]),
                               waits=[t_nbm], inc="vc")

                voff = pc * 128
                pending_ep = None
                for nt in range(NQT):
                    exp_toks = []
                    pv_toks = []
                    qk_toks = []

                    def emit_qk(i, nt=nt):
                        j = i % 2
                        sc = PS_SLOT[j]
                        if i < 2:
                            w = [ctx.tok("vc"),
                                 prev_unit.get("exp30") if i == 0 else prev_unit.get("exp31")]
                        else:
                            w = [exp_toks[i - 2]]
                        S.emit("tensor", lambda t, i=i, sc=sc, nt=nt: t.matmul(
                            ps[:, sc:sc + 512], Kp[0:64, i * 128:(i + 1) * 128],
                            Qp[0:64, nt * 512:(nt + 1) * 512], start=True, stop=True,
                            tile_position=(0, 0)), waits=w)
                        qk_toks.append(S.emit("tensor", lambda t, i=i, sc=sc, nt=nt: t.matmul(
                            ps[:, sc + 512:sc + 1024], Kp[64:128, i * 128:(i + 1) * 128],
                            Qp[64:128, nt * 512:(nt + 1) * 512], start=True, stop=True,
                            tile_position=(64, 0)), inc="tn"))

                    for i in range(2):
                        emit_qk(i)
                    if pending_ep is not None:
                        pending_ep()
                        pending_ep = None
                    for i in range(NKC):
                        sc = PS_SLOT[i % 2]
                        pb = (i % 3) * 1024
                        wexp = [qk_toks[i]] + ([t_nbf] if i == 0 else [])
                        if i >= 3:
                            wexp.append(pv_toks[i - 3])
                        elif prev_unit.get("pvlast") is not None:
                            wexp.append(prev_unit["pvlast"])
                        exp_toks.append(S.emit("scalar", lambda a, sc=sc, pb=pb, nt=nt: a.activation(
                            pbuf[:, pb:pb + 1024], ps[:, sc:sc + 1024], AF.Exp,
                            bias=negB[:, nt:nt + 1], scale=SCALE), waits=wexp, inc="ac"))
                        if i + 2 < NKC:
                            emit_qk(i + 2)
                        wpv = [exp_toks[i]]
                        if i == 0:
                            wpv.append(prev_unit["pv_evac"])
                        S.emit("tensor", lambda t, i=i, pb=pb, voff=voff: t.matmul(
                            ps[0:64, PS_PVA:PS_PVA + 512],
                            Vg[:, i * vgd + voff:i * vgd + voff + 64],
                            pbuf[:, pb:pb + 512], start=(i == 0), stop=(i == NKC - 1),
                            tile_position=(0, 0)), waits=wpv)
                        S.emit("tensor", lambda t, i=i, pb=pb, voff=voff: t.matmul(
                            ps[64:128, PS_PVB:PS_PVB + 512],
                            Vg[:, i * vgd + voff + 64:i * vgd + voff + 128],
                            pbuf[:, pb + 512:pb + 1024], start=(i == 0), stop=(i == NKC - 1),
                            tile_position=(0, 64)))
                        wrho = [prev_unit["rr_done"], prev_unit["pv_evac"]] if i == 0 else ()
                        S.emit("tensor", lambda t, i=i, pb=pb: t.matmul(
                            ps[0:1, PS_RHA:PS_RHA + 512], ones128[:, :],
                            pbuf[:, pb:pb + 512], start=(i == 0), stop=(i == NKC - 1),
                            tile_position=(0, 0)), waits=wrho)
                        pv_toks.append(S.emit("tensor", lambda t, i=i, pb=pb: t.matmul(
                            ps[32:33, PS_RHB:PS_RHB + 512], ones128[:, :],
                            pbuf[:, pb + 512:pb + 1024], start=(i == 0), stop=(i == NKC - 1),
                            tile_position=(0, 32)), inc="tn"))

                    def make_ep(pv_toks=pv_toks, pr=pr, nt=nt):
                        def ep():
                            def rho_fin(v):
                                v.tensor_copy(rr[0:1, 0:512], ps[0:1, PS_RHA:PS_RHA + 512])
                                v.tensor_copy(rr[0:1, 512:1024], ps[32:33, PS_RHB:PS_RHB + 512])
                                v.drain()
                                v.reciprocal(rr[0:1, 0:1024], rr[0:1, 0:1024])
                                v.drain()
                                return v.tensor_copy(rrb[0:1, 0:1024], rr[0:1, 0:1024])
                            t_rr = S.emit("vector", rho_fin, waits=[pv_toks[NKC - 1]], inc="vc")
                            prev_unit["rr_done"] = t_rr
                            t_R = mm_tile(S, ps[:, PS_RHA:PS_RHA + 512],
                                          [(uA[:, :], rrb[0:1, 0:512], None, True),
                                           (uB[:, :], rrb[0:1, 512:1024], None, False)],
                                          waits=[t_rr], pad=xnf[:, 0:128])

                            def pvn(v, pr=pr, nt=nt):
                                v.tensor_copy(Rsb[:, :], ps[:, PS_RHA:PS_RHA + 512])
                                v.drain()
                                v.tensor_tensor(
                                    attO[0:64, pr * T_OWN + nt * 512:pr * T_OWN + nt * 512 + 512],
                                    ps[0:64, PS_PVA:PS_PVA + 512], Rsb[0:64, :], op=ALU.mult)
                                return v.tensor_tensor(
                                    attO[64:128, pr * T_OWN + nt * 512:pr * T_OWN + nt * 512 + 512],
                                    ps[64:128, PS_PVB:PS_PVB + 512], Rsb[64:128, :], op=ALU.mult)
                            t_pvn = S.emit("vector", pvn, waits=[t_R], inc="vc")
                            prev_unit["pv_evac"] = t_pvn
                        return ep
                    prev_unit["exp30"] = exp_toks[NKC - 2]
                    prev_unit["exp31"] = exp_toks[NKC - 1]
                    prev_unit["pvlast"] = pv_toks[NKC - 1]
                    if nt == NQT - 1:
                        make_ep()()
                    else:
                        pending_ep = make_ep()
        S.fence()
        S.realize(nc)


_NC_CACHE = {}


def kernel(**inputs):
    if "prog" not in _NC_CACHE:
        _NC_CACHE["prog"] = build_nc()
    nc = _NC_CACHE["prog"]

    X = np.ascontiguousarray(inputs["L0_attn_x"], dtype=np.float32)
    wm = dict(
        aWq=inputs["L0_attn_Wq"], abq=inputs["L0_attn_bq"], aWl=inputs["L0_attn_Wl"],
        abl=inputs["L0_attn_bl"], aWk=inputs["L0_attn_Wk"], abk=inputs["L0_attn_bk"],
        aWv=inputs["L0_attn_Wv"], abv=inputs["L0_attn_bv"], aWo=inputs["L0_attn_Wo"],
        abo=inputs["L0_attn_bo"], fWq=inputs["L0_ffn_Wq"], fbq=inputs["L0_ffn_bq"],
        fWk=inputs["L0_ffn_Wk"], fbk=inputs["L0_ffn_bk"], fWv=inputs["L0_ffn_Wv"],
        fbv=inputs["L0_ffn_bv"], fWg1=inputs["L0_ffn_Wg1"], fbg1=inputs["L0_ffn_bg1"],
        fWg2=inputs["L0_ffn_Wg2"], fbg2=inputs["L0_ffn_bg2"], fWo=inputs["L0_ffn_Wo"],
        fbo=inputs["L0_ffn_bo"],
    )
    wm = {k: np.ascontiguousarray(v, dtype=np.float32) for k, v in wm.items()}
    wm["cident"] = np.eye(128, dtype=np.float32)
    in_maps = []
    for c in range(8):
        b, h = c // 2, c % 2
        m = dict(wm)
        m["x_own"] = np.ascontiguousarray(X[b, h * T_OWN:(h + 1) * T_OWN])
        m["x_oth"] = np.ascontiguousarray(X[b, (1 - h) * T_OWN:(2 - h) * T_OWN])
        in_maps.append(m)

    import time
    out = np.empty((4, 4096, 1024), dtype=np.float32)

    def fill(first_core, results):
        for i, r in enumerate(results):
            c = first_core + i
            b, h = c // 2, c % 2
            out[b, h * T_OWN:(h + 1) * T_OWN] = r["y"]

    if TRY_FUSED:
        try:
            res = run_bass_kernel_spmd(nc, in_maps, core_ids=list(range(8)))
            fill(0, res.results)
            return out
        except Exception:
            time.sleep(10)
    for g in range(4):
        cores = [2 * g, 2 * g + 1]
        last = None
        for attempt in range(3):
            try:
                res = run_bass_kernel_spmd(nc, [in_maps[c] for c in cores], core_ids=cores)
                fill(2 * g, res.results)
                last = None
                break
            except Exception as e:
                last = e
                time.sleep(10)
        if last is not None:
            raise last
    return out


# version 16
# speedup vs baseline: 1.0341x; 1.0137x over previous
DEBUG = False
STOP_AFTER = 99
CAST_OFF = False
TRY_FUSED = True
import contextlib
import numpy as np
import concourse.bass as bass
import concourse.mybir as mybir
from concourse.bass_utils import run_bass_kernel_spmd

F32 = mybir.dt.float32
BF16 = mybir.dt.bfloat16
AF = mybir.ActivationFunctionType
ALU = mybir.AluOpType
AX = mybir.AxisListType

D = 1024
NCH = 8
LAT = 256
LCH = 2
T_OWN = 2048
T_TOT = 4096
N_PAIRS = 8
N_GROUPS = 4
GRP_D = 256
SCALE = 0.125
NQT = 4
NKC = 32

PS_SLOT = [0, 1024]
PS_PVA = 2048
PS_PVB = 2560
PS_RHA = 3072
PS_RHB = 3584

RB_BV = 3072
RB_ST = 4096
RB_W = 5120
RB_NB = 5184


class _StopEmit(Exception):
    pass


class Sched:
    def __init__(self, ctx):
        self.ctx = ctx
        self.items = {k: [] for k in ("sync", "gpsimd", "tensor", "vector", "scalar")}

    def emit(self, eng, fn, waits=(), inc=None, amt=1):
        tok = None
        flat = []

        def _add(w):
            if w is None:
                return
            if isinstance(w, (list, tuple)) and (len(w) != 2 or not isinstance(w[0], str)):
                for x in w:
                    _add(x)
            else:
                flat.append(w)
        for w in waits:
            _add(w)
        if inc is not None:
            self.ctx.n[inc] = self.ctx.n.get(inc, 0) + amt
            tok = (inc, self.ctx.n[inc])
        self.items[eng].append((fn, tuple(flat), inc, amt))
        return tok

    def fence(self):
        ctx = self.ctx
        w = []
        for q in ("qd", "sd", "gd"):
            w.extend(ctx.qtoks(q))
        self.emit("sync", lambda e: e.nop(), waits=w)

    def dma(self, eng, qname, fn, waits=()):
        ctx = self.ctx
        n = 1 if qname == "sd" else NCHAIN
        i = ctx.qrr[qname] = (ctx.qrr.get(qname, -1) + 1) % n
        sem = "%s%d" % (qname, i)
        prev = ctx.tok(sem)
        return self.emit(eng, fn, waits=list(waits) + [prev], inc=sem, amt=16)

    def realize(self, nc):
        ctx = self.ctx
        with nc.Block() as block:
            for eng_name in ("sync", "gpsimd", "tensor", "vector", "scalar"):
                items = self.items[eng_name]
                if not items:
                    continue

                def runner(e, items=items):
                    for fn, waits, inc, amt in items:
                        for (s, v) in waits:
                            if v > 0:
                                e.wait_ge(ctx.sems[s], v)
                        r = fn(e)
                        if inc is not None:
                            r.then_inc(ctx.sems[inc], amt)

                getattr(block, eng_name)(runner)


NCHAIN = 4


class Ctx:
    def __init__(self, nc, sems):
        self.nc = nc
        self.sems = sems
        self.n = {}
        self.qrr = {}

    def tok(self, name):
        return (name, self.n.get(name, 0))

    def qtoks(self, qname):
        return [self.tok("%s%d" % (qname, i)) for i in range(NCHAIN)]


def mm_tile(S, out_ap, pieces, waits=(), inc="tn", pad=None):
    n = len(pieces)
    tok = None
    for j, (lhsT, rhs, tp, st) in enumerate(pieces):
        first, last = j == 0, j == n - 1

        def fn(t, lhsT=lhsT, rhs=rhs, tp=tp, st=st, first=first, last=last):
            return t.matmul(out_ap, lhsT, rhs, start=st if st is not None else first,
                            stop=last, tile_position=tp)
        tok = S.emit("tensor", fn, waits=waits if j == 0 else (),
                     inc=None if (pad is not None and last) else (inc if last else None))
    if pad is not None:
        tok = S.emit("tensor", lambda t: t.ldweights(pad), inc=inc)
    return tok


PAD = [None]


def row_to_fm(S, ctx, ps, id1f, src_ap, n, dst, dst_col, src_tok):
    prev = src_tok
    for c in range(n // 128):
        pcol = 1552 + 2 * (c % 2)
        S.emit("tensor", lambda t, c=c, pcol=pcol: t.transpose(
            ps[:, pcol:pcol + 1], src_ap[0:1, c * 128:(c + 1) * 128],
            id1f[:, :]), waits=[prev])
        t_t = S.emit("tensor", lambda t: t.ldweights(PAD[0]), inc="tn")
        prev = S.emit("vector", lambda v, c=c, pcol=pcol: v.tensor_copy(
            dst[:, dst_col + c:dst_col + c + 1], ps[:, pcol:pcol + 1]),
            waits=[t_t], inc="vc")
    return prev


def build_nc():
    nc = bass.Bass("TRN2", target_bir_lowering=False)

    x_own = nc.dram_tensor("x_own", [T_OWN, D], F32, kind="ExternalInput").ap()
    x_oth = nc.dram_tensor("x_oth", [T_OWN, D], F32, kind="ExternalInput").ap()
    aWq = nc.dram_tensor("aWq", [D, D], F32, kind="ExternalInput").ap()
    abq = nc.dram_tensor("abq", [D], F32, kind="ExternalInput").ap()
    aWl = nc.dram_tensor("aWl", [D, LAT], F32, kind="ExternalInput").ap()
    abl = nc.dram_tensor("abl", [LAT], F32, kind="ExternalInput").ap()
    aWk = nc.dram_tensor("aWk", [LAT, D], F32, kind="ExternalInput").ap()
    abk = nc.dram_tensor("abk", [D], F32, kind="ExternalInput").ap()
    aWv = nc.dram_tensor("aWv", [LAT, D], F32, kind="ExternalInput").ap()
    abv = nc.dram_tensor("abv", [D], F32, kind="ExternalInput").ap()
    aWo = nc.dram_tensor("aWo", [D, D], F32, kind="ExternalInput").ap()
    abo = nc.dram_tensor("abo", [D], F32, kind="ExternalInput").ap()
    fWq = nc.dram_tensor("fWq", [4, D, D], F32, kind="ExternalInput").ap()
    fbq = nc.dram_tensor("fbq", [4, D], F32, kind="ExternalInput").ap()
    fWk = nc.dram_tensor("fWk", [4, D, D], F32, kind="ExternalInput").ap()
    fbk = nc.dram_tensor("fbk", [4, D], F32, kind="ExternalInput").ap()
    fWv = nc.dram_tensor("fWv", [4, D, D], F32, kind="ExternalInput").ap()
    fbv = nc.dram_tensor("fbv", [4, D], F32, kind="ExternalInput").ap()
    fWg1 = nc.dram_tensor("fWg1", [D, 128], F32, kind="ExternalInput").ap()
    fbg1 = nc.dram_tensor("fbg1", [128], F32, kind="ExternalInput").ap()
    fWg2 = nc.dram_tensor("fWg2", [128, 4], F32, kind="ExternalInput").ap()
    fbg2 = nc.dram_tensor("fbg2", [4], F32, kind="ExternalInput").ap()
    fWo = nc.dram_tensor("fWo", [D, D], F32, kind="ExternalInput").ap()
    fbo = nc.dram_tensor("fbo", [D], F32, kind="ExternalInput").ap()

    cident = nc.dram_tensor("cident", [128, 128], F32, kind="ExternalInput").ap()
    y = nc.dram_tensor("y", [T_OWN, D], F32, kind="ExternalOutput").ap()
    if DEBUG:
        dbg_xnf = nc.dram_tensor("dbg_xnf", [128, 4096], BF16, kind="ExternalOutput").ap()
        dbg_attO = nc.dram_tensor("dbg_attO", [128, 4096], BF16, kind="ExternalOutput").ap()
        dbg_lg = nc.dram_tensor("dbg_lg", [1, 32], F32, kind="ExternalOutput").ap()
        dbg_qkm = nc.dram_tensor("dbg_qkm", [1, 32], F32, kind="ExternalOutput").ap()
        dbg_rr = nc.dram_tensor("dbg_rr", [1, 1024], F32, kind="ExternalOutput").ap()
        dbg_negB = nc.dram_tensor("dbg_negB", [128, 4], F32, kind="ExternalOutput").ap()
    spill = nc.dram_tensor("spill", [128, NCH * T_OWN], BF16, kind="ExternalOutput" if DEBUG else "Internal").ap()
    wcv = nc.dram_tensor("wcv", [128, NCH * D], BF16, kind="Internal").ap()

    est = contextlib.ExitStack()
    try:
      with est:
        sems = {}
        for s in ("tn", "vc", "ac"):
            sems[s] = est.enter_context(nc.semaphore(s))
        for q in ("qd", "sd", "gd"):
            for i in range(4):
                n = "%s%d" % (q, i)
                sems[n] = est.enter_context(nc.semaphore(n))
        ctx = Ctx(nc, sems)

        xnf = est.enter_context(nc.sbuf_tensor("xnf", [128, NCH * T_TOT], BF16))
        PAD[0] = xnf[:, 0:128]
        attO = est.enter_context(nc.sbuf_tensor("attO", [128, N_PAIRS * T_OWN], BF16))
        ps = est.enter_context(nc.psum_tensor("ps", [128, 4096], F32))
        bA = est.enter_context(nc.sbuf_tensor("bA", [128, 32], F32))
        bF = est.enter_context(nc.sbuf_tensor("bF", [128, 32], F32))
        rowsA = est.enter_context(nc.sbuf_tensor("rowsA", [1, 1024], F32))
        rowsB = est.enter_context(nc.sbuf_tensor("rowsB", [1, 5376], BF16))
        onesC = est.enter_context(nc.sbuf_tensor("onesC", [1, 128], BF16))
        uA = est.enter_context(nc.sbuf_tensor("uA", [1, 128], BF16))
        uB = est.enter_context(nc.sbuf_tensor("uB", [1, 128], BF16))
        ones128 = est.enter_context(nc.sbuf_tensor("ones128", [128, 1], BF16))
        id1 = est.enter_context(nc.sbuf_tensor("id1", [1, 1], BF16))
        id1f = est.enter_context(nc.sbuf_tensor("id1f", [1, 1], F32))
        id128 = est.enter_context(nc.sbuf_tensor("id128", [128, 128], F32))
        negB = est.enter_context(nc.sbuf_tensor("negB", [128, 4], F32))
        wbc = est.enter_context(nc.sbuf_tensor("wbc", [128, 4], F32))
        wT4 = est.enter_context(nc.sbuf_tensor("wT4", [4, 1], BF16))
        lg = est.enter_context(nc.sbuf_tensor("lg", [1, 32], F32))
        qkm = est.enter_context(nc.sbuf_tensor("qkm", [1, 32], F32))
        rr = est.enter_context(nc.sbuf_tensor("rr", [1, 1024], F32))
        rrb = est.enter_context(nc.sbuf_tensor("rrb", [1, 1024], BF16))
        Rsb = est.enter_context(nc.sbuf_tensor("Rsb", [128, 512], F32))
        pooled = est.enter_context(nc.sbuf_tensor("pooled", [128, 8], F32))
        pooled_b = est.enter_context(nc.sbuf_tensor("pooled_b", [128, 8], BF16))
        hgate = est.enter_context(nc.sbuf_tensor("hgate", [128, 1], BF16))
        WG2 = est.enter_context(nc.sbuf_tensor("WG2", [128, 4], BF16))
        bvt = est.enter_context(nc.sbuf_tensor("bvt", [128, GRP_D], F32))

        S0 = Sched(ctx)
        S0.dma("sync", "qd", lambda e: e.dma_start(out=id128[:, :], in_=cident[:, :]))
        S0.emit("gpsimd", lambda g: g.memset(onesC[:, :], 1.0))
        S0.emit("gpsimd", lambda g: g.memset(uA[0:1, 0:64], 1.0))
        S0.emit("gpsimd", lambda g: g.memset(uA[0:1, 64:128], 0.0))
        S0.emit("gpsimd", lambda g: g.memset(uB[0:1, 0:64], 0.0))
        S0.emit("gpsimd", lambda g: g.memset(uB[0:1, 64:128], 1.0))
        S0.emit("gpsimd", lambda g: g.memset(ones128[:, :], 1.0))
        S0.emit("gpsimd", lambda g: g.memset(id1[:, :], 1.0))
        S0.emit("gpsimd", lambda g: g.memset(id1f[:, :], 1.0))
        S0.emit("gpsimd", lambda g: g.memset(lg[:, :], 0.0))
        S0.emit("gpsimd", lambda g: g.memset(qkm[:, :], 0.0))
        S0.fence()
        S0.realize(nc)

        aux = dict(negB=negB, rr=rr, rrb=rrb, Rsb=Rsb, onesC=onesC, uA=uA, uB=uB,
                   ones128=ones128, qkm=qkm, bvt=bvt, rowsA=rowsA, rowsB=rowsB,
                   id1=id1, id1f=id1f)

        with (
            nc.sbuf_tensor("Wq_b", [128, NCH * D], BF16) as Wq_b,
            nc.sbuf_tensor("Wl_b", [128, NCH * LAT], BF16) as Wl_b,
            nc.sbuf_tensor("Wkv_b", [128, 2 * LCH * D], BF16) as Wkv_b,
            nc.sbuf_tensor("Wo_b", [128, NCH * D], BF16) as Wo_b,
            nc.sbuf_tensor("latf", [128, LCH * T_TOT], BF16) as latf,
        ):
            with (
                nc.sbuf_tensor("xt", [128, 2 * D], F32) as xt,
                nc.sbuf_tensor("xsq", [128, D], F32) as xsq,
                nc.sbuf_tensor("xnt", [128, 2 * D], F32) as xnt,
                nc.sbuf_tensor("st8", [128, 16], F32) as st8,
                nc.sbuf_tensor("WG1", [128, NCH * 128], BF16) as WG1,
            ):
                S = Sched(ctx)

                def wload(dst_ap, src_ap):
                    if CAST_OFF:
                        return S.dma("gpsimd", "gd", lambda g: g.dma_start(
                            out=xt[:, 0:D], in_=aWq[0:128, :]))
                    return S.dma("gpsimd", "gd", lambda g, d=dst_ap, s=src_ap: g.dma_start(
                        out=d, in_=s))
                wload(Wq_b[:, :].rearrange("p (c o) -> p c o", o=D), aWq.rearrange("(c p) o -> p c o", p=128))
                wload(Wl_b[:, :].rearrange("p (c o) -> p c o", o=LAT), aWl.rearrange("(c p) o -> p c o", p=128))
                wload(Wkv_b[:, 0:LCH * D].rearrange("p (c o) -> p c o", o=D), aWk.rearrange("(c p) o -> p c o", p=128))
                wload(Wkv_b[:, LCH * D:2 * LCH * D].rearrange("p (c o) -> p c o", o=D), aWv.rearrange("(c p) o -> p c o", p=128))
                wload(Wo_b[:, :].rearrange("p (c o) -> p c o", o=D), aWo.rearrange("(c p) o -> p c o", p=128))
                wload(WG1[:, :].rearrange("p (c o) -> p c o", o=128), fWg1.rearrange("(c p) o -> p c o", p=128))
                wload(WG2[:, :], fWg2[:, :])

                t_bg1 = S.dma("sync", "qd", lambda e: e.dma_start(
                    out=bA[:, 26:27], in_=fbg1.rearrange("(p c) -> p c", c=1)))
                t_bg2 = S.dma("sync", "qd", lambda e: e.dma_start(
                    out=lg[0:1, 28:32], in_=fbg2[None, :]))
                wload(rowsB[0:1, RB_BV:RB_BV + D], abv[None, :])

                xt_free = [None, None]
                xnt_free = [None, None]
                tr_slot = [None, None, None, None]
                for it in range(32):
                    par = it % 2
                    src = x_own if it < 16 else x_oth
                    row0 = (it % 16) * 128
                    t_ld = S.dma("sync", "qd", lambda e, s=src, r=row0, par=par: e.dma_start(
                        out=xt[:, par * D:(par + 1) * D], in_=s[r:r + 128, :]),
                        waits=[xt_free[par]])
                    c0 = par * 8

                    def stats(v, par=par, c0=c0):
                        v.reduce_sum(st8[:, c0:c0 + 1], xt[:, par * D:(par + 1) * D], axis=AX.X)
                        v.tensor_tensor(xsq[:, :], xt[:, par * D:(par + 1) * D],
                                        xt[:, par * D:(par + 1) * D], op=ALU.mult)
                        v.drain()
                        v.reduce_sum(st8[:, c0 + 1:c0 + 2], xsq[:, :], axis=AX.X)
                        v.drain()
                        v.tensor_scalar_mul(st8[:, c0:c0 + 1], st8[:, c0:c0 + 1], 1.0 / D)
                        v.tensor_scalar_mul(st8[:, c0 + 1:c0 + 2], st8[:, c0 + 1:c0 + 2], 1.0 / D)
                        v.drain()
                        v.tensor_tensor(st8[:, c0 + 2:c0 + 3], st8[:, c0:c0 + 1],
                                        st8[:, c0:c0 + 1], op=ALU.mult)
                        v.drain()
                        v.tensor_sub(st8[:, c0 + 1:c0 + 2], st8[:, c0 + 1:c0 + 2], st8[:, c0 + 2:c0 + 3])
                        v.drain()
                        return v.tensor_scalar_add(st8[:, c0 + 1:c0 + 2], st8[:, c0 + 1:c0 + 2], 1e-5)
                    t_st = S.emit("vector", stats, waits=[t_ld], inc="vc")
                    t_sd = S.emit("scalar", lambda a, c0=c0: a.activation(
                        st8[:, c0 + 2:c0 + 3], st8[:, c0 + 1:c0 + 2], AF.Sqrt),
                        waits=[t_st], inc="ac")

                    def fin(v, c0=c0):
                        v.reciprocal(st8[:, c0 + 3:c0 + 4], st8[:, c0 + 2:c0 + 3])
                        v.drain()
                        v.tensor_tensor(st8[:, c0 + 4:c0 + 5], st8[:, c0:c0 + 1],
                                        st8[:, c0 + 3:c0 + 4], op=ALU.mult)
                        v.drain()
                        return v.tensor_scalar_mul(st8[:, c0 + 4:c0 + 5], st8[:, c0 + 4:c0 + 5], -1.0)
                    t_fin = S.emit("vector", fin, waits=[t_sd], inc="vc")
                    t_xn = S.emit("scalar", lambda a, par=par, c0=c0: a.activation(
                        xnt[:, par * D:(par + 1) * D], xt[:, par * D:(par + 1) * D], AF.Identity,
                        bias=st8[:, c0 + 4:c0 + 5], scale=st8[:, c0 + 3:c0 + 4]),
                        waits=[t_fin, xnt_free[par]], inc="ac")
                    xt_free[par] = t_xn
                    last_ev = None
                    for c in range(NCH):
                        bk = c % 4
                        S.emit("tensor", lambda t, par=par, c=c, bk=bk: t.transpose(
                            ps[:, bk * 512:bk * 512 + 128],
                            xnt[:, par * D + c * 128:par * D + (c + 1) * 128], id128[:, :]),
                            waits=[t_xn, tr_slot[bk]])
                        t_t = S.emit("tensor", lambda t: t.ldweights(xnf[:, 0:128]), inc="tn")
                        last_ev = S.emit("vector", lambda v, it=it, c=c, bk=bk: v.tensor_copy(
                            xnf[:, c * T_TOT + it * 128:c * T_TOT + (it + 1) * 128],
                            ps[:, bk * 512:bk * 512 + 128]), waits=[t_t], inc="vc")
                        tr_slot[bk] = last_ev
                    xnt_free[par] = last_ev
                t_allxnf = ctx.tok("vc")

                def do_pooled(v):
                    for ci in range(NCH):
                        v.reduce_sum(pooled[:, ci:ci + 1], xnf[:, ci * T_TOT:(ci + 1) * T_TOT], axis=AX.X)
                    v.drain()
                    return v.tensor_scalar_mul(pooled_b[:, :], pooled[:, :], 1.0 / T_TOT)
                t_pool = S.emit("vector", do_pooled, waits=[t_allxnf], inc="vc")
                t_h = mm_tile(S, ps[:, 1536:1537],
                              [(WG1[:, ci * 128:(ci + 1) * 128], pooled_b[:, ci:ci + 1], None, None)
                               for ci in range(NCH)], waits=[t_pool, ctx.qtoks("gd")], pad=xnf[:, 0:128])
                t_hr = S.emit("scalar", lambda a: a.activation(
                    hgate[:, :], ps[:, 1536:1537], AF.Relu, bias=bA[:, 26:27], scale=1.0),
                    waits=[t_h, t_bg1], inc="ac")
                t_lgmm = mm_tile(S, ps[0:1, 1540:1544], [(hgate[:, :], WG2[:, :], None, None)],
                                 waits=[t_hr], pad=xnf[:, 0:128])

                def gate(v):
                    v.tensor_add(lg[0:1, 0:4], lg[0:1, 28:32], ps[0:1, 1540:1544])
                    v.drain()
                    v.reduce_max(lg[0:1, 4:5], lg[0:1, 0:4], axis=AX.X)
                    v.drain()
                    v.tensor_scalar_mul(lg[0:1, 5:6], lg[0:1, 4:5], -1.0)
                    v.tensor_single_scalar(lg[0:1, 8:12], lg[0:1, 0:4], lg[0:1, 4:5], op=ALU.is_ge)
                    v.drain()
                    v.scalar_tensor_tensor(lg[0:1, 12:16], lg[0:1, 8:12], -1e30, lg[0:1, 0:4],
                                           op0=ALU.mult, op1=ALU.add)
                    v.drain()
                    return v.reduce_max(lg[0:1, 16:17], lg[0:1, 12:16], axis=AX.X)
                t_gate = S.emit("vector", gate, waits=[t_lgmm, t_bg2], inc="vc")
                t_e = S.emit("scalar", lambda a: a.activation(
                    lg[0:1, 20:24], lg[0:1, 0:4], AF.Exp, bias=lg[0:1, 5:6], scale=1.0),
                    waits=[t_gate], inc="ac")

                def gate2(v):
                    v.tensor_single_scalar(lg[0:1, 8:12], lg[0:1, 0:4], lg[0:1, 16:17], op=ALU.is_ge)
                    v.drain()
                    v.tensor_mul(lg[0:1, 24:28], lg[0:1, 20:24], lg[0:1, 8:12])
                    v.drain()
                    v.reduce_sum(lg[0:1, 17:18], lg[0:1, 24:28], axis=AX.X)
                    v.drain()
                    v.reciprocal(lg[0:1, 18:19], lg[0:1, 17:18])
                    v.drain()
                    v.tensor_single_scalar(lg[0:1, 24:28], lg[0:1, 24:28], lg[0:1, 18:19], op=ALU.mult)
                    v.drain()
                    return v.tensor_copy(rowsB[0:1, RB_W:RB_W + 4], lg[0:1, 24:28])
                t_g2 = S.emit("vector", gate2, waits=[t_e], inc="vc")
                S.emit("tensor", lambda t: t.transpose(
                    ps[0:4, 1544:1545], lg[0:1, 24:28], id1f[:, :]),
                    waits=[t_g2])
                t_wt = S.emit("tensor", lambda t: t.ldweights(xnf[:, 0:128]), inc="tn")
                t_wbc_mm = mm_tile(S, ps[:, 1548:1552],
                                   [(onesC[:, :], rowsB[0:1, RB_W:RB_W + 4], None, None)],
                                   waits=[t_g2], pad=xnf[:, 0:128])

                def wfin(v):
                    v.tensor_copy(wT4[:, :], ps[0:4, 1544:1545])
                    return v.tensor_copy(wbc[:, :], ps[:, 1548:1552])
                t_wfin = S.emit("vector", wfin, waits=[t_wt, t_wbc_mm], inc="vc")

                prev_fm = None
                for src_t, n, dcol in ((abq, D, 0), (abk, D, 8), (abo, D, 16), (abl, LAT, 24)):
                    t_r = S.dma("sync", "qd", lambda e, s=src_t, n=n: e.dma_start(
                        out=rowsA[0:1, 0:n], in_=s[None, :]),
                        waits=[prev_fm])
                    prev_fm = row_to_fm(S, ctx, ps, id1f, rowsA, n, bA, dcol, t_r)
                S.fence()
                S.realize(nc)

            if STOP_AFTER >= 2:
                emit_branch(nc, ctx, dict(
                kind="attn", Wqb=Wq_b, Wkb=Wkv_b, Wk_off=0, Wvb=Wkv_b,
                Wv_off=LCH * D, bq_col=0, bk_col=8, bias_fm=bA,
                bv_off=RB_BV, Wl=Wl_b, bl_col=24,
                ), xnf, attO, ps, latf, aux, None)

            with nc.sbuf_tensor("stage", [128, T_OWN], BF16) as stage:
                if STOP_AFTER < 3:
                    raise _StopEmit
                S = Sched(ctx)
                pslot = {0: [ctx.tok("vc"), ctx.tok("ac")], 1: [ctx.tok("vc"), ctx.tok("ac")]}
                prev_st = None
                for oc in range(NCH):
                    ev_toks = []
                    for nt in range(NQT):
                        sl = nt % 2
                        col = sl * 512
                        t_mm = mm_tile(S, ps[:, col:col + 512],
                                       [(Wo_b[:, pr * D + oc * 128:pr * D + oc * 128 + 128],
                                         attO[:, pr * T_OWN + nt * 512:pr * T_OWN + nt * 512 + 512],
                                         None, None) for pr in range(N_PAIRS)],
                                       waits=[pslot[sl]])
                        ev_toks.append(S.emit("vector", lambda v, nt=nt, col=col, oc=oc: v.tensor_scalar(
                            stage[:, nt * 512:nt * 512 + 512], ps[:, col:col + 512],
                            bA[:, 16 + oc:16 + oc + 1], None, op0=ALU.add),
                            waits=[t_mm, prev_st], inc="vc"))
                        pslot[sl] = [ev_toks[-1]]
                    prev_st = S.dma("sync", "qd", lambda e, oc=oc: e.dma_start(
                        out=spill[:, oc * T_OWN:(oc + 1) * T_OWN], in_=stage[:, :]),
                        waits=[ev_toks[-1]])
                S.fence()
                S.realize(nc)

        if STOP_AFTER < 4:
            raise _StopEmit
        with nc.sbuf_tensor("Wo2_b", [128, NCH * D], BF16) as Wo2_b:
            with nc.sbuf_tensor("Wc_b", [128, 2 * NCH * D], BF16) as Wc_b:
                with (
                    nc.sbuf_tensor("acc", [128, 2048], F32) as acc,
                    nc.sbuf_tensor("stm", [128, 2 * 2048], F32) as stm,
                    nc.sbuf_tensor("vstg", [128, 2048], BF16) as vstg,
                    nc.sbuf_tensor("bE", [4, 3 * D], BF16) as bE,
                ):
                    S = Sched(ctx)
                    S.dma("gpsimd", "gd", lambda g: g.dma_start(
                        out=Wo2_b[:, :].rearrange("p (c o) -> p c o", o=D),
                        in_=fWo.rearrange("(c p) o -> p c o", p=128)))
                    S.dma("gpsimd", "gd", lambda g: g.dma_start(out=bE[:, 0:D], in_=fbq[:, :]))
                    S.dma("gpsimd", "gd", lambda g: g.dma_start(out=bE[:, D:2 * D], in_=fbk[:, :]))
                    S.dma("gpsimd", "gd", lambda g: g.dma_start(out=bE[:, 2 * D:3 * D], in_=fbv[:, :]))
                    slices = [(P, sl) for P in range(3) for sl in range(4)]
                    dma_toks = {}
                    cons_toks = {}
                    prev_wcv = [None]
                    def emit_dma(idx, e):
                        P, sl = slices[idx]
                        buf = (idx * 4 + e) % 2
                        Wsrc = (fWq, fWk, fWv)[P]
                        w = [cons_toks.get((idx * 4 + e) - 2)]
                        eng, q = ("sync", "qd") if (idx * 4 + e) % 2 == 0 else ("gpsimd", "gd")
                        dma_toks[idx * 4 + e] = S.dma(eng, q, lambda s, e=e, sl=sl, buf=buf, Wsrc=Wsrc: s.dma_start(
                            out=stm[:, buf * 2048:(buf + 1) * 2048].rearrange("p (c o) -> p c o", o=256),
                            in_=Wsrc[e][:, sl * 256:(sl + 1) * 256].rearrange("(c p) o -> p c o", p=128)),
                            waits=w)
                    emit_dma(0, 0)
                    emit_dma(0, 1)
                    for idx, (P, sl) in enumerate(slices):
                        for e in range(4):
                            buf = (idx * 4 + e) % 2
                            if e == 0:
                                fn = lambda v, buf=buf: (v.drain(), v.tensor_scalar(
                                    acc[:, :], stm[:, buf * 2048:(buf + 1) * 2048],
                                    wbc[0:128, 0:1], None, op0=ALU.mult))[1]
                            else:
                                fn = lambda v, e=e, buf=buf: (v.drain(), v.scalar_tensor_tensor(
                                    acc[:, :], stm[:, buf * 2048:(buf + 1) * 2048],
                                    wbc[0:128, e:e + 1], acc[:, :],
                                    op0=ALU.mult, op1=ALU.add))[1]
                            cons_toks[idx * 4 + e] = S.emit("vector", fn, waits=[dma_toks[idx * 4 + e]], inc="vc")
                            nxt = idx * 4 + e + 2
                            if nxt < len(slices) * 4:
                                emit_dma(nxt // 4, nxt % 4)
                        if P < 2:
                            S.emit("vector", lambda v, P=P, sl=sl: (v.drain(), v.tensor_copy(
                                Wc_b[:, :].rearrange("p (P c o) -> p P c o", P=2, c=NCH)[:, P, :, sl * 256:(sl + 1) * 256],
                                acc[:, :].rearrange("p (c o) -> p c o", c=NCH)))[1], inc="vc")
                        else:
                            t_cv = S.emit("vector", lambda v: (v.drain(), v.tensor_copy(
                                vstg[:, :], acc[:, :]))[1], waits=[prev_wcv[0]], inc="vc")
                            prev_wcv[0] = S.dma("sync", "qd", lambda e2, sl=sl: e2.dma_start(
                                out=wcv.rearrange("p (c o) -> p c o", c=NCH)[:, :, sl * 256:(sl + 1) * 256],
                                in_=vstg[:, :].rearrange("p (c o) -> p c o", o=256)),
                                waits=[t_cv])
                    for P in range(3):
                        t1 = mm_tile(S, ps[0:1, 1024:1536],
                                     [(wT4[:, :], bE[:, P * D:P * D + 512], None, None)],
                                     waits=[ctx.tok("vc"), ctx.qtoks("gd")])
                        t2 = mm_tile(S, ps[0:1, 1536:2048],
                                     [(wT4[:, :], bE[:, P * D + 512:(P + 1) * D], None, None)],
                                     pad=xnf[:, 0:128])

                        if P < 2:
                            def cprow(v, P=P):
                                v.tensor_copy(rowsA[0:1, 0:512], ps[0:1, 1024:1536])
                                return v.tensor_copy(rowsA[0:1, 512:1024], ps[0:1, 1536:2048])
                            t_cp = S.emit("vector", cprow, waits=[t1, t2, ctx.tok("tn")], inc="vc")
                            row_to_fm(S, ctx, ps, id1f, rowsA, D, bF, P * 8, t_cp)
                        else:
                            def cprow(v, P=P):
                                v.tensor_copy(rowsB[0:1, P * D:P * D + 512], ps[0:1, 1024:1536])
                                return v.tensor_copy(rowsB[0:1, P * D + 512:(P + 1) * D], ps[0:1, 1536:2048])
                            S.emit("vector", cprow, waits=[t1, t2], inc="vc")
                    t_bo2 = S.dma("sync", "qd", lambda e: e.dma_start(
                        out=rowsA[0:1, 0:D], in_=fbo[None, :]), waits=[ctx.tok("tn"), ctx.tok("vc")])
                    row_to_fm(S, ctx, ps, id1f, rowsA, D, bF, 16, t_bo2)
                    S.fence()
                    S.realize(nc)

                emit_branch(nc, ctx, dict(
                    kind="ffn", Wqb=Wc_b, Wq_base=0, Wkb=Wc_b, Wk_base=NCH * D,
                    bq_col=0, bk_col=8, bias_fm=bF, bv_off=2 * D,
                ), xnf, attO, ps, None, aux, wcv)

            if STOP_AFTER < 6:
                raise _StopEmit
            with (
                nc.sbuf_tensor("fsl", [128, 4096], BF16) as fsl,
                nc.sbuf_tensor("sprd", [128, 2 * 4096], BF16) as sprd,
                nc.sbuf_tensor("dtok", [128, 1024], BF16) as dtok,
                nc.sbuf_tensor("d32", [128, 1024], F32) as d32,
                nc.sbuf_tensor("dbf", [128, 512], BF16) as dbf,
                nc.sbuf_tensor("xres", [128, 2 * 1024], F32) as xres,
                nc.sbuf_tensor("yt", [128, 1024], F32) as yt,
            ):
                S = Sched(ctx)
                sp_free = [None, None]
                xres_free = [None, None]
                yt_free = None
                dtok_free = None
                for nt in range(NQT):
                    par = nt % 2
                    t_sp = S.dma("sync", "qd", lambda e, nt=nt, par=par: e.dma_start(
                        out=sprd[:, par * 4096:(par + 1) * 4096].rearrange("p (c f) -> p c f", c=NCH),
                        in_=spill.rearrange("p (c q) -> p c q", c=NCH)[:, :, nt * 512:(nt + 1) * 512]),
                        waits=[sp_free[par]])
                    ev_toks = []
                    if nt == 0:
                        pslot = {0: [ctx.tok("vc"), ctx.tok("ac")], 1: [ctx.tok("vc"), ctx.tok("ac")]}
                    for oc in range(NCH):
                        col = (oc % 2) * 512
                        t_mm = mm_tile(S, ps[:, col:col + 512],
                                       [(Wo2_b[:, pr * D + oc * 128:pr * D + oc * 128 + 128],
                                         attO[:, pr * T_OWN + nt * 512:pr * T_OWN + nt * 512 + 512],
                                         None, None) for pr in range(N_PAIRS)],
                                       waits=[pslot[oc % 2]])

                        def ev(v, oc=oc, col=col, par=par):
                            v.drain()
                            v.tensor_scalar(dbf[:, 0:512], ps[:, col:col + 512],
                                            bF[:, 16 + oc:16 + oc + 1], None, op0=ALU.add)
                            v.drain()
                            r = None
                            for t4 in range(4):
                                r = v.tensor_add(
                                    fsl[:, t4 * 1024 + oc * 128:t4 * 1024 + (oc + 1) * 128],
                                    dbf[:, t4 * 128:(t4 + 1) * 128],
                                    sprd[:, par * 4096 + oc * 512 + t4 * 128:par * 4096 + oc * 512 + (t4 + 1) * 128])
                            return r
                        ev_toks.append(S.emit("vector", ev, waits=[t_mm, t_sp], inc="vc"))
                        pslot[oc % 2] = [ev_toks[-1]]
                    sp_free[par] = ev_toks[-1]
                    for t4 in range(4):
                        tok_row = nt * 512 + t4 * 128
                        t_tr = S.dma("scalar", "sd", lambda a, t4=t4: a.dma_start(
                            out=dtok[:, :].rearrange("p (c t) -> p c t", c=NCH),
                            in_=fsl[:, t4 * 1024:(t4 + 1) * 1024].rearrange("p (c f) -> p c f", c=NCH),
                            transpose=True), waits=[ev_toks[-1], dtok_free])
                        xp = t4 % 2
                        t_x = S.dma("sync", "qd", lambda e, tok_row=tok_row, xp=xp: e.dma_start(
                            out=xres[:, xp * D:(xp + 1) * D], in_=x_own[tok_row:tok_row + 128, :]),
                            waits=[xres_free[xp]])

                        def res(v, xp=xp):
                            v.tensor_copy(d32[:, :], dtok[:, :])
                            v.drain()
                            return v.tensor_add(yt[:, :], xres[:, xp * D:(xp + 1) * D], d32[:, :])
                        t_y = S.emit("vector", res, waits=[t_tr, t_x, yt_free], inc="vc")
                        dtok_free = t_y
                        xres_free[xp] = t_y
                        yt_free = S.dma("sync", "qd", lambda e, tok_row=tok_row: e.dma_start(
                            out=y[tok_row:tok_row + 128, :], in_=yt[:, :]),
                            waits=[t_y])
                S.emit("sync", lambda e: e.nop(), waits=[ctx.qtoks("qd"), ctx.qtoks("sd")])
                S.realize(nc)

            if DEBUG:
                S = Sched(ctx)
                S.dma("sync", "qd", lambda e: e.dma_start(out=dbg_xnf[:, :], in_=xnf[:, 0:4096]))
                S.dma("sync", "qd", lambda e: e.dma_start(out=dbg_attO[:, :], in_=attO[:, 0:4096]))
                S.dma("sync", "qd", lambda e: e.dma_start(out=dbg_lg[:, :], in_=lg[0:1, :]))
                S.dma("sync", "qd", lambda e: e.dma_start(out=dbg_qkm[:, :], in_=qkm[0:1, :]))
                S.dma("sync", "qd", lambda e: e.dma_start(out=dbg_rr[:, :], in_=rr[0:1, :]))
                S.dma("sync", "qd", lambda e: e.dma_start(out=dbg_negB[:, :], in_=negB[:, :]))
                S.emit("sync", lambda e: e.nop(), waits=[ctx.qtoks("qd")])
                S.realize(nc)
    except _StopEmit:
        pass

    return nc


def emit_branch(nc, ctx, cfg, xnf, attO, ps, latf, aux, wcv):
    negB, rr, rrb, Rsb = aux["negB"], aux["rr"], aux["rrb"], aux["Rsb"]
    onesC, uA, uB, ones128 = aux["onesC"], aux["uA"], aux["uB"], aux["ones128"]
    qkm, bvt = aux["qkm"], aux["bvt"]
    rowsA, rowsB, id1 = aux["rowsA"], aux["rowsB"], aux["id1"]
    bias_fm = cfg["bias_fm"]
    attn = cfg["kind"] == "attn"
    vgd = 128 if attn else 256
    n_vg = N_PAIRS if attn else N_GROUPS
    ppg = 1 if attn else 2

    with (
        nc.sbuf_tensor("Kp_%s" % cfg["kind"], [128, T_TOT], BF16) as Kp,
        nc.sbuf_tensor("Vg_%s" % cfg["kind"], [128, NKC * vgd], BF16) as Vg,
        nc.sbuf_tensor("Qp_%s" % cfg["kind"], [128, T_OWN], BF16) as Qp,
        nc.sbuf_tensor("pb_%s" % cfg["kind"], [128, 3 * 1024], BF16) as pbuf,
        contextlib.ExitStack() as bst,
    ):
        sqscr = pbuf
        Vwp = None
        if not attn:
            Vwp = bst.enter_context(nc.sbuf_tensor("Vwp", [128, NCH * vgd], BF16))
        S = Sched(ctx)
        if attn:
            pslot = {0: [ctx.tok("vc"), ctx.qtoks("sd"), ctx.qtoks("gd")],
                     1: [ctx.tok("vc"), ctx.qtoks("sd"), ctx.qtoks("gd")]}
            for lc in range(LCH):
                for nt8 in range(8):
                    sl = (lc * 8 + nt8) % 2
                    col = sl * 512
                    t_mm = mm_tile(S, ps[:, col:col + 512],
                                   [(cfg["Wl"][:, ci * LAT + lc * 128:ci * LAT + lc * 128 + 128],
                                     xnf[:, ci * T_TOT + nt8 * 512:ci * T_TOT + nt8 * 512 + 512],
                                     None, None) for ci in range(NCH)],
                                   waits=[pslot[sl]])
                    pslot[sl] = [S.emit("vector", lambda v, lc=lc, nt8=nt8, col=col: v.tensor_scalar(
                        latf[:, lc * T_TOT + nt8 * 512:lc * T_TOT + nt8 * 512 + 512],
                        ps[:, col:col + 512], bias_fm[:, cfg["bl_col"] + lc:cfg["bl_col"] + lc + 1],
                        None, op0=ALU.add), waits=[t_mm], inc="vc")]

        prev_unit = {"pv_evac": None, "rr_done": None, "exp30": None,
                     "exp31": None, "pvlast": None}
        pending_ep = None
        for g in range(n_vg):
            bsrc = rowsB[0:1, cfg["bv_off"] + g * vgd:cfg["bv_off"] + (g + 1) * vgd]
            t_bmm = mm_tile(S, ps[:, 1024:1024 + vgd], [(onesC[:, :], bsrc, None, None)],
                            waits=[ctx.tok("vc"), ctx.qtoks("gd")], pad=xnf[:, 0:128])
            t_bvt = S.emit("vector", lambda v: v.tensor_copy(bvt[:, 0:vgd], ps[:, 1024:1024 + vgd]),
                           waits=[t_bmm], inc="vc")
            t_vw = None
            if not attn:
                t_vw = S.dma("sync", "qd", lambda e, g=g: e.dma_start(
                    out=Vwp[:, :].rearrange("p (c o) -> p c o", c=NCH),
                    in_=wcv.rearrange("p (c o) -> p c o", c=NCH)[:, :, g * vgd:(g + 1) * vgd]),
                    waits=[ctx.qtoks("qd"), ctx.tok("tn")])
            pslot = {0: [ctx.tok("vc"), ctx.tok("ac")], 1: [ctx.tok("vc"), ctx.tok("ac")]}
            for tt in range(NKC):
                sl = tt % 2
                col = sl * 512
                if attn:
                    pieces = [(latf[:, lc * T_TOT + tt * 128:lc * T_TOT + tt * 128 + 128],
                               cfg["Wvb"][:, cfg["Wv_off"] + lc * D + g * vgd:cfg["Wv_off"] + lc * D + (g + 1) * vgd],
                               None, None) for lc in range(LCH)]
                    w = [pslot[sl]]
                else:
                    pieces = [(xnf[:, ci * T_TOT + tt * 128:ci * T_TOT + tt * 128 + 128],
                               Vwp[:, ci * vgd:(ci + 1) * vgd],
                               None, None) for ci in range(NCH)]
                    w = [pslot[sl], t_vw]
                t_mm = mm_tile(S, ps[:, col:col + vgd], pieces, waits=w)
                pslot[sl] = [S.emit("vector", lambda v, tt=tt, col=col: v.tensor_add(
                    Vg[:, tt * vgd:(tt + 1) * vgd], ps[:, col:col + vgd], bvt[:, 0:vgd]),
                    waits=[t_mm, t_bvt], inc="vc")]

            for pc in range(ppg):
                pr = g * ppg + pc
                pslot = {0: [ctx.tok("vc"), ctx.tok("ac")], 1: [ctx.tok("vc"), ctx.tok("ac")]}
                for nt in range(NQT):
                    sl = nt % 2
                    col = sl * 512
                    base = 0 if attn else cfg["Wq_base"]
                    pieces = [(cfg["Wqb"][:, base + ci * D + pr * 128:base + ci * D + pr * 128 + 128],
                               xnf[:, ci * T_TOT + nt * 512:ci * T_TOT + nt * 512 + 512],
                               None, None) for ci in range(NCH)]
                    t_mm = mm_tile(S, ps[:, col:col + 512], pieces, waits=[pslot[sl]])
                    pslot[sl] = [S.emit("vector", lambda v, nt=nt, col=col, pr=pr: v.tensor_scalar(
                        Qp[:, nt * 512:nt * 512 + 512], ps[:, col:col + 512],
                        bias_fm[:, cfg["bq_col"] + pr:cfg["bq_col"] + pr + 1], None, op0=ALU.add),
                        waits=[t_mm], inc="vc")]
                for nt8 in range(8):
                    sl = nt8 % 2
                    col = sl * 512
                    if attn:
                        pieces = [(cfg["Wkb"][:, cfg["Wk_off"] + lc * D + pr * 128:cfg["Wk_off"] + lc * D + pr * 128 + 128],
                                   latf[:, lc * T_TOT + nt8 * 512:lc * T_TOT + nt8 * 512 + 512],
                                   None, None) for lc in range(LCH)]
                    else:
                        pieces = [(cfg["Wkb"][:, cfg["Wk_base"] + ci * D + pr * 128:cfg["Wk_base"] + ci * D + pr * 128 + 128],
                                   xnf[:, ci * T_TOT + nt8 * 512:ci * T_TOT + nt8 * 512 + 512],
                                   None, None) for ci in range(NCH)]
                    t_mm = mm_tile(S, ps[:, col:col + 512], pieces, waits=[pslot[sl]])
                    pslot[sl] = [S.emit("vector", lambda v, nt8=nt8, col=col, pr=pr: v.tensor_scalar(
                        Kp[:, nt8 * 512:nt8 * 512 + 512], ps[:, col:col + 512],
                        bias_fm[:, cfg["bk_col"] + pr:cfg["bk_col"] + pr + 1], None, op0=ALU.add),
                        waits=[t_mm], inc="vc")]

                t_qsq = S.emit("vector", lambda v: v.tensor_tensor(
                    sqscr[:, 0:2048], Qp[:, :], Qp[:, :], op=ALU.mult),
                    waits=[ctx.tok("vc"), ctx.tok("ac")], inc="vc")
                bcol = (1024, 1536)
                red_tok = [None, None]
                last = t_qsq
                seq = [("q", 0, nt) for nt in range(NQT)]
                seq += [("k", s2, q4) for s2 in range(2) for q4 in range(4)]
                ksq_tok = {}
                for j, (kind, s2, q4) in enumerate(seq):
                    par = j % 2
                    if kind == "k" and q4 == 0:
                        last = S.emit("vector", lambda v, s2=s2: (v.drain(), v.tensor_tensor(
                            sqscr[:, 0:2048], Kp[:, s2 * 2048:(s2 + 1) * 2048],
                            Kp[:, s2 * 2048:(s2 + 1) * 2048], op=ALU.mult))[1],
                            waits=[ctx.tok("tn")], inc="vc")
                    w = [last, red_tok[par]]
                    t_mm = mm_tile(S, ps[0:1, bcol[par]:bcol[par] + 512],
                                   [(ones128[:, :], sqscr[:, q4 * 512:(q4 + 1) * 512], None, None)],
                                   waits=w, pad=xnf[:, 0:128])
                    qi = q4 if kind == "q" else 8 + s2 * 4 + q4
                    red_tok[par] = S.emit("vector", lambda v, qi=qi, par=par: v.reduce_max(
                        qkm[0:1, qi:qi + 1], ps[0:1, bcol[par]:bcol[par] + 512], axis=AX.X),
                        waits=[t_mm], inc="vc")
                last = red_tok[0] if red_tok[0] and (red_tok[1] is None or red_tok[0][1] > red_tok[1][1]) else red_tok[1]
                last = S.emit("vector", lambda v: v.drain(), waits=[red_tok[0], red_tok[1]], inc="vc")

                def mk_negb(v):
                    v.drain()
                    v.reduce_max(qkm[0:1, 16:17], qkm[0:1, 8:16], axis=AX.X)
                    v.drain()
                    v.tensor_scalar(qkm[0:1, 20:24], qkm[0:1, 0:4], qkm[0:1, 16:17],
                                    -0.5 * SCALE, op0=ALU.add, op1=ALU.mult)
                    v.drain()
                    return v.tensor_copy(rowsB[0:1, RB_NB:RB_NB + 4], qkm[0:1, 20:24])
                t_nb = S.emit("vector", mk_negb, waits=[last, ctx.tok("ac")], inc="vc")
                t_nbm = mm_tile(S, ps[:, 1536:1540],
                                [(onesC[:, :], rowsB[0:1, RB_NB:RB_NB + 4], None, None)],
                                waits=[t_nb], pad=xnf[:, 0:128])
                t_nbf = S.emit("vector", lambda v: v.tensor_copy(negB[:, :], ps[:, 1536:1540]),
                               waits=[t_nbm], inc="vc")

                voff = pc * 128
                for nt in range(NQT):
                    exp_toks = []
                    pv_toks = []
                    qk_toks = []

                    def emit_qk(i, nt=nt):
                        j = i % 2
                        sc = PS_SLOT[j]
                        if i < 2:
                            w = [ctx.tok("vc"),
                                 prev_unit.get("exp30") if i == 0 else prev_unit.get("exp31")]
                        else:
                            w = [exp_toks[i - 2]]
                        S.emit("tensor", lambda t, i=i, sc=sc, nt=nt: t.matmul(
                            ps[:, sc:sc + 512], Kp[0:64, i * 128:(i + 1) * 128],
                            Qp[0:64, nt * 512:(nt + 1) * 512], start=True, stop=True,
                            tile_position=(0, 0)), waits=w)
                        qk_toks.append(S.emit("tensor", lambda t, i=i, sc=sc, nt=nt: t.matmul(
                            ps[:, sc + 512:sc + 1024], Kp[64:128, i * 128:(i + 1) * 128],
                            Qp[64:128, nt * 512:(nt + 1) * 512], start=True, stop=True,
                            tile_position=(64, 0)), inc="tn"))

                    for i in range(2):
                        emit_qk(i)
                    if pending_ep is not None:
                        pending_ep()
                        pending_ep = None
                    for i in range(NKC):
                        sc = PS_SLOT[i % 2]
                        pb = (i % 3) * 1024
                        wexp = [qk_toks[i]] + ([t_nbf] if i == 0 else [])
                        if i >= 3:
                            wexp.append(pv_toks[i - 3])
                        elif prev_unit.get("pvlast") is not None:
                            wexp.append(prev_unit["pvlast"])
                        exp_toks.append(S.emit("scalar", lambda a, sc=sc, pb=pb, nt=nt: a.activation(
                            pbuf[:, pb:pb + 1024], ps[:, sc:sc + 1024], AF.Exp,
                            bias=negB[:, nt:nt + 1], scale=SCALE), waits=wexp, inc="ac"))
                        if i + 2 < NKC:
                            emit_qk(i + 2)
                        wpv = [exp_toks[i]]
                        if i == 0:
                            wpv.append(prev_unit["pv_evac"])
                        S.emit("tensor", lambda t, i=i, pb=pb, voff=voff: t.matmul(
                            ps[0:64, PS_PVA:PS_PVA + 512],
                            Vg[:, i * vgd + voff:i * vgd + voff + 64],
                            pbuf[:, pb:pb + 512], start=(i == 0), stop=(i == NKC - 1),
                            tile_position=(0, 0)), waits=wpv)
                        S.emit("tensor", lambda t, i=i, pb=pb, voff=voff: t.matmul(
                            ps[64:128, PS_PVB:PS_PVB + 512],
                            Vg[:, i * vgd + voff + 64:i * vgd + voff + 128],
                            pbuf[:, pb + 512:pb + 1024], start=(i == 0), stop=(i == NKC - 1),
                            tile_position=(0, 64)))
                        wrho = [prev_unit["rr_done"], prev_unit["pv_evac"]] if i == 0 else ()
                        S.emit("tensor", lambda t, i=i, pb=pb: t.matmul(
                            ps[0:1, PS_RHA:PS_RHA + 512], ones128[:, :],
                            pbuf[:, pb:pb + 512], start=(i == 0), stop=(i == NKC - 1),
                            tile_position=(0, 0)), waits=wrho)
                        pv_toks.append(S.emit("tensor", lambda t, i=i, pb=pb: t.matmul(
                            ps[32:33, PS_RHB:PS_RHB + 512], ones128[:, :],
                            pbuf[:, pb + 512:pb + 1024], start=(i == 0), stop=(i == NKC - 1),
                            tile_position=(0, 32)), inc="tn"))

                    def make_ep(pv_toks=pv_toks, pr=pr, nt=nt):
                        def ep():
                            def rho_fin(v):
                                v.tensor_copy(rr[0:1, 0:512], ps[0:1, PS_RHA:PS_RHA + 512])
                                v.tensor_copy(rr[0:1, 512:1024], ps[32:33, PS_RHB:PS_RHB + 512])
                                v.drain()
                                v.reciprocal(rr[0:1, 0:1024], rr[0:1, 0:1024])
                                v.drain()
                                return v.tensor_copy(rrb[0:1, 0:1024], rr[0:1, 0:1024])
                            t_rr = S.emit("vector", rho_fin, waits=[pv_toks[NKC - 1]], inc="vc")
                            prev_unit["rr_done"] = t_rr
                            t_R = mm_tile(S, ps[:, PS_RHA:PS_RHA + 512],
                                          [(uA[:, :], rrb[0:1, 0:512], None, True),
                                           (uB[:, :], rrb[0:1, 512:1024], None, False)],
                                          waits=[t_rr], pad=xnf[:, 0:128])

                            def pvn(v, pr=pr, nt=nt):
                                v.tensor_copy(Rsb[:, :], ps[:, PS_RHA:PS_RHA + 512])
                                v.drain()
                                v.tensor_tensor(
                                    attO[0:64, pr * T_OWN + nt * 512:pr * T_OWN + nt * 512 + 512],
                                    ps[0:64, PS_PVA:PS_PVA + 512], Rsb[0:64, :], op=ALU.mult)
                                return v.tensor_tensor(
                                    attO[64:128, pr * T_OWN + nt * 512:pr * T_OWN + nt * 512 + 512],
                                    ps[64:128, PS_PVB:PS_PVB + 512], Rsb[64:128, :], op=ALU.mult)
                            t_pvn = S.emit("vector", pvn, waits=[t_R], inc="vc")
                            prev_unit["pv_evac"] = t_pvn
                        return ep
                    prev_unit["exp30"] = exp_toks[NKC - 2]
                    prev_unit["exp31"] = exp_toks[NKC - 1]
                    prev_unit["pvlast"] = pv_toks[NKC - 1]
                    pending_ep = make_ep()
        if pending_ep is not None:
            pending_ep()
            pending_ep = None
        S.fence()
        S.realize(nc)


_NC_CACHE = {}


def kernel(**inputs):
    if "prog" not in _NC_CACHE:
        _NC_CACHE["prog"] = build_nc()
    nc = _NC_CACHE["prog"]

    X = np.ascontiguousarray(inputs["L0_attn_x"], dtype=np.float32)
    wm = dict(
        aWq=inputs["L0_attn_Wq"], abq=inputs["L0_attn_bq"], aWl=inputs["L0_attn_Wl"],
        abl=inputs["L0_attn_bl"], aWk=inputs["L0_attn_Wk"], abk=inputs["L0_attn_bk"],
        aWv=inputs["L0_attn_Wv"], abv=inputs["L0_attn_bv"], aWo=inputs["L0_attn_Wo"],
        abo=inputs["L0_attn_bo"], fWq=inputs["L0_ffn_Wq"], fbq=inputs["L0_ffn_bq"],
        fWk=inputs["L0_ffn_Wk"], fbk=inputs["L0_ffn_bk"], fWv=inputs["L0_ffn_Wv"],
        fbv=inputs["L0_ffn_bv"], fWg1=inputs["L0_ffn_Wg1"], fbg1=inputs["L0_ffn_bg1"],
        fWg2=inputs["L0_ffn_Wg2"], fbg2=inputs["L0_ffn_bg2"], fWo=inputs["L0_ffn_Wo"],
        fbo=inputs["L0_ffn_bo"],
    )
    wm = {k: np.ascontiguousarray(v, dtype=np.float32) for k, v in wm.items()}
    wm["cident"] = np.eye(128, dtype=np.float32)
    in_maps = []
    for c in range(8):
        b, h = c // 2, c % 2
        m = dict(wm)
        m["x_own"] = np.ascontiguousarray(X[b, h * T_OWN:(h + 1) * T_OWN])
        m["x_oth"] = np.ascontiguousarray(X[b, (1 - h) * T_OWN:(2 - h) * T_OWN])
        in_maps.append(m)

    import time
    out = np.empty((4, 4096, 1024), dtype=np.float32)

    def fill(first_core, results):
        for i, r in enumerate(results):
            c = first_core + i
            b, h = c // 2, c % 2
            out[b, h * T_OWN:(h + 1) * T_OWN] = r["y"]

    if TRY_FUSED:
        try:
            res = run_bass_kernel_spmd(nc, in_maps, core_ids=list(range(8)))
            fill(0, res.results)
            return out
        except Exception:
            time.sleep(10)
    for g in range(4):
        cores = [2 * g, 2 * g + 1]
        last = None
        for attempt in range(3):
            try:
                res = run_bass_kernel_spmd(nc, [in_maps[c] for c in cores], core_ids=cores)
                fill(2 * g, res.results)
                last = None
                break
            except Exception as e:
                last = e
                time.sleep(10)
        if last is not None:
            raise last
    return out


# version 17 (final)
# speedup vs baseline: 1.0468x; 1.0123x over previous
DEBUG = False
STOP_AFTER = 99
CAST_OFF = False
TRY_FUSED = True
import contextlib
import numpy as np
import concourse.bass as bass
import concourse.mybir as mybir
from concourse.bass_utils import run_bass_kernel_spmd

F32 = mybir.dt.float32
BF16 = mybir.dt.bfloat16
AF = mybir.ActivationFunctionType
ALU = mybir.AluOpType
AX = mybir.AxisListType

D = 1024
NCH = 8
LAT = 256
LCH = 2
T_OWN = 2048
T_TOT = 4096
N_PAIRS = 8
N_GROUPS = 4
GRP_D = 256
SCALE = 0.125
NQT = 4
NKC = 32

PS_SLOT = [0, 1024]
PS_PVA = 2048
PS_PVB = 2560
PS_RHA = 3072
PS_RHB = 3584

RB_BV = 3072
RB_ST = 4096
RB_W = 5120
RB_NB = 5184


class _StopEmit(Exception):
    pass


class Sched:
    def __init__(self, ctx):
        self.ctx = ctx
        self.items = {k: [] for k in ("sync", "gpsimd", "tensor", "vector", "scalar")}

    def emit(self, eng, fn, waits=(), inc=None, amt=1):
        tok = None
        flat = []

        def _add(w):
            if w is None:
                return
            if isinstance(w, (list, tuple)) and (len(w) != 2 or not isinstance(w[0], str)):
                for x in w:
                    _add(x)
            else:
                flat.append(w)
        for w in waits:
            _add(w)
        if inc is not None:
            self.ctx.n[inc] = self.ctx.n.get(inc, 0) + amt
            tok = (inc, self.ctx.n[inc])
        self.items[eng].append((fn, tuple(flat), inc, amt))
        return tok

    def fence(self):
        ctx = self.ctx
        w = []
        for q in ("qd", "sd", "gd"):
            w.extend(ctx.qtoks(q))
        self.emit("sync", lambda e: e.nop(), waits=w)

    def dma(self, eng, qname, fn, waits=()):
        ctx = self.ctx
        n = 1 if qname == "sd" else NCHAIN
        i = ctx.qrr[qname] = (ctx.qrr.get(qname, -1) + 1) % n
        sem = "%s%d" % (qname, i)
        prev = ctx.tok(sem)
        return self.emit(eng, fn, waits=list(waits) + [prev], inc=sem, amt=16)

    def realize(self, nc):
        ctx = self.ctx
        with nc.Block() as block:
            for eng_name in ("sync", "gpsimd", "tensor", "vector", "scalar"):
                items = self.items[eng_name]
                if not items:
                    continue

                def runner(e, items=items):
                    done = {}
                    for fn, waits, inc, amt in items:
                        best = {}
                        for (s, v) in waits:
                            if v > best.get(s, 0):
                                best[s] = v
                        for s, v in best.items():
                            if v > done.get(s, 0):
                                e.wait_ge(ctx.sems[s], v)
                                done[s] = v
                        r = fn(e)
                        if inc is not None:
                            r.then_inc(ctx.sems[inc], amt)

                getattr(block, eng_name)(runner)


NCHAIN = 4


class Ctx:
    def __init__(self, nc, sems):
        self.nc = nc
        self.sems = sems
        self.n = {}
        self.qrr = {}

    def tok(self, name):
        return (name, self.n.get(name, 0))

    def qtoks(self, qname):
        return [self.tok("%s%d" % (qname, i)) for i in range(NCHAIN)]


def mm_tile(S, out_ap, pieces, waits=(), inc="tn", pad=None):
    n = len(pieces)
    tok = None
    for j, (lhsT, rhs, tp, st) in enumerate(pieces):
        first, last = j == 0, j == n - 1

        def fn(t, lhsT=lhsT, rhs=rhs, tp=tp, st=st, first=first, last=last):
            return t.matmul(out_ap, lhsT, rhs, start=st if st is not None else first,
                            stop=last, tile_position=tp)
        tok = S.emit("tensor", fn, waits=waits if j == 0 else (),
                     inc=None if (pad is not None and last) else (inc if last else None))
    if pad is not None:
        tok = S.emit("tensor", lambda t: t.ldweights(pad), inc=inc)
    return tok


PAD = [None]


def row_to_fm(S, ctx, ps, id1f, src_ap, n, dst, dst_col, src_tok):
    prev = src_tok
    for c in range(n // 128):
        pcol = 1552 + 2 * (c % 2)
        S.emit("tensor", lambda t, c=c, pcol=pcol: t.transpose(
            ps[:, pcol:pcol + 1], src_ap[0:1, c * 128:(c + 1) * 128],
            id1f[:, :]), waits=[prev])
        t_t = S.emit("tensor", lambda t: t.ldweights(PAD[0]), inc="tn")
        prev = S.emit("vector", lambda v, c=c, pcol=pcol: v.tensor_copy(
            dst[:, dst_col + c:dst_col + c + 1], ps[:, pcol:pcol + 1]),
            waits=[t_t], inc="vc")
    return prev


def build_nc():
    nc = bass.Bass("TRN2", target_bir_lowering=False)

    x_own = nc.dram_tensor("x_own", [T_OWN, D], F32, kind="ExternalInput").ap()
    x_oth = nc.dram_tensor("x_oth", [T_OWN, D], F32, kind="ExternalInput").ap()
    aWq = nc.dram_tensor("aWq", [D, D], F32, kind="ExternalInput").ap()
    abq = nc.dram_tensor("abq", [D], F32, kind="ExternalInput").ap()
    aWl = nc.dram_tensor("aWl", [D, LAT], F32, kind="ExternalInput").ap()
    abl = nc.dram_tensor("abl", [LAT], F32, kind="ExternalInput").ap()
    aWk = nc.dram_tensor("aWk", [LAT, D], F32, kind="ExternalInput").ap()
    abk = nc.dram_tensor("abk", [D], F32, kind="ExternalInput").ap()
    aWv = nc.dram_tensor("aWv", [LAT, D], F32, kind="ExternalInput").ap()
    abv = nc.dram_tensor("abv", [D], F32, kind="ExternalInput").ap()
    aWo = nc.dram_tensor("aWo", [D, D], F32, kind="ExternalInput").ap()
    abo = nc.dram_tensor("abo", [D], F32, kind="ExternalInput").ap()
    fWq = nc.dram_tensor("fWq", [4, D, D], F32, kind="ExternalInput").ap()
    fbq = nc.dram_tensor("fbq", [4, D], F32, kind="ExternalInput").ap()
    fWk = nc.dram_tensor("fWk", [4, D, D], F32, kind="ExternalInput").ap()
    fbk = nc.dram_tensor("fbk", [4, D], F32, kind="ExternalInput").ap()
    fWv = nc.dram_tensor("fWv", [4, D, D], F32, kind="ExternalInput").ap()
    fbv = nc.dram_tensor("fbv", [4, D], F32, kind="ExternalInput").ap()
    fWg1 = nc.dram_tensor("fWg1", [D, 128], F32, kind="ExternalInput").ap()
    fbg1 = nc.dram_tensor("fbg1", [128], F32, kind="ExternalInput").ap()
    fWg2 = nc.dram_tensor("fWg2", [128, 4], F32, kind="ExternalInput").ap()
    fbg2 = nc.dram_tensor("fbg2", [4], F32, kind="ExternalInput").ap()
    fWo = nc.dram_tensor("fWo", [D, D], F32, kind="ExternalInput").ap()
    fbo = nc.dram_tensor("fbo", [D], F32, kind="ExternalInput").ap()

    cident = nc.dram_tensor("cident", [128, 128], F32, kind="ExternalInput").ap()
    y = nc.dram_tensor("y", [T_OWN, D], F32, kind="ExternalOutput").ap()
    if DEBUG:
        dbg_xnf = nc.dram_tensor("dbg_xnf", [128, 4096], BF16, kind="ExternalOutput").ap()
        dbg_attO = nc.dram_tensor("dbg_attO", [128, 4096], BF16, kind="ExternalOutput").ap()
        dbg_lg = nc.dram_tensor("dbg_lg", [1, 32], F32, kind="ExternalOutput").ap()
        dbg_qkm = nc.dram_tensor("dbg_qkm", [1, 32], F32, kind="ExternalOutput").ap()
        dbg_rr = nc.dram_tensor("dbg_rr", [1, 1024], F32, kind="ExternalOutput").ap()
        dbg_negB = nc.dram_tensor("dbg_negB", [128, 4], F32, kind="ExternalOutput").ap()
    spill = nc.dram_tensor("spill", [128, NCH * T_OWN], BF16, kind="ExternalOutput" if DEBUG else "Internal").ap()
    wcv = nc.dram_tensor("wcv", [128, NCH * D], BF16, kind="Internal").ap()

    est = contextlib.ExitStack()
    try:
      with est:
        sems = {}
        for s in ("tn", "vc", "ac"):
            sems[s] = est.enter_context(nc.semaphore(s))
        for q in ("qd", "sd", "gd"):
            for i in range(4):
                n = "%s%d" % (q, i)
                sems[n] = est.enter_context(nc.semaphore(n))
        ctx = Ctx(nc, sems)

        xnf = est.enter_context(nc.sbuf_tensor("xnf", [128, NCH * T_TOT], BF16))
        PAD[0] = xnf[:, 0:128]
        attO = est.enter_context(nc.sbuf_tensor("attO", [128, N_PAIRS * T_OWN], BF16))
        ps = est.enter_context(nc.psum_tensor("ps", [128, 4096], F32))
        bA = est.enter_context(nc.sbuf_tensor("bA", [128, 32], F32))
        bF = est.enter_context(nc.sbuf_tensor("bF", [128, 32], F32))
        rowsA = est.enter_context(nc.sbuf_tensor("rowsA", [1, 1024], F32))
        rowsB = est.enter_context(nc.sbuf_tensor("rowsB", [1, 5376], BF16))
        onesC = est.enter_context(nc.sbuf_tensor("onesC", [1, 128], BF16))
        uA = est.enter_context(nc.sbuf_tensor("uA", [1, 128], BF16))
        uB = est.enter_context(nc.sbuf_tensor("uB", [1, 128], BF16))
        ones128 = est.enter_context(nc.sbuf_tensor("ones128", [128, 1], BF16))
        id1 = est.enter_context(nc.sbuf_tensor("id1", [1, 1], BF16))
        id1f = est.enter_context(nc.sbuf_tensor("id1f", [1, 1], F32))
        id128 = est.enter_context(nc.sbuf_tensor("id128", [128, 128], F32))
        negB = est.enter_context(nc.sbuf_tensor("negB", [128, 4], F32))
        wbc = est.enter_context(nc.sbuf_tensor("wbc", [128, 4], F32))
        wT4 = est.enter_context(nc.sbuf_tensor("wT4", [4, 1], BF16))
        lg = est.enter_context(nc.sbuf_tensor("lg", [1, 32], F32))
        qkm = est.enter_context(nc.sbuf_tensor("qkm", [1, 32], F32))
        rr = est.enter_context(nc.sbuf_tensor("rr", [1, 1024], F32))
        rrb = est.enter_context(nc.sbuf_tensor("rrb", [1, 1024], BF16))
        Rsb = est.enter_context(nc.sbuf_tensor("Rsb", [128, 512], F32))
        pooled = est.enter_context(nc.sbuf_tensor("pooled", [128, 8], F32))
        pooled_b = est.enter_context(nc.sbuf_tensor("pooled_b", [128, 8], BF16))
        hgate = est.enter_context(nc.sbuf_tensor("hgate", [128, 1], BF16))
        WG2 = est.enter_context(nc.sbuf_tensor("WG2", [128, 4], BF16))
        bvt = est.enter_context(nc.sbuf_tensor("bvt", [128, GRP_D], F32))

        S0 = Sched(ctx)
        S0.dma("sync", "qd", lambda e: e.dma_start(out=id128[:, :], in_=cident[:, :]))
        S0.emit("gpsimd", lambda g: g.memset(onesC[:, :], 1.0))
        S0.emit("gpsimd", lambda g: g.memset(uA[0:1, 0:64], 1.0))
        S0.emit("gpsimd", lambda g: g.memset(uA[0:1, 64:128], 0.0))
        S0.emit("gpsimd", lambda g: g.memset(uB[0:1, 0:64], 0.0))
        S0.emit("gpsimd", lambda g: g.memset(uB[0:1, 64:128], 1.0))
        S0.emit("gpsimd", lambda g: g.memset(ones128[:, :], 1.0))
        S0.emit("gpsimd", lambda g: g.memset(id1[:, :], 1.0))
        S0.emit("gpsimd", lambda g: g.memset(id1f[:, :], 1.0))
        S0.emit("gpsimd", lambda g: g.memset(lg[:, :], 0.0))
        S0.emit("gpsimd", lambda g: g.memset(qkm[:, :], 0.0))
        S0.fence()
        S0.realize(nc)

        aux = dict(negB=negB, rr=rr, rrb=rrb, Rsb=Rsb, onesC=onesC, uA=uA, uB=uB,
                   ones128=ones128, qkm=qkm, bvt=bvt, rowsA=rowsA, rowsB=rowsB,
                   id1=id1, id1f=id1f)

        with (
            nc.sbuf_tensor("Wq_b", [128, NCH * D], BF16) as Wq_b,
            nc.sbuf_tensor("Wl_b", [128, NCH * LAT], BF16) as Wl_b,
            nc.sbuf_tensor("Wkv_b", [128, 2 * LCH * D], BF16) as Wkv_b,
            nc.sbuf_tensor("Wo_b", [128, NCH * D], BF16) as Wo_b,
            nc.sbuf_tensor("latf", [128, LCH * T_TOT], BF16) as latf,
        ):
            with (
                nc.sbuf_tensor("xt", [128, 2 * D], F32) as xt,
                nc.sbuf_tensor("xsq", [128, D], F32) as xsq,
                nc.sbuf_tensor("xnt", [128, 2 * D], F32) as xnt,
                nc.sbuf_tensor("st8", [128, 16], F32) as st8,
                nc.sbuf_tensor("WG1", [128, NCH * 128], BF16) as WG1,
            ):
                S = Sched(ctx)

                def wload(dst_ap, src_ap):
                    if CAST_OFF:
                        return S.dma("gpsimd", "gd", lambda g: g.dma_start(
                            out=xt[:, 0:D], in_=aWq[0:128, :]))
                    return S.dma("gpsimd", "gd", lambda g, d=dst_ap, s=src_ap: g.dma_start(
                        out=d, in_=s))
                wload(Wq_b[:, :].rearrange("p (c o) -> p c o", o=D), aWq.rearrange("(c p) o -> p c o", p=128))
                wload(Wl_b[:, :].rearrange("p (c o) -> p c o", o=LAT), aWl.rearrange("(c p) o -> p c o", p=128))
                wload(Wkv_b[:, 0:LCH * D].rearrange("p (c o) -> p c o", o=D), aWk.rearrange("(c p) o -> p c o", p=128))
                wload(Wkv_b[:, LCH * D:2 * LCH * D].rearrange("p (c o) -> p c o", o=D), aWv.rearrange("(c p) o -> p c o", p=128))
                wload(Wo_b[:, :].rearrange("p (c o) -> p c o", o=D), aWo.rearrange("(c p) o -> p c o", p=128))
                wload(WG1[:, :].rearrange("p (c o) -> p c o", o=128), fWg1.rearrange("(c p) o -> p c o", p=128))
                wload(WG2[:, :], fWg2[:, :])

                t_bg1 = S.dma("sync", "qd", lambda e: e.dma_start(
                    out=bA[:, 26:27], in_=fbg1.rearrange("(p c) -> p c", c=1)))
                t_bg2 = S.dma("sync", "qd", lambda e: e.dma_start(
                    out=lg[0:1, 28:32], in_=fbg2[None, :]))
                wload(rowsB[0:1, RB_BV:RB_BV + D], abv[None, :])

                xt_free = [None, None]
                xnt_free = [None, None]
                tr_slot = [None, None, None, None]
                for it in range(32):
                    par = it % 2
                    src = x_own if it < 16 else x_oth
                    row0 = (it % 16) * 128
                    t_ld = S.dma("sync", "qd", lambda e, s=src, r=row0, par=par: e.dma_start(
                        out=xt[:, par * D:(par + 1) * D], in_=s[r:r + 128, :]),
                        waits=[xt_free[par]])
                    c0 = par * 8

                    def stats(v, par=par, c0=c0):
                        v.reduce_sum(st8[:, c0:c0 + 1], xt[:, par * D:(par + 1) * D], axis=AX.X)
                        v.tensor_tensor(xsq[:, :], xt[:, par * D:(par + 1) * D],
                                        xt[:, par * D:(par + 1) * D], op=ALU.mult)
                        v.drain()
                        v.reduce_sum(st8[:, c0 + 1:c0 + 2], xsq[:, :], axis=AX.X)
                        v.drain()
                        v.tensor_scalar_mul(st8[:, c0:c0 + 1], st8[:, c0:c0 + 1], 1.0 / D)
                        v.tensor_scalar_mul(st8[:, c0 + 1:c0 + 2], st8[:, c0 + 1:c0 + 2], 1.0 / D)
                        v.drain()
                        v.tensor_tensor(st8[:, c0 + 2:c0 + 3], st8[:, c0:c0 + 1],
                                        st8[:, c0:c0 + 1], op=ALU.mult)
                        v.drain()
                        v.tensor_sub(st8[:, c0 + 1:c0 + 2], st8[:, c0 + 1:c0 + 2], st8[:, c0 + 2:c0 + 3])
                        v.drain()
                        return v.tensor_scalar_add(st8[:, c0 + 1:c0 + 2], st8[:, c0 + 1:c0 + 2], 1e-5)
                    t_st = S.emit("vector", stats, waits=[t_ld], inc="vc")
                    t_sd = S.emit("scalar", lambda a, c0=c0: a.activation(
                        st8[:, c0 + 2:c0 + 3], st8[:, c0 + 1:c0 + 2], AF.Sqrt),
                        waits=[t_st], inc="ac")

                    def fin(v, c0=c0):
                        v.reciprocal(st8[:, c0 + 3:c0 + 4], st8[:, c0 + 2:c0 + 3])
                        v.drain()
                        v.tensor_tensor(st8[:, c0 + 4:c0 + 5], st8[:, c0:c0 + 1],
                                        st8[:, c0 + 3:c0 + 4], op=ALU.mult)
                        v.drain()
                        return v.tensor_scalar_mul(st8[:, c0 + 4:c0 + 5], st8[:, c0 + 4:c0 + 5], -1.0)
                    t_fin = S.emit("vector", fin, waits=[t_sd], inc="vc")
                    t_xn = S.emit("scalar", lambda a, par=par, c0=c0: a.activation(
                        xnt[:, par * D:(par + 1) * D], xt[:, par * D:(par + 1) * D], AF.Identity,
                        bias=st8[:, c0 + 4:c0 + 5], scale=st8[:, c0 + 3:c0 + 4]),
                        waits=[t_fin, xnt_free[par]], inc="ac")
                    xt_free[par] = t_xn
                    last_ev = None
                    for c in range(NCH):
                        bk = c % 4
                        S.emit("tensor", lambda t, par=par, c=c, bk=bk: t.transpose(
                            ps[:, bk * 512:bk * 512 + 128],
                            xnt[:, par * D + c * 128:par * D + (c + 1) * 128], id128[:, :]),
                            waits=[t_xn, tr_slot[bk]])
                        t_t = S.emit("tensor", lambda t: t.ldweights(xnf[:, 0:128]), inc="tn")
                        last_ev = S.emit("vector", lambda v, it=it, c=c, bk=bk: v.tensor_copy(
                            xnf[:, c * T_TOT + it * 128:c * T_TOT + (it + 1) * 128],
                            ps[:, bk * 512:bk * 512 + 128]), waits=[t_t], inc="vc")
                        tr_slot[bk] = last_ev
                    xnt_free[par] = last_ev
                t_allxnf = ctx.tok("vc")

                def do_pooled(v):
                    for ci in range(NCH):
                        v.reduce_sum(pooled[:, ci:ci + 1], xnf[:, ci * T_TOT:(ci + 1) * T_TOT], axis=AX.X)
                    v.drain()
                    return v.tensor_scalar_mul(pooled_b[:, :], pooled[:, :], 1.0 / T_TOT)
                t_pool = S.emit("vector", do_pooled, waits=[t_allxnf], inc="vc")
                t_h = mm_tile(S, ps[:, 1536:1537],
                              [(WG1[:, ci * 128:(ci + 1) * 128], pooled_b[:, ci:ci + 1], None, None)
                               for ci in range(NCH)], waits=[t_pool, ctx.qtoks("gd")], pad=xnf[:, 0:128])
                t_hr = S.emit("scalar", lambda a: a.activation(
                    hgate[:, :], ps[:, 1536:1537], AF.Relu, bias=bA[:, 26:27], scale=1.0),
                    waits=[t_h, t_bg1], inc="ac")
                t_lgmm = mm_tile(S, ps[0:1, 1540:1544], [(hgate[:, :], WG2[:, :], None, None)],
                                 waits=[t_hr], pad=xnf[:, 0:128])

                def gate(v):
                    v.tensor_add(lg[0:1, 0:4], lg[0:1, 28:32], ps[0:1, 1540:1544])
                    v.drain()
                    v.reduce_max(lg[0:1, 4:5], lg[0:1, 0:4], axis=AX.X)
                    v.drain()
                    v.tensor_scalar_mul(lg[0:1, 5:6], lg[0:1, 4:5], -1.0)
                    v.tensor_single_scalar(lg[0:1, 8:12], lg[0:1, 0:4], lg[0:1, 4:5], op=ALU.is_ge)
                    v.drain()
                    v.scalar_tensor_tensor(lg[0:1, 12:16], lg[0:1, 8:12], -1e30, lg[0:1, 0:4],
                                           op0=ALU.mult, op1=ALU.add)
                    v.drain()
                    return v.reduce_max(lg[0:1, 16:17], lg[0:1, 12:16], axis=AX.X)
                t_gate = S.emit("vector", gate, waits=[t_lgmm, t_bg2], inc="vc")
                t_e = S.emit("scalar", lambda a: a.activation(
                    lg[0:1, 20:24], lg[0:1, 0:4], AF.Exp, bias=lg[0:1, 5:6], scale=1.0),
                    waits=[t_gate], inc="ac")

                def gate2(v):
                    v.tensor_single_scalar(lg[0:1, 8:12], lg[0:1, 0:4], lg[0:1, 16:17], op=ALU.is_ge)
                    v.drain()
                    v.tensor_mul(lg[0:1, 24:28], lg[0:1, 20:24], lg[0:1, 8:12])
                    v.drain()
                    v.reduce_sum(lg[0:1, 17:18], lg[0:1, 24:28], axis=AX.X)
                    v.drain()
                    v.reciprocal(lg[0:1, 18:19], lg[0:1, 17:18])
                    v.drain()
                    v.tensor_single_scalar(lg[0:1, 24:28], lg[0:1, 24:28], lg[0:1, 18:19], op=ALU.mult)
                    v.drain()
                    return v.tensor_copy(rowsB[0:1, RB_W:RB_W + 4], lg[0:1, 24:28])
                t_g2 = S.emit("vector", gate2, waits=[t_e], inc="vc")
                S.emit("tensor", lambda t: t.transpose(
                    ps[0:4, 1544:1545], lg[0:1, 24:28], id1f[:, :]),
                    waits=[t_g2])
                t_wt = S.emit("tensor", lambda t: t.ldweights(xnf[:, 0:128]), inc="tn")
                t_wbc_mm = mm_tile(S, ps[:, 1548:1552],
                                   [(onesC[:, :], rowsB[0:1, RB_W:RB_W + 4], None, None)],
                                   waits=[t_g2], pad=xnf[:, 0:128])

                def wfin(v):
                    v.tensor_copy(wT4[:, :], ps[0:4, 1544:1545])
                    return v.tensor_copy(wbc[:, :], ps[:, 1548:1552])
                t_wfin = S.emit("vector", wfin, waits=[t_wt, t_wbc_mm], inc="vc")

                prev_fm = None
                for src_t, n, dcol in ((abq, D, 0), (abk, D, 8), (abo, D, 16), (abl, LAT, 24)):
                    t_r = S.dma("sync", "qd", lambda e, s=src_t, n=n: e.dma_start(
                        out=rowsA[0:1, 0:n], in_=s[None, :]),
                        waits=[prev_fm])
                    prev_fm = row_to_fm(S, ctx, ps, id1f, rowsA, n, bA, dcol, t_r)
                S.fence()
                S.realize(nc)

            if STOP_AFTER >= 2:
                emit_branch(nc, ctx, dict(
                kind="attn", Wqb=Wq_b, Wkb=Wkv_b, Wk_off=0, Wvb=Wkv_b,
                Wv_off=LCH * D, bq_col=0, bk_col=8, bias_fm=bA,
                bv_off=RB_BV, Wl=Wl_b, bl_col=24,
                ), xnf, attO, ps, latf, aux, None)

            with nc.sbuf_tensor("stage", [128, T_OWN], BF16) as stage:
                if STOP_AFTER < 3:
                    raise _StopEmit
                S = Sched(ctx)
                pslot = {0: [ctx.tok("vc"), ctx.tok("ac")], 1: [ctx.tok("vc"), ctx.tok("ac")]}
                prev_st = None
                for oc in range(NCH):
                    ev_toks = []
                    for nt in range(NQT):
                        sl = nt % 2
                        col = sl * 512
                        t_mm = mm_tile(S, ps[:, col:col + 512],
                                       [(Wo_b[:, pr * D + oc * 128:pr * D + oc * 128 + 128],
                                         attO[:, pr * T_OWN + nt * 512:pr * T_OWN + nt * 512 + 512],
                                         None, None) for pr in range(N_PAIRS)],
                                       waits=[pslot[sl]])
                        ev_toks.append(S.emit("vector", lambda v, nt=nt, col=col, oc=oc: v.tensor_scalar(
                            stage[:, nt * 512:nt * 512 + 512], ps[:, col:col + 512],
                            bA[:, 16 + oc:16 + oc + 1], None, op0=ALU.add),
                            waits=[t_mm, prev_st], inc="vc"))
                        pslot[sl] = [ev_toks[-1]]
                    prev_st = S.dma("sync", "qd", lambda e, oc=oc: e.dma_start(
                        out=spill[:, oc * T_OWN:(oc + 1) * T_OWN], in_=stage[:, :]),
                        waits=[ev_toks[-1]])
                S.fence()
                S.realize(nc)

        if STOP_AFTER < 4:
            raise _StopEmit
        with nc.sbuf_tensor("Wo2_b", [128, NCH * D], BF16) as Wo2_b:
            with nc.sbuf_tensor("Wc_b", [128, 2 * NCH * D], BF16) as Wc_b:
                with (
                    nc.sbuf_tensor("acc", [128, 2048], F32) as acc,
                    nc.sbuf_tensor("stm", [128, 2 * 2048], F32) as stm,
                    nc.sbuf_tensor("vstg", [128, 2048], BF16) as vstg,
                    nc.sbuf_tensor("bE", [4, 3 * D], BF16) as bE,
                ):
                    S = Sched(ctx)
                    S.dma("gpsimd", "gd", lambda g: g.dma_start(
                        out=Wo2_b[:, :].rearrange("p (c o) -> p c o", o=D),
                        in_=fWo.rearrange("(c p) o -> p c o", p=128)))
                    S.dma("gpsimd", "gd", lambda g: g.dma_start(out=bE[:, 0:D], in_=fbq[:, :]))
                    S.dma("gpsimd", "gd", lambda g: g.dma_start(out=bE[:, D:2 * D], in_=fbk[:, :]))
                    S.dma("gpsimd", "gd", lambda g: g.dma_start(out=bE[:, 2 * D:3 * D], in_=fbv[:, :]))
                    slices = [(P, sl) for P in range(3) for sl in range(4)]
                    dma_toks = {}
                    cons_toks = {}
                    prev_wcv = [None]
                    def emit_dma(idx, e):
                        P, sl = slices[idx]
                        buf = (idx * 4 + e) % 2
                        Wsrc = (fWq, fWk, fWv)[P]
                        w = [cons_toks.get((idx * 4 + e) - 2)]
                        eng, q = ("sync", "qd") if (idx * 4 + e) % 2 == 0 else ("gpsimd", "gd")
                        dma_toks[idx * 4 + e] = S.dma(eng, q, lambda s, e=e, sl=sl, buf=buf, Wsrc=Wsrc: s.dma_start(
                            out=stm[:, buf * 2048:(buf + 1) * 2048].rearrange("p (c o) -> p c o", o=256),
                            in_=Wsrc[e][:, sl * 256:(sl + 1) * 256].rearrange("(c p) o -> p c o", p=128)),
                            waits=w)
                    emit_dma(0, 0)
                    emit_dma(0, 1)
                    for idx, (P, sl) in enumerate(slices):
                        for e in range(4):
                            buf = (idx * 4 + e) % 2
                            if e == 0:
                                fn = lambda v, buf=buf: (v.drain(), v.tensor_scalar(
                                    acc[:, :], stm[:, buf * 2048:(buf + 1) * 2048],
                                    wbc[0:128, 0:1], None, op0=ALU.mult))[1]
                            else:
                                fn = lambda v, e=e, buf=buf: (v.drain(), v.scalar_tensor_tensor(
                                    acc[:, :], stm[:, buf * 2048:(buf + 1) * 2048],
                                    wbc[0:128, e:e + 1], acc[:, :],
                                    op0=ALU.mult, op1=ALU.add))[1]
                            cons_toks[idx * 4 + e] = S.emit("vector", fn, waits=[dma_toks[idx * 4 + e]], inc="vc")
                            nxt = idx * 4 + e + 2
                            if nxt < len(slices) * 4:
                                emit_dma(nxt // 4, nxt % 4)
                        if P < 2:
                            S.emit("vector", lambda v, P=P, sl=sl: (v.drain(), v.tensor_copy(
                                Wc_b[:, :].rearrange("p (P c o) -> p P c o", P=2, c=NCH)[:, P, :, sl * 256:(sl + 1) * 256],
                                acc[:, :].rearrange("p (c o) -> p c o", c=NCH)))[1], inc="vc")
                        else:
                            t_cv = S.emit("vector", lambda v: (v.drain(), v.tensor_copy(
                                vstg[:, :], acc[:, :]))[1], waits=[prev_wcv[0]], inc="vc")
                            prev_wcv[0] = S.dma("sync", "qd", lambda e2, sl=sl: e2.dma_start(
                                out=wcv.rearrange("p (c o) -> p c o", c=NCH)[:, :, sl * 256:(sl + 1) * 256],
                                in_=vstg[:, :].rearrange("p (c o) -> p c o", o=256)),
                                waits=[t_cv])
                    for P in range(3):
                        t1 = mm_tile(S, ps[0:1, 1024:1536],
                                     [(wT4[:, :], bE[:, P * D:P * D + 512], None, None)],
                                     waits=[ctx.tok("vc"), ctx.qtoks("gd")])
                        t2 = mm_tile(S, ps[0:1, 1536:2048],
                                     [(wT4[:, :], bE[:, P * D + 512:(P + 1) * D], None, None)],
                                     pad=xnf[:, 0:128])

                        if P < 2:
                            def cprow(v, P=P):
                                v.tensor_copy(rowsA[0:1, 0:512], ps[0:1, 1024:1536])
                                return v.tensor_copy(rowsA[0:1, 512:1024], ps[0:1, 1536:2048])
                            t_cp = S.emit("vector", cprow, waits=[t1, t2, ctx.tok("tn")], inc="vc")
                            row_to_fm(S, ctx, ps, id1f, rowsA, D, bF, P * 8, t_cp)
                        else:
                            def cprow(v, P=P):
                                v.tensor_copy(rowsB[0:1, P * D:P * D + 512], ps[0:1, 1024:1536])
                                return v.tensor_copy(rowsB[0:1, P * D + 512:(P + 1) * D], ps[0:1, 1536:2048])
                            S.emit("vector", cprow, waits=[t1, t2], inc="vc")
                    t_bo2 = S.dma("sync", "qd", lambda e: e.dma_start(
                        out=rowsA[0:1, 0:D], in_=fbo[None, :]), waits=[ctx.tok("tn"), ctx.tok("vc")])
                    row_to_fm(S, ctx, ps, id1f, rowsA, D, bF, 16, t_bo2)
                    S.fence()
                    S.realize(nc)

                emit_branch(nc, ctx, dict(
                    kind="ffn", Wqb=Wc_b, Wq_base=0, Wkb=Wc_b, Wk_base=NCH * D,
                    bq_col=0, bk_col=8, bias_fm=bF, bv_off=2 * D,
                ), xnf, attO, ps, None, aux, wcv)

            if STOP_AFTER < 6:
                raise _StopEmit
            with (
                nc.sbuf_tensor("fsl", [128, 4096], BF16) as fsl,
                nc.sbuf_tensor("sprd", [128, 2 * 4096], BF16) as sprd,
                nc.sbuf_tensor("dtok", [128, 1024], BF16) as dtok,
                nc.sbuf_tensor("d32", [128, 1024], F32) as d32,
                nc.sbuf_tensor("dbf", [128, 512], BF16) as dbf,
                nc.sbuf_tensor("xres", [128, 2 * 1024], F32) as xres,
                nc.sbuf_tensor("yt", [128, 1024], F32) as yt,
            ):
                S = Sched(ctx)
                sp_free = [None, None]
                xres_free = [None, None]
                yt_free = None
                dtok_free = None
                for nt in range(NQT):
                    par = nt % 2
                    t_sp = S.dma("sync", "qd", lambda e, nt=nt, par=par: e.dma_start(
                        out=sprd[:, par * 4096:(par + 1) * 4096].rearrange("p (c f) -> p c f", c=NCH),
                        in_=spill.rearrange("p (c q) -> p c q", c=NCH)[:, :, nt * 512:(nt + 1) * 512]),
                        waits=[sp_free[par]])
                    ev_toks = []
                    if nt == 0:
                        pslot = {0: [ctx.tok("vc"), ctx.tok("ac")], 1: [ctx.tok("vc"), ctx.tok("ac")]}
                    for oc in range(NCH):
                        col = (oc % 2) * 512
                        t_mm = mm_tile(S, ps[:, col:col + 512],
                                       [(Wo2_b[:, pr * D + oc * 128:pr * D + oc * 128 + 128],
                                         attO[:, pr * T_OWN + nt * 512:pr * T_OWN + nt * 512 + 512],
                                         None, None) for pr in range(N_PAIRS)],
                                       waits=[pslot[oc % 2]])

                        def ev(v, oc=oc, col=col, par=par):
                            v.drain()
                            v.tensor_scalar(dbf[:, 0:512], ps[:, col:col + 512],
                                            bF[:, 16 + oc:16 + oc + 1], None, op0=ALU.add)
                            v.drain()
                            r = None
                            for t4 in range(4):
                                r = v.tensor_add(
                                    fsl[:, t4 * 1024 + oc * 128:t4 * 1024 + (oc + 1) * 128],
                                    dbf[:, t4 * 128:(t4 + 1) * 128],
                                    sprd[:, par * 4096 + oc * 512 + t4 * 128:par * 4096 + oc * 512 + (t4 + 1) * 128])
                            return r
                        ev_toks.append(S.emit("vector", ev, waits=[t_mm, t_sp], inc="vc"))
                        pslot[oc % 2] = [ev_toks[-1]]
                    sp_free[par] = ev_toks[-1]
                    for t4 in range(4):
                        tok_row = nt * 512 + t4 * 128
                        t_tr = S.dma("scalar", "sd", lambda a, t4=t4: a.dma_start(
                            out=dtok[:, :].rearrange("p (c t) -> p c t", c=NCH),
                            in_=fsl[:, t4 * 1024:(t4 + 1) * 1024].rearrange("p (c f) -> p c f", c=NCH),
                            transpose=True), waits=[ev_toks[-1], dtok_free])
                        xp = t4 % 2
                        t_x = S.dma("sync", "qd", lambda e, tok_row=tok_row, xp=xp: e.dma_start(
                            out=xres[:, xp * D:(xp + 1) * D], in_=x_own[tok_row:tok_row + 128, :]),
                            waits=[xres_free[xp]])

                        def res(v, xp=xp):
                            v.tensor_copy(d32[:, :], dtok[:, :])
                            v.drain()
                            return v.tensor_add(yt[:, :], xres[:, xp * D:(xp + 1) * D], d32[:, :])
                        t_y = S.emit("vector", res, waits=[t_tr, t_x, yt_free], inc="vc")
                        dtok_free = t_y
                        xres_free[xp] = t_y
                        yt_free = S.dma("sync", "qd", lambda e, tok_row=tok_row: e.dma_start(
                            out=y[tok_row:tok_row + 128, :], in_=yt[:, :]),
                            waits=[t_y])
                S.emit("sync", lambda e: e.nop(), waits=[ctx.qtoks("qd"), ctx.qtoks("sd")])
                S.realize(nc)

            if DEBUG:
                S = Sched(ctx)
                S.dma("sync", "qd", lambda e: e.dma_start(out=dbg_xnf[:, :], in_=xnf[:, 0:4096]))
                S.dma("sync", "qd", lambda e: e.dma_start(out=dbg_attO[:, :], in_=attO[:, 0:4096]))
                S.dma("sync", "qd", lambda e: e.dma_start(out=dbg_lg[:, :], in_=lg[0:1, :]))
                S.dma("sync", "qd", lambda e: e.dma_start(out=dbg_qkm[:, :], in_=qkm[0:1, :]))
                S.dma("sync", "qd", lambda e: e.dma_start(out=dbg_rr[:, :], in_=rr[0:1, :]))
                S.dma("sync", "qd", lambda e: e.dma_start(out=dbg_negB[:, :], in_=negB[:, :]))
                S.emit("sync", lambda e: e.nop(), waits=[ctx.qtoks("qd")])
                S.realize(nc)
    except _StopEmit:
        pass

    return nc


def emit_branch(nc, ctx, cfg, xnf, attO, ps, latf, aux, wcv):
    negB, rr, rrb, Rsb = aux["negB"], aux["rr"], aux["rrb"], aux["Rsb"]
    onesC, uA, uB, ones128 = aux["onesC"], aux["uA"], aux["uB"], aux["ones128"]
    qkm, bvt = aux["qkm"], aux["bvt"]
    rowsA, rowsB, id1 = aux["rowsA"], aux["rowsB"], aux["id1"]
    bias_fm = cfg["bias_fm"]
    attn = cfg["kind"] == "attn"
    vgd = 128 if attn else 256
    n_vg = N_PAIRS if attn else N_GROUPS
    ppg = 1 if attn else 2

    with (
        nc.sbuf_tensor("Kp_%s" % cfg["kind"], [128, T_TOT], BF16) as Kp,
        nc.sbuf_tensor("Vg_%s" % cfg["kind"], [128, NKC * vgd], BF16) as Vg,
        nc.sbuf_tensor("Qp_%s" % cfg["kind"], [128, T_OWN], BF16) as Qp,
        nc.sbuf_tensor("pb_%s" % cfg["kind"], [128, 3 * 1024], BF16) as pbuf,
        contextlib.ExitStack() as bst,
    ):
        sqscr = pbuf
        Vwp = None
        if not attn:
            Vwp = bst.enter_context(nc.sbuf_tensor("Vwp", [128, NCH * vgd], BF16))
        S = Sched(ctx)
        if attn:
            pslot = {0: [ctx.tok("vc"), ctx.qtoks("sd"), ctx.qtoks("gd")],
                     1: [ctx.tok("vc"), ctx.qtoks("sd"), ctx.qtoks("gd")]}
            for lc in range(LCH):
                for nt8 in range(8):
                    sl = (lc * 8 + nt8) % 2
                    col = sl * 512
                    t_mm = mm_tile(S, ps[:, col:col + 512],
                                   [(cfg["Wl"][:, ci * LAT + lc * 128:ci * LAT + lc * 128 + 128],
                                     xnf[:, ci * T_TOT + nt8 * 512:ci * T_TOT + nt8 * 512 + 512],
                                     None, None) for ci in range(NCH)],
                                   waits=[pslot[sl]])
                    pslot[sl] = [S.emit("vector", lambda v, lc=lc, nt8=nt8, col=col: v.tensor_scalar(
                        latf[:, lc * T_TOT + nt8 * 512:lc * T_TOT + nt8 * 512 + 512],
                        ps[:, col:col + 512], bias_fm[:, cfg["bl_col"] + lc:cfg["bl_col"] + lc + 1],
                        None, op0=ALU.add), waits=[t_mm], inc="vc")]

        prev_unit = {"pv_evac": None, "rr_done": None, "exp30": None,
                     "exp31": None, "pvlast": None}
        pending_ep = None
        for g in range(n_vg):
            bsrc = rowsB[0:1, cfg["bv_off"] + g * vgd:cfg["bv_off"] + (g + 1) * vgd]
            t_bmm = mm_tile(S, ps[:, 1024:1024 + vgd], [(onesC[:, :], bsrc, None, None)],
                            waits=[ctx.tok("vc"), ctx.qtoks("gd")], pad=xnf[:, 0:128])
            t_bvt = S.emit("vector", lambda v: v.tensor_copy(bvt[:, 0:vgd], ps[:, 1024:1024 + vgd]),
                           waits=[t_bmm], inc="vc")
            t_vw = None
            if not attn:
                t_vw = S.dma("sync", "qd", lambda e, g=g: e.dma_start(
                    out=Vwp[:, :].rearrange("p (c o) -> p c o", c=NCH),
                    in_=wcv.rearrange("p (c o) -> p c o", c=NCH)[:, :, g * vgd:(g + 1) * vgd]),
                    waits=[ctx.qtoks("qd"), ctx.tok("tn")])
            pslot = {0: [ctx.tok("vc"), ctx.tok("ac")], 1: [ctx.tok("vc"), ctx.tok("ac")]}
            for tt in range(NKC):
                sl = tt % 2
                col = sl * 512
                if attn:
                    pieces = [(latf[:, lc * T_TOT + tt * 128:lc * T_TOT + tt * 128 + 128],
                               cfg["Wvb"][:, cfg["Wv_off"] + lc * D + g * vgd:cfg["Wv_off"] + lc * D + (g + 1) * vgd],
                               None, None) for lc in range(LCH)]
                    w = [pslot[sl]]
                else:
                    pieces = [(xnf[:, ci * T_TOT + tt * 128:ci * T_TOT + tt * 128 + 128],
                               Vwp[:, ci * vgd:(ci + 1) * vgd],
                               None, None) for ci in range(NCH)]
                    w = [pslot[sl], t_vw]
                t_mm = mm_tile(S, ps[:, col:col + vgd], pieces, waits=w)
                pslot[sl] = [S.emit("vector", lambda v, tt=tt, col=col: v.tensor_add(
                    Vg[:, tt * vgd:(tt + 1) * vgd], ps[:, col:col + vgd], bvt[:, 0:vgd]),
                    waits=[t_mm, t_bvt], inc="vc")]

            for pc in range(ppg):
                pr = g * ppg + pc
                pslot = {0: [ctx.tok("vc"), ctx.tok("ac")], 1: [ctx.tok("vc"), ctx.tok("ac")]}
                for nt in range(NQT):
                    sl = nt % 2
                    col = sl * 512
                    base = 0 if attn else cfg["Wq_base"]
                    pieces = [(cfg["Wqb"][:, base + ci * D + pr * 128:base + ci * D + pr * 128 + 128],
                               xnf[:, ci * T_TOT + nt * 512:ci * T_TOT + nt * 512 + 512],
                               None, None) for ci in range(NCH)]
                    t_mm = mm_tile(S, ps[:, col:col + 512], pieces, waits=[pslot[sl]])
                    pslot[sl] = [S.emit("vector", lambda v, nt=nt, col=col, pr=pr: v.tensor_scalar(
                        Qp[:, nt * 512:nt * 512 + 512], ps[:, col:col + 512],
                        bias_fm[:, cfg["bq_col"] + pr:cfg["bq_col"] + pr + 1], None, op0=ALU.add),
                        waits=[t_mm], inc="vc")]
                for nt8 in range(8):
                    sl = nt8 % 2
                    col = sl * 512
                    if attn:
                        pieces = [(cfg["Wkb"][:, cfg["Wk_off"] + lc * D + pr * 128:cfg["Wk_off"] + lc * D + pr * 128 + 128],
                                   latf[:, lc * T_TOT + nt8 * 512:lc * T_TOT + nt8 * 512 + 512],
                                   None, None) for lc in range(LCH)]
                    else:
                        pieces = [(cfg["Wkb"][:, cfg["Wk_base"] + ci * D + pr * 128:cfg["Wk_base"] + ci * D + pr * 128 + 128],
                                   xnf[:, ci * T_TOT + nt8 * 512:ci * T_TOT + nt8 * 512 + 512],
                                   None, None) for ci in range(NCH)]
                    t_mm = mm_tile(S, ps[:, col:col + 512], pieces, waits=[pslot[sl]])
                    pslot[sl] = [S.emit("vector", lambda v, nt8=nt8, col=col, pr=pr: v.tensor_scalar(
                        Kp[:, nt8 * 512:nt8 * 512 + 512], ps[:, col:col + 512],
                        bias_fm[:, cfg["bk_col"] + pr:cfg["bk_col"] + pr + 1], None, op0=ALU.add),
                        waits=[t_mm], inc="vc")]

                t_qsq = S.emit("vector", lambda v: v.tensor_tensor(
                    sqscr[:, 0:2048], Qp[:, :], Qp[:, :], op=ALU.mult),
                    waits=[ctx.tok("vc"), ctx.tok("ac")], inc="vc")
                bcol = (1024, 1536)
                red_tok = [None, None]
                last = t_qsq
                seq = [("q", 0, nt) for nt in range(NQT)]
                seq += [("k", s2, q4) for s2 in range(2) for q4 in range(4)]
                ksq_tok = {}
                for j, (kind, s2, q4) in enumerate(seq):
                    par = j % 2
                    if kind == "k" and q4 == 0:
                        last = S.emit("vector", lambda v, s2=s2: (v.drain(), v.tensor_tensor(
                            sqscr[:, 0:2048], Kp[:, s2 * 2048:(s2 + 1) * 2048],
                            Kp[:, s2 * 2048:(s2 + 1) * 2048], op=ALU.mult))[1],
                            waits=[ctx.tok("tn")], inc="vc")
                    w = [last, red_tok[par]]
                    t_mm = mm_tile(S, ps[0:1, bcol[par]:bcol[par] + 512],
                                   [(ones128[:, :], sqscr[:, q4 * 512:(q4 + 1) * 512], None, None)],
                                   waits=w, pad=xnf[:, 0:128])
                    qi = q4 if kind == "q" else 8 + s2 * 4 + q4
                    red_tok[par] = S.emit("vector", lambda v, qi=qi, par=par: v.reduce_max(
                        qkm[0:1, qi:qi + 1], ps[0:1, bcol[par]:bcol[par] + 512], axis=AX.X),
                        waits=[t_mm], inc="vc")
                last = red_tok[0] if red_tok[0] and (red_tok[1] is None or red_tok[0][1] > red_tok[1][1]) else red_tok[1]
                last = S.emit("vector", lambda v: v.drain(), waits=[red_tok[0], red_tok[1]], inc="vc")

                def mk_negb(v):
                    v.drain()
                    v.reduce_max(qkm[0:1, 16:17], qkm[0:1, 8:16], axis=AX.X)
                    v.drain()
                    v.tensor_scalar(qkm[0:1, 20:24], qkm[0:1, 0:4], qkm[0:1, 16:17],
                                    -0.5 * SCALE, op0=ALU.add, op1=ALU.mult)
                    v.drain()
                    return v.tensor_copy(rowsB[0:1, RB_NB:RB_NB + 4], qkm[0:1, 20:24])
                t_nb = S.emit("vector", mk_negb, waits=[last, ctx.tok("ac")], inc="vc")
                t_nbm = mm_tile(S, ps[:, 1536:1540],
                                [(onesC[:, :], rowsB[0:1, RB_NB:RB_NB + 4], None, None)],
                                waits=[t_nb], pad=xnf[:, 0:128])
                t_nbf = S.emit("vector", lambda v: v.tensor_copy(negB[:, :], ps[:, 1536:1540]),
                               waits=[t_nbm], inc="vc")

                voff = pc * 128
                for nt in range(NQT):
                    exp_toks = []
                    pv_toks = []
                    qk_toks = []

                    def emit_qk(i, nt=nt):
                        j = i % 2
                        sc = PS_SLOT[j]
                        if i < 2:
                            w = [ctx.tok("vc"),
                                 prev_unit.get("exp30") if i == 0 else prev_unit.get("exp31")]
                        else:
                            w = [exp_toks[i - 2]]
                        S.emit("tensor", lambda t, i=i, sc=sc, nt=nt: t.matmul(
                            ps[:, sc:sc + 512], Kp[0:64, i * 128:(i + 1) * 128],
                            Qp[0:64, nt * 512:(nt + 1) * 512], start=True, stop=True,
                            tile_position=(0, 0)), waits=w)
                        qk_toks.append(S.emit("tensor", lambda t, i=i, sc=sc, nt=nt: t.matmul(
                            ps[:, sc + 512:sc + 1024], Kp[64:128, i * 128:(i + 1) * 128],
                            Qp[64:128, nt * 512:(nt + 1) * 512], start=True, stop=True,
                            tile_position=(64, 0)), inc="tn"))

                    for i in range(2):
                        emit_qk(i)
                    if pending_ep is not None:
                        pending_ep()
                        pending_ep = None
                    for i in range(NKC):
                        sc = PS_SLOT[i % 2]
                        pb = (i % 3) * 1024
                        wexp = [qk_toks[i]] + ([t_nbf] if i == 0 else [])
                        if i >= 3:
                            wexp.append(pv_toks[i - 3])
                        elif prev_unit.get("pvlast") is not None:
                            wexp.append(prev_unit["pvlast"])
                        exp_toks.append(S.emit("scalar", lambda a, sc=sc, pb=pb, nt=nt: a.activation(
                            pbuf[:, pb:pb + 1024], ps[:, sc:sc + 1024], AF.Exp,
                            bias=negB[:, nt:nt + 1], scale=SCALE), waits=wexp, inc="ac"))
                        if i + 2 < NKC:
                            emit_qk(i + 2)
                        wpv = [exp_toks[i]]
                        if i == 0:
                            wpv.append(prev_unit["pv_evac"])
                        S.emit("tensor", lambda t, i=i, pb=pb, voff=voff: t.matmul(
                            ps[0:64, PS_PVA:PS_PVA + 512],
                            Vg[:, i * vgd + voff:i * vgd + voff + 64],
                            pbuf[:, pb:pb + 512], start=(i == 0), stop=(i == NKC - 1),
                            tile_position=(0, 0)), waits=wpv)
                        S.emit("tensor", lambda t, i=i, pb=pb, voff=voff: t.matmul(
                            ps[64:128, PS_PVB:PS_PVB + 512],
                            Vg[:, i * vgd + voff + 64:i * vgd + voff + 128],
                            pbuf[:, pb + 512:pb + 1024], start=(i == 0), stop=(i == NKC - 1),
                            tile_position=(0, 64)))
                        wrho = [prev_unit["rr_done"], prev_unit["pv_evac"]] if i == 0 else ()
                        S.emit("tensor", lambda t, i=i, pb=pb: t.matmul(
                            ps[0:1, PS_RHA:PS_RHA + 512], ones128[:, :],
                            pbuf[:, pb:pb + 512], start=(i == 0), stop=(i == NKC - 1),
                            tile_position=(0, 0)), waits=wrho)
                        pv_toks.append(S.emit("tensor", lambda t, i=i, pb=pb: t.matmul(
                            ps[32:33, PS_RHB:PS_RHB + 512], ones128[:, :],
                            pbuf[:, pb + 512:pb + 1024], start=(i == 0), stop=(i == NKC - 1),
                            tile_position=(0, 32)), inc="tn"))

                    def make_ep(pv_toks=pv_toks, pr=pr, nt=nt):
                        def ep():
                            def rho_fin(v):
                                v.tensor_copy(rr[0:1, 0:512], ps[0:1, PS_RHA:PS_RHA + 512])
                                v.tensor_copy(rr[0:1, 512:1024], ps[32:33, PS_RHB:PS_RHB + 512])
                                v.drain()
                                v.reciprocal(rr[0:1, 0:1024], rr[0:1, 0:1024])
                                v.drain()
                                return v.tensor_copy(rrb[0:1, 0:1024], rr[0:1, 0:1024])
                            t_rr = S.emit("vector", rho_fin, waits=[pv_toks[NKC - 1]], inc="vc")
                            prev_unit["rr_done"] = t_rr
                            t_R = mm_tile(S, ps[:, PS_RHA:PS_RHA + 512],
                                          [(uA[:, :], rrb[0:1, 0:512], None, True),
                                           (uB[:, :], rrb[0:1, 512:1024], None, False)],
                                          waits=[t_rr], pad=xnf[:, 0:128])

                            def pvn(v, pr=pr, nt=nt):
                                v.tensor_copy(Rsb[:, :], ps[:, PS_RHA:PS_RHA + 512])
                                v.drain()
                                v.tensor_tensor(
                                    attO[0:64, pr * T_OWN + nt * 512:pr * T_OWN + nt * 512 + 512],
                                    ps[0:64, PS_PVA:PS_PVA + 512], Rsb[0:64, :], op=ALU.mult)
                                return v.tensor_tensor(
                                    attO[64:128, pr * T_OWN + nt * 512:pr * T_OWN + nt * 512 + 512],
                                    ps[64:128, PS_PVB:PS_PVB + 512], Rsb[64:128, :], op=ALU.mult)
                            t_pvn = S.emit("vector", pvn, waits=[t_R], inc="vc")
                            prev_unit["pv_evac"] = t_pvn
                        return ep
                    prev_unit["exp30"] = exp_toks[NKC - 2]
                    prev_unit["exp31"] = exp_toks[NKC - 1]
                    prev_unit["pvlast"] = pv_toks[NKC - 1]
                    pending_ep = make_ep()
        if pending_ep is not None:
            pending_ep()
            pending_ep = None
        S.fence()
        S.realize(nc)


_NC_CACHE = {}


def kernel(**inputs):
    if "prog" not in _NC_CACHE:
        _NC_CACHE["prog"] = build_nc()
    nc = _NC_CACHE["prog"]

    X = np.ascontiguousarray(inputs["L0_attn_x"], dtype=np.float32)
    wm = dict(
        aWq=inputs["L0_attn_Wq"], abq=inputs["L0_attn_bq"], aWl=inputs["L0_attn_Wl"],
        abl=inputs["L0_attn_bl"], aWk=inputs["L0_attn_Wk"], abk=inputs["L0_attn_bk"],
        aWv=inputs["L0_attn_Wv"], abv=inputs["L0_attn_bv"], aWo=inputs["L0_attn_Wo"],
        abo=inputs["L0_attn_bo"], fWq=inputs["L0_ffn_Wq"], fbq=inputs["L0_ffn_bq"],
        fWk=inputs["L0_ffn_Wk"], fbk=inputs["L0_ffn_bk"], fWv=inputs["L0_ffn_Wv"],
        fbv=inputs["L0_ffn_bv"], fWg1=inputs["L0_ffn_Wg1"], fbg1=inputs["L0_ffn_bg1"],
        fWg2=inputs["L0_ffn_Wg2"], fbg2=inputs["L0_ffn_bg2"], fWo=inputs["L0_ffn_Wo"],
        fbo=inputs["L0_ffn_bo"],
    )
    wm = {k: np.ascontiguousarray(v, dtype=np.float32) for k, v in wm.items()}
    wm["cident"] = np.eye(128, dtype=np.float32)
    in_maps = []
    for c in range(8):
        b, h = c // 2, c % 2
        m = dict(wm)
        m["x_own"] = np.ascontiguousarray(X[b, h * T_OWN:(h + 1) * T_OWN])
        m["x_oth"] = np.ascontiguousarray(X[b, (1 - h) * T_OWN:(2 - h) * T_OWN])
        in_maps.append(m)

    import time
    out = np.empty((4, 4096, 1024), dtype=np.float32)

    def fill(first_core, results):
        for i, r in enumerate(results):
            c = first_core + i
            b, h = c // 2, c % 2
            out[b, h * T_OWN:(h + 1) * T_OWN] = r["y"]

    if TRY_FUSED:
        try:
            res = run_bass_kernel_spmd(nc, in_maps, core_ids=list(range(8)))
            fill(0, res.results)
            return out
        except Exception:
            time.sleep(10)
    for g in range(4):
        cores = [2 * g, 2 * g + 1]
        last = None
        for attempt in range(3):
            try:
                res = run_bass_kernel_spmd(nc, [in_maps[c] for c in cores], core_ids=cores)
                fill(2 * g, res.results)
                last = None
                break
            except Exception as e:
                last = e
                time.sleep(10)
        if last is not None:
            raise last
    return out
